# version 03
# baseline (speedup 1.0000x reference)
import sys
sys.path.insert(0, "/opt/trn_rl_repo")
import numpy as np
import ml_dtypes
import concourse.bass as bass
import concourse.mybir as mybir

F32 = mybir.dt.float32
BF16 = mybir.dt.bfloat16
AF = mybir.ActivationFunctionType
ALU = mybir.AluOpType
AX = mybir.AxisListType

ENGS = ("tensor", "vector", "scalar", "gpsimd", "sync")


class Step:
    __slots__ = ("eng", "fn", "deps", "idx", "val", "ndma", "_sem_holder")

    def __init__(self, eng, fn, deps, idx, ndma):
        self.eng, self.fn, self.deps, self.idx, self.ndma = eng, fn, deps, idx, ndma
        self.val = None


class PB:
    def __init__(self, nc):
        self.nc = nc
        self.steps = []
        self.counts = {e: 0 for e in ENGS}
        self.prev = None

    def step(self, eng, fn, deps=None, ndma=0):
        if deps is None:
            deps = [self.prev] if self.prev is not None else []
        deps = [d for d in deps if d is not None]
        s = Step(eng, fn, deps, len(self.steps), ndma)
        self.counts[eng] += (16 * ndma) if ndma else 1
        s.val = self.counts[eng]
        self.steps.append(s)
        self.prev = s
        return s

    def dma(self, queue, pairs, deps=None):
        holder = {}

        def fn(e):
            sem = holder["sem"]
            for o, i in pairs:
                e.dma_start(out=o, in_=i).then_inc(sem, 16)
            return None
        s = self.step(queue, fn, deps=deps, ndma=len(pairs))
        s._sem_holder = holder
        return s

    def emit(self, block, sems):
        per_eng = {e: [s for s in self.steps if s.eng == e] for e in ENGS}

        def make_body(eng):
            def body(e):
                watermark = {}
                for s in per_eng[eng]:
                    best = {}
                    for d in s.deps:
                        if d.eng not in best or d.val > best[d.eng].val:
                            best[d.eng] = d
                    for d in best.values():
                        if d.eng == eng == "tensor":
                            continue
                        if d.val <= watermark.get(d.eng, -1):
                            continue
                        e.wait_ge(sems[d.eng], d.val)
                        watermark[d.eng] = d.val
                    if s.ndma:
                        s._sem_holder["sem"] = sems[eng]
                        s.fn(e)
                    else:
                        last = s.fn(e)
                        assert last is not None, f"step {s.idx} on {eng} returned None"
                        last.then_inc(sems[eng], 1)
            return body

        for eng in ENGS:
            if per_eng[eng]:
                getattr(block, eng)(make_body(eng))


D, T = 1024, 2048
NCD = 8
H, DK = 8, 128
CH = 64
NCHUNK = T // CH
TT = 512
NTT = T // TT
DH = 4096
NH = DH // 128
LN_EPS = 1e-5


def colslice_ap(w_d, o0, width, nchunks):
    return w_d.rearrange("(c p) o -> p c o", p=128)[:, :nchunks, o0:o0 + width]


class SMap:

    def __init__(self, nc):
        self.nc = nc
        self.off = (nc.sbuf_base + 31) & ~31
        self.top = nc.sbuf_top
        self.offsets = {}

    def at(self, name, shape, dtype, offset):
        return self.nc.alloc_sbuf_tensor_at(name, shape, dtype, offset=offset)

    def seq(self, name, shape, dtype):
        per = int(np.prod(shape[1:])) * mybir.dt.size(dtype)
        h = self.at(name, shape, dtype, self.off)
        self.offsets[name] = self.off
        self.off = (self.off + per + 31) & ~31
        assert self.off <= self.top, f"SBUF overflow at {name}: {self.off} > {self.top}"
        return h


def build_launchA():
    nc = bass.Bass("TRN2", target_bir_lowering=False)

    xT_d = nc.dram_tensor("xT", [D, T], F32, kind="ExternalInput").ap()
    wq_d = nc.dram_tensor("Wq", [D, D], F32, kind="ExternalInput").ap()
    wk_d = nc.dram_tensor("Wk", [D, D], F32, kind="ExternalInput").ap()
    wv_d = nc.dram_tensor("Wv", [D, D], F32, kind="ExternalInput").ap()
    wg_d = nc.dram_tensor("Wg", [D, D], F32, kind="ExternalInput").ap()
    wo_d = nc.dram_tensor("Wo", [D, D], F32, kind="ExternalInput").ap()
    w1_d = nc.dram_tensor("w1T", [D, DH], F32, kind="ExternalInput").ap()
    w2_d = nc.dram_tensor("w2T", [D, DH], F32, kind="ExternalInput").ap()
    h1_d = nc.dram_tensor("h1T", [DH, DH], F32, kind="ExternalInput").ap()
    h2_d = nc.dram_tensor("h2T", [DH, DH], F32, kind="ExternalInput").ap()
    w3_d = nc.dram_tensor("w3T", [DH, D], F32, kind="ExternalInput").ap()
    s1_d = nc.dram_tensor("s1", [128, NH], F32, kind="ExternalInput").ap()
    s2_d = nc.dram_tensor("s2", [128, NH], F32, kind="ExternalInput").ap()
    sh1_d = nc.dram_tensor("sh1", [128, NH], F32, kind="ExternalInput").ap()
    sh2_d = nc.dram_tensor("sh2", [128, NH], F32, kind="ExternalInput").ap()
    s3_d = nc.dram_tensor("s3", [128, NCD], F32, kind="ExternalInput").ap()
    yT_d = nc.dram_tensor("yT", [D, T], F32, kind="ExternalOutput").ap()

    tri_d = nc.inline_tensor(np.triu(np.ones((CH, CH), dtype=np.float32)), name="tric")
    id_d = nc.inline_tensor(np.eye(128, dtype=ml_dtypes.bfloat16), name="idc")

    sm = SMap(nc)
    x = sm.seq("x", [128, NCD * T], F32)
    xln = sm.seq("xln", [128, NCD * T], BF16)
    ofm = sm.seq("ofm", [128, H * T], BF16)
    wa = sm.seq("wa", [128, 4096], BF16)
    rowa = sm.seq("rowa", [1, 512], F32)
    rowb = sm.seq("rowb", [1, 512], F32)
    rowc = sm.seq("rowc", [1, 512], F32)
    tmpf = sm.seq("tmpf", [128, 512], F32)
    scr2 = sm.seq("scr2", [128, 512], BF16)
    sqln = sm.seq("sqln", [128, 512], BF16)
    ones1 = sm.seq("ones1", [1, 128], F32)
    onesc = sm.seq("onesc", [128, 1], BF16)
    tri = sm.seq("tri", [CH, CH], F32)
    idm = sm.seq("idm", [128, 128], BF16)
    scol = sm.seq("scol", [128, 2 * NH + NCD], F32)
    scol2 = sm.seq("scol2", [128, 2 * NH], F32)
    cns = sm.seq("cns", [1, 8], F32)
    U = sm.off
    vtm = sm.at("vtm", [64, NCHUNK * 128], BF16, U + 0)
    qh = sm.at("qh", [128, T], BF16, U + 8192)
    kh = sm.at("kh", [128, T], BF16, U + 12288)
    gh = sm.at("gh", [128, T], F32, U + 16384)
    ke = sm.at("ke", [128, T], BF16, U + 16384)
    ah = sm.at("ah", [128, T], F32, U + 24576)
    qs = sm.at("qs", [128, T], BF16, U + 32768)
    ks = sm.at("ks", [128, T], BF16, U + 36864)
    sst = sm.at("sst", [128, 128], F32, U + 40960)
    pm = sm.at("pm", [64, CH], BF16, U + 41472)
    sstb = sm.at("sstb", [128, 128], BF16, U + 41600)
    rahg = sm.at("rahg", [128, 512], F32, U + 41888)
    sq = sm.at("sq", [128, T], BF16, U + 43936)
    whead = sm.at("whead", [128, 4096], BF16, U + 48032)
    ketm = sm.at("ketm", [64, NCHUNK * 128], BF16, U + 48032)
    wb = sm.at("wb", [128, 4096], BF16, U + 48032)
    GLA_END = U + 48032 + 8192
    hb = sm.at("hb", [128, NH * TT], BF16, U)
    g1 = sm.at("g1", [128, NH * TT], BF16, sm.offsets["ofm"])
    assert GLA_END <= sm.top and U + NH * TT * 2 <= sm.top, (U, GLA_END, sm.top)

    from contextlib import ExitStack
    with ExitStack() as es:
        E = es.enter_context
        ps = [E(nc.psum_tensor(f"ps{i}", [128, 512], F32)) for i in range(6)]
        psT = E(nc.psum_tensor("psT", [128, 1024], BF16))
        psP = E(nc.psum_tensor("psP", [64, 512], F32))
        sems = {e: E(nc.semaphore("sem_" + e)) for e in ENGS}
        block = E(nc.Block())

        pb = PB(nc)

        ld0 = pb.dma("gpsimd", [(x[:, c * T:(c + 1) * T], xT_d[c * 128:(c + 1) * 128, :])
                                for c in range(NCD)], deps=[])
        ldc = pb.dma("sync", [(tri[:, :], tri_d.ap()), (idm[:, :], id_d.ap()),
                              (scol[:, 0:NH], s1_d[:, :]), (scol[:, NH:2 * NH], s2_d[:, :]),
                              (scol[:, 2 * NH:2 * NH + NCD], s3_d[:, :]),
                              (scol2[:, 0:NH], sh1_d[:, :]), (scol2[:, NH:2 * NH], sh2_d[:, :])],
                     deps=[])

        def ones_fn(e):
            e.memset(ones1[:, :], 1.0)
            e.memset(cns[:1, :], LN_EPS)
            return e.memset(onesc[:, :], 1.0)
        pb.step("vector", ones_fn, deps=[])
        pb.step("scalar", lambda e: e.nop(), deps=[ld0, ldc, pb.prev])

        def emit_ln(src):
            for g in range(4):
                sl = slice(g * 512, (g + 1) * 512)
                for c in range(NCD):
                    cs = slice(c * T + sl.start, c * T + sl.stop)
                    pb.step("vector", lambda e, cs=cs: e.tensor_copy(scr2[:, :], src[:, cs]))
                    cast = pb.prev
                    pb.step("tensor", lambda e, c=c: e.matmul(
                        ps[0][:1, :], onesc[:, :], scr2[:, :], start=(c == 0), stop=(c == NCD - 1)))
                    pb.step("scalar", lambda e: e.activation(sqln[:, :], scr2[:, :], AF.Square),
                            deps=[cast, pb.prev])
                    pb.step("tensor", lambda e, c=c: e.matmul(
                        ps[1][:1, :], onesc[:, :], sqln[:, :], start=(c == 0), stop=(c == NCD - 1)))
                pb.step("vector", lambda e: e.tensor_scalar(rowa[:1, :], ps[0][:1, :], 1.0 / D, None, ALU.mult))
                pb.step("vector", lambda e: e.tensor_scalar(rowb[:1, :], ps[1][:1, :], 1.0 / D, None, ALU.mult))
                pb.step("vector", lambda e: e.tensor_tensor(rowc[:1, :], rowa[:1, :], rowa[:1, :], ALU.mult))
                pb.step("vector", lambda e: e.tensor_tensor(rowb[:1, :], rowb[:1, :], rowc[:1, :], ALU.subtract))
                pb.step("scalar", lambda e: e.activation(rowb[:1, :], rowb[:1, :], AF.Ln, bias=cns[:1, :1]))
                pb.step("scalar", lambda e: e.activation(rowb[:1, :], rowb[:1, :], AF.Exp, scale=-0.5))
                pb.step("vector", lambda e: e.scalar_tensor_tensor(rowc[:1, :], rowa[:1, :], -1.0, rowb[:1, :], ALU.mult, ALU.mult))
                pb.step("tensor", lambda e: e.matmul(ps[2][:, :], ones1[:, :], rowb[:1, :], start=True, stop=True))
                bA = pb.prev
                pb.step("tensor", lambda e: e.matmul(ps[3][:, :], ones1[:, :], rowc[:1, :], start=True, stop=True))
                bB = pb.prev
                for c in range(NCD):
                    cs = slice(c * T + sl.start, c * T + sl.stop)
                    pb.step("vector", lambda e, cs=cs: e.tensor_tensor(tmpf[:, :], src[:, cs], ps[2][:, :], ALU.mult),
                            deps=[bA, pb.prev])
                    pb.step("vector", lambda e, cs=cs: e.tensor_tensor(xln[:, cs], tmpf[:, :], ps[3][:, :], ALU.add),
                            deps=[bB, pb.prev])

        emit_ln(x)

        for h in range(H):
            pb.dma("gpsimd", [
                (whead[:, 0:1024].rearrange("p (c o) -> p c o", o=128), colslice_ap(wq_d, h * 128, 128, 8)),
                (whead[:, 1024:2048].rearrange("p (c o) -> p c o", o=128), colslice_ap(wk_d, h * 128, 128, 8)),
                (whead[:, 2048:3072].rearrange("p (c o) -> p c o", o=128), colslice_ap(wv_d, h * 128, 128, 8)),
                (whead[:, 3072:4096].rearrange("p (c o) -> p c o", o=128), colslice_ap(wg_d, h * 128, 128, 8))])
            lw = pb.prev
            for off, dst, mode in ((0, qh, "c"), (1024, kh, "c"), (3072, gh, "s")):
                for g in range(4):
                    sl = slice(g * 512, (g + 1) * 512)

                    def mmfn(e, off=off, sl=sl):
                        r = None
                        for c in range(NCD):
                            r = e.matmul(ps[0][:, :], whead[:, off + c * 128:off + (c + 1) * 128],
                                         xln[:, c * T + sl.start:c * T + sl.stop],
                                         start=(c == 0), stop=(c == NCD - 1))
                        return r
                    pb.step("tensor", mmfn, deps=[lw, pb.prev])
                    if mode == "c":
                        pb.step("vector", lambda e, dst=dst, sl=sl: e.tensor_copy(dst[:, sl], ps[0][:, :]))
                    else:
                        pb.step("scalar", lambda e, dst=dst, sl=sl: e.activation(dst[:, sl], ps[0][:, :], AF.Sigmoid))
            for cc in range(NCHUNK):
                tsl = slice(cc * CH, (cc + 1) * CH)

                def vfn(e, tsl=tsl):
                    r = None
                    for c in range(NCD):
                        r = e.matmul(ps[1][:64, :128], xln[:, c * T + tsl.start:c * T + tsl.stop],
                                     whead[:, 2048 + c * 128:2048 + (c + 1) * 128],
                                     start=(c == 0), stop=(c == NCD - 1))
                    return r
                pb.step("tensor", vfn)
                pb.step("vector", lambda e, cc=cc: e.tensor_copy(vtm[:, cc * 128:(cc + 1) * 128], ps[1][:64, :128]))
            pb.step("scalar", lambda e: e.activation(gh[:, :], gh[:, :], AF.Ln))
            for cc in range(NCHUNK):
                sl = slice(cc * CH, (cc + 1) * CH)
                pb.step("vector", lambda e, sl=sl: e.tensor_tensor_scan(gh[:, sl], gh[:, sl], gh[:, sl], 0.0, ALU.add, ALU.bypass))
            pb.step("scalar", lambda e: e.activation(ah[:, :], gh[:, :], AF.Exp))
            eA = pb.prev
            pb.step("scalar", lambda e: e.activation(sq[:, :], qh[:, :], AF.Square))
            sqq = pb.prev
            for g in range(4):
                sl = slice(g * 512, (g + 1) * 512)
                pb.step("tensor", lambda e, sl=sl: e.matmul(ps[0][:1, :], onesc[:, :], sq[:, sl], start=True, stop=True),
                        deps=[sqq, pb.prev])
                pb.step("vector", lambda e: e.tensor_copy(rowa[:1, :], ps[0][:1, :]))
                pb.step("scalar", lambda e: e.activation(rowa[:1, :], rowa[:1, :], AF.Ln))
                pb.step("scalar", lambda e: e.activation(rowa[:1, :], rowa[:1, :], AF.Exp, scale=-0.5))
                pb.step("tensor", lambda e: e.matmul(ps[2][:, :], ones1[:, :], rowa[:1, :], start=True, stop=True))
                pb.step("vector", lambda e, sl=sl: e.tensor_tensor(tmpf[:, :], ah[:, sl], ps[2][:, :], ALU.mult),
                        deps=[eA, pb.prev])
                pb.step("vector", lambda e, sl=sl: e.tensor_tensor(qs[:, sl], tmpf[:, :], qh[:, sl], ALU.mult))
            pb.step("scalar", lambda e: e.activation(sq[:, :], kh[:, :], AF.Square))
            sqk = pb.prev
            for g in range(4):
                sl = slice(g * 512, (g + 1) * 512)
                pb.step("tensor", lambda e, sl=sl: e.matmul(ps[0][:1, :], onesc[:, :], sq[:, sl], start=True, stop=True),
                        deps=[sqk, pb.prev])
                pb.step("vector", lambda e: e.tensor_copy(rowa[:1, :], ps[0][:1, :]))
                pb.step("scalar", lambda e: e.activation(rowa[:1, :], rowa[:1, :], AF.Ln))
                pb.step("scalar", lambda e: e.activation(rowa[:1, :], rowa[:1, :], AF.Exp, scale=-0.5))
                pb.step("tensor", lambda e: e.matmul(ps[2][:, :], ones1[:, :], rowa[:1, :], start=True, stop=True))
                pb.step("scalar", lambda e, sl=sl: e.activation(rahg[:, :], gh[:, sl], AF.Exp, scale=-1.0))
                pb.step("vector", lambda e: e.tensor_tensor(rahg[:, :], rahg[:, :], ps[2][:, :], ALU.mult))
                pb.step("vector", lambda e, sl=sl: e.tensor_tensor(ks[:, sl], rahg[:, :], kh[:, sl], ALU.mult))
            for cc in range(NCHUNK):
                sl = slice(cc * CH, (cc + 1) * CH)
                col = slice(cc * CH + CH - 1, cc * CH + CH)
                pb.step("vector", lambda e, sl=sl, col=col: e.tensor_scalar(ke[:, sl], ks[:, sl], ah[:, col], None, ALU.mult))
            for cc in range(NCHUNK):
                sl = slice(cc * CH, (cc + 1) * CH)
                pb.step("tensor", lambda e, sl=sl: e.transpose(psT[:64, :128], ke[:, sl], idm[:, :]))
                pb.step("vector", lambda e, cc=cc: e.tensor_copy(ketm[:, cc * 128:(cc + 1) * 128], psT[:64, :128]))
            pb.step("vector", lambda e: e.memset(sst[:, :], 0.0))
            pb.step("vector", lambda e: e.memset(sstb[:, :], 0.0))
            for cc in range(NCHUNK):
                sl = slice(cc * CH, (cc + 1) * CH)
                col = slice(cc * CH + CH - 1, cc * CH + CH)
                pb.step("tensor", lambda e, sl=sl: e.matmul(psP[:64, :64], ks[:, sl], qs[:, sl], start=True, stop=True))
                pb.step("vector", lambda e: e.tensor_tensor(pm[:, :], psP[:64, :64], tri[:, :], ALU.mult))

                def ofn(e, sl=sl, cc=cc):
                    e.matmul(ps[4][:, :64], vtm[:, cc * 128:(cc + 1) * 128], pm[:, :], start=True, stop=False)
                    return e.matmul(ps[4][:, :64], sstb[:, :], qs[:, sl], start=False, stop=True)
                pb.step("tensor", ofn)
                pb.step("vector", lambda e, sl=sl, h=h: e.tensor_copy(ofm[:, h * T + sl.start:h * T + sl.stop], ps[4][:, :64]))
                oev = pb.prev

                def tfn(e, cc=cc):
                    return e.matmul(ps[5][:, :128], ketm[:, cc * 128:(cc + 1) * 128],
                                    vtm[:, cc * 128:(cc + 1) * 128], start=True, stop=True)
                pb.step("tensor", tfn, deps=[oev])
                pb.step("vector", lambda e, col=col: e.scalar_tensor_tensor(
                    sst[:, :], sst[:, :], ah[:, col], ps[5][:, :128], ALU.mult, ALU.add), deps=[pb.prev, oev])
                pb.step("vector", lambda e: e.tensor_copy(sstb[:, :], sst[:, :]))

        rdr = {}
        gla_done = pb.prev
        ldW = {}
        mmW = {}
        accW = {}
        for dc in range(NCD):
            par = dc % 2
            buf = wa if par == 0 else wb
            ldW[dc] = pb.dma("gpsimd", [(buf[:, 0:1024].rearrange("p (c o) -> p c o", o=128),
                                         colslice_ap(wo_d, dc * 128, 128, 8))],
                             deps=[rdr.get(("wbuf", par)), gla_done])
            for g in range(4):
                sl = slice(g * 512, (g + 1) * 512)
                bank = 2 * par + (g % 2)

                def mmfn(e, sl=sl, buf=buf, bank=bank):
                    r = None
                    for c in range(NCD):
                        r = e.matmul(ps[bank][:, :], buf[:, c * 128:(c + 1) * 128],
                                     ofm[:, c * T + sl.start:c * T + sl.stop],
                                     start=(c == 0), stop=(c == NCD - 1))
                    return r
                mmW[(dc, g)] = pb.step("tensor", mmfn, deps=[ldW[dc], rdr.get(("ps", bank)), gla_done])
                rdr[("wbuf", par)] = mmW[(dc, g)]
                cs = slice(dc * T + sl.start, dc * T + sl.stop)
                accW[(dc, g)] = pb.step("vector", lambda e, cs=cs, bank=bank: e.tensor_tensor(
                    x[:, cs], x[:, cs], ps[bank][:, :], ALU.add), deps=[mmW[(dc, g)]])
                rdr[("ps", bank)] = accW[(dc, g)]
        wo_done = pb.step("scalar", lambda e: e.nop(),
                          deps=[accW[(NCD - 1, 3)], accW[(NCD - 2, 3)], accW[(NCD - 1, 2)]])

        pb.prev = wo_done
        emit_ln(x)
        ln2_done = pb.prev

        ofm_free = wo_done
        hb_free = gla_done
        prev_tt_w3 = ln2_done
        prev_tt_h2 = None
        sA = [scr2, sqln]
        for tt in range(NTT):
            tsl = slice(tt * TT, (tt + 1) * TT)
            mt1, mt2, sil, hm, ld12 = {}, {}, {}, {}, {}
            for oc in range(NH):
                par = oc % 2
                buf = wa if par == 0 else wb
                ld12[oc] = pb.dma("gpsimd", [
                    (buf[:, 0:1024].rearrange("p (c o) -> p c o", o=128),
                     colslice_ap(w1_d, oc * 128, 128, 8)),
                    (buf[:, 1024:2048].rearrange("p (c o) -> p c o", o=128),
                     colslice_ap(w2_d, oc * 128, 128, 8))],
                    deps=[rdr.get(("wbuf", par)), ln2_done])
                b1, b2 = 2 * par, 2 * par + 1

                def m1f(e, tsl=tsl, buf=buf, b1=b1):
                    r = None
                    for c in range(NCD):
                        r = e.matmul(ps[b1][:, :], buf[:, c * 128:(c + 1) * 128],
                                     xln[:, c * T + tsl.start:c * T + tsl.stop],
                                     start=(c == 0), stop=(c == NCD - 1))
                    return r
                mt1[oc] = pb.step("tensor", m1f, deps=[ld12[oc], rdr.get(("ps", b1)), ln2_done])
                rdr[("wbuf", par)] = mt1[oc]

                def m2f(e, tsl=tsl, buf=buf, b2=b2):
                    r = None
                    for c in range(NCD):
                        r = e.matmul(ps[b2][:, :], buf[:, 1024 + c * 128:1024 + (c + 1) * 128],
                                     xln[:, c * T + tsl.start:c * T + tsl.stop],
                                     start=(c == 0), stop=(c == NCD - 1))
                    return r
                mt2[oc] = pb.step("tensor", m2f, deps=[ld12[oc], rdr.get(("ps", b2))])
                rdr[("wbuf", par)] = mt2[oc]
                sil[oc] = pb.step("scalar", lambda e, oc=oc, par=par, b1=b1: e.activation(
                    sA[par][:, :], ps[b1][:, :], AF.Silu, scale=scol[:, oc:oc + 1]),
                    deps=[mt1[oc], rdr.get(("sA", par))])
                rdr[("ps", b1)] = sil[oc]
                hm[oc] = pb.step("vector", lambda e, oc=oc, par=par, b2=b2: e.scalar_tensor_tensor(
                    hb[:, oc * TT:(oc + 1) * TT], ps[b2][:, :], scol[:, NH + oc:NH + oc + 1],
                    sA[par][:, :], ALU.mult, ALU.mult),
                    deps=[mt2[oc], sil[oc], prev_tt_w3, hb_free])
                rdr[("ps", b2)] = hm[oc]
                rdr[("sA", par)] = hm[oc]
            hm_last = hm[NH - 1]
            mmh, gel, ldh = {}, {}, {}
            for oc in range(NH):
                par = oc % 2
                buf = wa if par == 0 else wb
                ldh[oc] = pb.dma("gpsimd", [(buf[:, :].rearrange("p (c o) -> p c o", o=128),
                                             colslice_ap(h1_d, oc * 128, 128, 32))],
                                 deps=[rdr.get(("wbuf", par))])
                bank = par

                def mhf(e, buf=buf, bank=bank):
                    r = None
                    for ic in range(NH):
                        r = e.matmul(ps[bank][:, :], buf[:, ic * 128:(ic + 1) * 128],
                                     hb[:, ic * TT:(ic + 1) * TT],
                                     start=(ic == 0), stop=(ic == NH - 1))
                    return r
                mmh[oc] = pb.step("tensor", mhf, deps=[ldh[oc], rdr.get(("ps", bank)), hm_last])
                rdr[("wbuf", par)] = mmh[oc]
                gel[oc] = pb.step("scalar", lambda e, oc=oc, bank=bank: e.activation(
                    g1[:, oc * TT:(oc + 1) * TT], ps[bank][:, :], AF.Gelu, scale=scol2[:, oc:oc + 1]),
                    deps=[mmh[oc], ofm_free, prev_tt_h2])
                rdr[("ps", bank)] = gel[oc]
            gel_last = gel[NH - 1]
            mmh2, gel2, ldh2 = {}, {}, {}
            for oc in range(NH):
                par = oc % 2
                buf = wa if par == 0 else wb
                ldh2[oc] = pb.dma("gpsimd", [(buf[:, :].rearrange("p (c o) -> p c o", o=128),
                                              colslice_ap(h2_d, oc * 128, 128, 32))],
                                  deps=[rdr.get(("wbuf", par))])
                bank = 2 + par

                def mh2f(e, buf=buf, bank=bank):
                    r = None
                    for ic in range(NH):
                        r = e.matmul(ps[bank][:, :], buf[:, ic * 128:(ic + 1) * 128],
                                     g1[:, ic * TT:(ic + 1) * TT],
                                     start=(ic == 0), stop=(ic == NH - 1))
                    return r
                mmh2[oc] = pb.step("tensor", mh2f, deps=[ldh2[oc], rdr.get(("ps", bank)), gel_last])
                rdr[("wbuf", par)] = mmh2[oc]
                gel2[oc] = pb.step("scalar", lambda e, oc=oc, bank=bank: e.activation(
                    hb[:, oc * TT:(oc + 1) * TT], ps[bank][:, :], AF.Gelu, scale=scol2[:, NH + oc:NH + oc + 1]),
                    deps=[mmh2[oc]])
                rdr[("ps", bank)] = gel2[oc]
            gel2_last = gel2[NH - 1]
            prev_tt_h2 = gel2_last
            mmy, accy, ldy = {}, {}, {}
            for dc in range(NCD):
                par = dc % 2
                buf = wa if par == 0 else wb
                ldy[dc] = pb.dma("gpsimd", [(buf[:, :].rearrange("p (c o) -> p c o", o=128),
                                             colslice_ap(w3_d, dc * 128, 128, 32))],
                                 deps=[rdr.get(("wbuf", par))])
                bank = 4 + par

                def myf(e, buf=buf, bank=bank):
                    r = None
                    for ic in range(NH):
                        r = e.matmul(ps[bank][:, :], buf[:, ic * 128:(ic + 1) * 128],
                                     hb[:, ic * TT:(ic + 1) * TT],
                                     start=(ic == 0), stop=(ic == NH - 1))
                    return r
                mmy[dc] = pb.step("tensor", myf, deps=[ldy[dc], rdr.get(("ps", bank)), gel2_last])
                rdr[("wbuf", par)] = mmy[dc]
                cs = slice(dc * T + tsl.start, dc * T + tsl.stop)
                accy[dc] = pb.step("vector", lambda e, cs=cs, dc=dc, bank=bank: e.scalar_tensor_tensor(
                    x[:, cs], ps[bank][:, :], scol[:, 2 * NH + dc:2 * NH + dc + 1], x[:, cs], ALU.mult, ALU.add),
                    deps=[mmy[dc]])
                rdr[("ps", bank)] = accy[dc]
            prev_tt_w3 = pb.step("scalar", lambda e: e.nop(),
                                 deps=[accy[NCD - 1], accy[NCD - 2]])
            pb.prev = prev_tt_w3

        stv = pb.counts["sync"] + 16 * NCD
        pb.dma("sync", [(yT_d[c * 128:(c + 1) * 128, :], x[:, c * T:(c + 1) * T]) for c in range(NCD)])

        def fin(e):
            e.wait_ge(sems["sync"], stv)
            return e.nop()
        pb.step("sync", fin)

        pb.emit(block, sems)
    return nc


def prep_inputs_A(inputs, b):
    asc = np.ascontiguousarray
    return {
        "xT": asc(inputs["L0_attn_x"][b].T),
        "Wq": asc(inputs["L0_attn_Wq"]), "Wk": asc(inputs["L0_attn_Wk"]),
        "Wv": asc(inputs["L0_attn_Wv"]), "Wg": asc(inputs["L0_attn_Wg"]),
        "Wo": asc(inputs["L0_attn_Wo"]),
        "w1T": asc(inputs["L0_ffn_w1_w"].T), "w2T": asc(inputs["L0_ffn_w2_w"].T),
        "h1T": asc(inputs["L0_ffn_h1_w"].T), "h2T": asc(inputs["L0_ffn_h2_w"].T),
        "w3T": asc(inputs["L0_ffn_w3_w"].T),
        "s1": asc(inputs["L0_ffn_w1_s"].reshape(NH, 128).T),
        "s2": asc(inputs["L0_ffn_w2_s"].reshape(NH, 128).T),
        "sh1": asc(inputs["L0_ffn_h1_s"].reshape(NH, 128).T),
        "sh2": asc(inputs["L0_ffn_h2_s"].reshape(NH, 128).T),
        "s3": asc(inputs["L0_ffn_w3_s"].reshape(NCD, 128).T),
    }


TT2 = 1024
NT2 = T // TT2
NE = 8
NB = 8
TB = 256


def build_launchB(dbg=False):
    nc = bass.Bass("TRN2", target_bir_lowering=False)

    xT_d = nc.dram_tensor("xT", [D, T], F32, kind="ExternalInput").ap()
    win_d = nc.dram_tensor("WinT", [D, D], F32, kind="ExternalInput").ap()
    wlam_d = nc.dram_tensor("WlamT", [D, D], F32, kind="ExternalInput").ap()
    wgate_d = nc.dram_tensor("WgateT", [D, D], F32, kind="ExternalInput").ap()
    wout_d = nc.dram_tensor("WoutT", [D, D], F32, kind="ExternalInput").ap()
    bcols_d = nc.dram_tensor("bcols", [128, 48], F32, kind="ExternalInput").ap()
    gwc_d = nc.dram_tensor("gwc", [128, 64], F32, kind="ExternalInput").ap()
    wgT_d = nc.dram_tensor("wgT", [NE, D, DH], F32, kind="ExternalInput").ap()
    wuT_d = nc.dram_tensor("wuT", [NE, D, DH], F32, kind="ExternalInput").ap()
    wdT_d = nc.dram_tensor("wdT", [NE, DH, D], F32, kind="ExternalInput").ap()
    outT_d = nc.dram_tensor("outT", [D, T], F32, kind="ExternalOutput").ap()
    if dbg:
        dbgA_d = nc.dram_tensor("dbgA", [D, T], F32, kind="ExternalOutput").ap()
        dbgL_d = nc.dram_tensor("dbgL", [D, T], BF16, kind="ExternalOutput").ap()
        dbgH_d = nc.dram_tensor("dbgH", [D, T], BF16, kind="ExternalOutput").ap()
        cmb_d = nc.dram_tensor("cmbo", [8, T], F32, kind="ExternalOutput").ap()
        cmbt_d = nc.dram_tensor("cmbt_o", [128, 128], F32, kind="ExternalOutput").ap()
        dbgZ_d = nc.dram_tensor("dbgZ", [D, T], BF16, kind="ExternalOutput").ap()
        dbgX_d = nc.dram_tensor("dbgX", [D, T], F32, kind="ExternalOutput").ap()

    idf_d = nc.inline_tensor(np.eye(128, dtype=np.float32), name="idf")
    sel_np = np.zeros((8, 1024), dtype=np.float32)
    for e in range(8):
        sel_np[e, e * 128:(e + 1) * 128] = 1.0
    sel_d = nc.inline_tensor(sel_np, name="selc")
    cmb_dram = nc.dram_tensor("cmbscr", [128, 128], F32)

    sm = SMap(nc)
    x = sm.seq("x", [128, NCD * T], F32)
    xln = sm.seq("xln", [128, NCD * T], BF16)
    wa = sm.seq("wa", [128, 4096], BF16)
    wb = sm.seq("wb", [128, 4096], BF16)
    scr3 = sm.seq("scr3", [128, 512], BF16)
    scr4 = sm.seq("scr4", [128, 512], BF16)
    rowa = sm.seq("rowa", [1, 512], F32)
    rowb = sm.seq("rowb", [1, 512], F32)
    rowc = sm.seq("rowc", [1, 512], F32)
    tmpf = sm.seq("tmpf", [128, 512], F32)
    scr2 = sm.seq("scr2", [128, 512], BF16)
    sqln = sm.seq("sqln", [128, 512], BF16)
    ones1 = sm.seq("ones1", [1, 128], F32)
    onesc = sm.seq("onesc", [128, 1], BF16)
    idf = sm.seq("idf", [128, 128], F32)
    sel = sm.seq("sel", [8, 1024], F32)
    bcols = sm.seq("bcols", [128, 48], F32)
    wgw = sm.seq("wgw", [128, 64], BF16)
    cmbt = sm.seq("cmbt", [128, 128], F32)
    cmbu = sm.seq("cmbu", [128, 128], F32)
    cfm8 = sm.seq("cfm8", [8, T], F32)
    cns = sm.seq("cns", [1, 8], F32)
    UB = sm.off
    lam = sm.at("lam", [128, NCD * T], BF16, UB)
    hbuf = sm.at("hbuf", [128, NCD * T], BF16, UB + 32768)
    hmoe = sm.at("hmoe", [128, NH * TT2], BF16, UB)
    cb = sm.at("cb", [128, T], BF16, sm.offsets["rowa"])
    assert UB + 65536 <= sm.top, (UB, sm.top)

    from contextlib import ExitStack
    with ExitStack() as es:
        E = es.enter_context
        ps = [E(nc.psum_tensor(f"ps{i}", [128, 512], F32)) for i in range(6)]
        psT = E(nc.psum_tensor("psT", [128, 512], F32))
        psX = E(nc.psum_tensor("psX", [128, 512], F32))
        sems = {e: E(nc.semaphore("sem_" + e)) for e in ENGS}
        block = E(nc.Block())

        pb = PB(nc)

        pb.dma("gpsimd", [(x[:, c * T:(c + 1) * T], xT_d[c * 128:(c + 1) * 128, :])
                          for c in range(NCD)], deps=[])
        ldc = pb.dma("sync", [(idf[:, :], idf_d.ap()), (sel[:, :], sel_d.ap()),
                              (bcols[:, :], bcols_d[:, :])], deps=[])
        ldw = pb.dma("gpsimd", [(wgw[:, :], gwc_d[:, :])], deps=[])

        def ones_fn(e):
            e.memset(ones1[:, :], 1.0)
            e.memset(cns[:1, :], LN_EPS)
            return e.memset(onesc[:, :], 1.0)
        pb.step("vector", ones_fn, deps=[])
        pb.step("scalar", lambda e: e.nop(), deps=[pb.steps[0], ldc, ldw, pb.prev])

        def emit_ln(src, src_is_bf16=False):
            for g in range(4):
                sl = slice(g * 512, (g + 1) * 512)
                for c in range(NCD):
                    cs = slice(c * T + sl.start, c * T + sl.stop)
                    if src_is_bf16:
                        pb.step("tensor", lambda e, c=c, cs=cs: e.matmul(
                            ps[0][:1, :], onesc[:, :], src[:, cs], start=(c == 0), stop=(c == NCD - 1)))
                        pb.step("scalar", lambda e, cs=cs: e.activation(sqln[:, :], src[:, cs], AF.Square))
                    else:
                        pb.step("vector", lambda e, cs=cs: e.tensor_copy(scr2[:, :], src[:, cs]))
                        cast = pb.prev
                        pb.step("tensor", lambda e, c=c: e.matmul(
                            ps[0][:1, :], onesc[:, :], scr2[:, :], start=(c == 0), stop=(c == NCD - 1)))
                        pb.step("scalar", lambda e: e.activation(sqln[:, :], scr2[:, :], AF.Square),
                                deps=[cast, pb.prev])
                    pb.step("tensor", lambda e, c=c: e.matmul(
                        ps[1][:1, :], onesc[:, :], sqln[:, :], start=(c == 0), stop=(c == NCD - 1)))
                pb.step("vector", lambda e: e.tensor_scalar(rowa[:1, :], ps[0][:1, :], 1.0 / D, None, ALU.mult))
                pb.step("vector", lambda e: e.tensor_scalar(rowb[:1, :], ps[1][:1, :], 1.0 / D, None, ALU.mult))
                pb.step("vector", lambda e: e.tensor_tensor(rowc[:1, :], rowa[:1, :], rowa[:1, :], ALU.mult))
                pb.step("vector", lambda e: e.tensor_tensor(rowb[:1, :], rowb[:1, :], rowc[:1, :], ALU.subtract))
                pb.step("scalar", lambda e: e.activation(rowb[:1, :], rowb[:1, :], AF.Ln, bias=cns[:1, :1]))
                pb.step("scalar", lambda e: e.activation(rowb[:1, :], rowb[:1, :], AF.Exp, scale=-0.5))
                pb.step("vector", lambda e: e.scalar_tensor_tensor(rowc[:1, :], rowa[:1, :], -1.0, rowb[:1, :], ALU.mult, ALU.mult))
                pb.step("tensor", lambda e: e.matmul(ps[2][:, :], ones1[:, :], rowb[:1, :], start=True, stop=True))
                bA = pb.prev
                pb.step("tensor", lambda e: e.matmul(ps[3][:, :], ones1[:, :], rowc[:1, :], start=True, stop=True))
                bB = pb.prev
                for c in range(NCD):
                    cs = slice(c * T + sl.start, c * T + sl.stop)
                    pb.step("vector", lambda e, cs=cs: e.tensor_tensor(tmpf[:, :], src[:, cs], ps[2][:, :], ALU.mult),
                            deps=[bA, pb.prev])
                    pb.step("vector", lambda e, cs=cs: e.tensor_tensor(xln[:, cs], tmpf[:, :], ps[3][:, :], ALU.add),
                            deps=[bB, pb.prev])

        emit_ln(x)
        rdr = {}
        ln3_done = pb.prev
        sA = [scr2, scr3]
        sB = [tmpf]
        for oc in range(NCD):
            par = oc % 2
            buf = wa if par == 0 else wb
            ldL = pb.dma("gpsimd", [(buf[:, 0:1024].rearrange("p (c o) -> p c o", o=128),
                                     colslice_ap(wlam_d, oc * 128, 128, 8)),
                                    (buf[:, 1024:2048].rearrange("p (c o) -> p c o", o=128),
                                     colslice_ap(win_d, oc * 128, 128, 8))],
                         deps=[rdr.get(("wbuf", par)), ln3_done])
            for g in range(4):
                sl = slice(g * 512, (g + 1) * 512)
                b1, b2 = 2 * par, 2 * par + 1

                def mml(e, sl=sl, buf=buf, b1=b1):
                    r = None
                    for c in range(NCD):
                        r = e.matmul(ps[b1][:, :], buf[:, c * 128:(c + 1) * 128],
                                     xln[:, c * T + sl.start:c * T + sl.stop],
                                     start=(c == 0), stop=(c == NCD - 1))
                    return r
                s_mml = pb.step("tensor", mml, deps=[ldL, rdr.get(("ps", b1)), ln3_done])
                rdr[("wbuf", par)] = s_mml
                s_sig = pb.step("scalar", lambda e, oc=oc, sl=sl, b1=b1: e.activation(
                    lam[:, oc * T + sl.start:oc * T + sl.stop], ps[b1][:, :], AF.Sigmoid,
                    bias=bcols[:, 8 + oc:9 + oc]), deps=[s_mml])
                rdr[("ps", b1)] = s_sig

                def mmi(e, sl=sl, buf=buf, b2=b2):
                    r = None
                    for c in range(NCD):
                        r = e.matmul(ps[b2][:, :], buf[:, 1024 + c * 128:1024 + (c + 1) * 128],
                                     xln[:, c * T + sl.start:c * T + sl.stop],
                                     start=(c == 0), stop=(c == NCD - 1))
                    return r
                s_mmi = pb.step("tensor", mmi, deps=[ldL, rdr.get(("ps", b2))])
                rdr[("wbuf", par)] = s_mmi
                s_sil = pb.step("scalar", lambda e, oc=oc, par=par, b2=b2: e.activation(
                    sA[par][:, :], ps[b2][:, :], AF.Silu, bias=bcols[:, oc:1 + oc]),
                    deps=[s_mmi, rdr.get(("sA", par))])
                rdr[("ps", b2)] = s_sil
                s_om = pb.step("vector", lambda e, oc=oc, sl=sl: e.tensor_scalar(
                    tmpf[:, :], lam[:, oc * T + sl.start:oc * T + sl.stop], -1.0, 1.0, ALU.mult, ALU.add),
                    deps=[s_sig, rdr.get("tmpf")])
                s_hm = pb.step("vector", lambda e, oc=oc, sl=sl, par=par: e.tensor_tensor(
                    hbuf[:, oc * T + sl.start:oc * T + sl.stop], tmpf[:, :], sA[par][:, :], ALU.mult),
                    deps=[s_om, s_sil])
                rdr[("sA", par)] = s_hm
                rdr["tmpf"] = s_hm
        pb.prev = pb.step("scalar", lambda e: e.nop(), deps=[pb.prev])
        for c in range(NCD):
            for b in range(1, NB):
                s_prev = slice(c * T + (b - 1) * TB, c * T + b * TB)
                s_cur = slice(c * T + b * TB, c * T + (b + 1) * TB)
                pb.step("vector", lambda e, s_prev=s_prev, s_cur=s_cur: e.tensor_tensor(
                    tmpf[:, :TB], lam[:, s_cur], hbuf[:, s_prev], ALU.mult))
                pb.step("vector", lambda e, s_cur=s_cur: e.tensor_tensor(
                    hbuf[:, s_cur], hbuf[:, s_cur], tmpf[:, :TB], ALU.add))
        if dbg:
            pb.dma("sync", [(dbgL_d[c * 128:(c + 1) * 128, :], lam[:, c * T:(c + 1) * T])
                            for c in range(NCD)])
            pb.dma("sync", [(dbgH_d[c * 128:(c + 1) * 128, :], hbuf[:, c * T:(c + 1) * T])
                            for c in range(NCD)])
        for g in range(4):
            sl = slice(g * 512, (g + 1) * 512)
            for c in range(NCD):
                cs = slice(c * T + sl.start, c * T + sl.stop)
                pb.step("tensor", lambda e, c=c, cs=cs: e.matmul(
                    ps[0][:1, :], onesc[:, :], hbuf[:, cs], start=(c == 0), stop=(c == NCD - 1)))
                pb.step("scalar", lambda e, cs=cs: e.activation(sqln[:, :], hbuf[:, cs], AF.Square))
                pb.step("tensor", lambda e, c=c: e.matmul(
                    ps[1][:1, :], onesc[:, :], sqln[:, :], start=(c == 0), stop=(c == NCD - 1)))
            pb.step("vector", lambda e: e.tensor_scalar(rowa[:1, :], ps[0][:1, :], 1.0 / D, None, ALU.mult))
            pb.step("vector", lambda e: e.tensor_scalar(rowb[:1, :], ps[1][:1, :], 1.0 / D, None, ALU.mult))
            pb.step("vector", lambda e: e.tensor_tensor(rowc[:1, :], rowa[:1, :], rowa[:1, :], ALU.mult))
            pb.step("vector", lambda e: e.tensor_tensor(rowb[:1, :], rowb[:1, :], rowc[:1, :], ALU.subtract))
            pb.step("scalar", lambda e: e.activation(rowb[:1, :], rowb[:1, :], AF.Ln, bias=cns[:1, :1]))
            pb.step("scalar", lambda e: e.activation(rowb[:1, :], rowb[:1, :], AF.Exp, scale=-0.5))
            pb.step("vector", lambda e: e.scalar_tensor_tensor(rowc[:1, :], rowa[:1, :], -1.0, rowb[:1, :], ALU.mult, ALU.mult))
            pb.step("tensor", lambda e: e.matmul(ps[2][:, :], ones1[:, :], rowb[:1, :], start=True, stop=True))
            bA = pb.prev
            pb.step("tensor", lambda e: e.matmul(ps[3][:, :], ones1[:, :], rowc[:1, :], start=True, stop=True))
            bB = pb.prev
            for c in range(NCD):
                cs = slice(c * T + sl.start, c * T + sl.stop)
                pb.dma("gpsimd", [(wa[:, 0:1024].rearrange("p (cc o) -> p cc o", o=128),
                                   colslice_ap(wgate_d, c * 128, 128, 8))])
                lw = pb.prev

                def mmg(e, sl=sl):
                    r = None
                    for cc in range(NCD):
                        r = e.matmul(ps[0][:, :], wa[:, cc * 128:(cc + 1) * 128],
                                     xln[:, cc * T + sl.start:cc * T + sl.stop],
                                     start=(cc == 0), stop=(cc == NCD - 1))
                    return r
                pb.step("tensor", mmg, deps=[lw, pb.prev])
                pb.step("scalar", lambda e, c=c: e.activation(scr2[:, :], ps[0][:, :], AF.Silu, bias=bcols[:, 16 + c:17 + c]))
                pb.step("vector", lambda e, cs=cs: e.tensor_tensor(tmpf[:, :], hbuf[:, cs], ps[2][:, :], ALU.mult),
                        deps=[bA, pb.prev])
                pb.step("vector", lambda e: e.tensor_tensor(tmpf[:, :], tmpf[:, :], ps[3][:, :], ALU.add),
                        deps=[bB, pb.prev])
                pb.step("vector", lambda e, c=c: e.tensor_scalar(
                    tmpf[:, :], tmpf[:, :], bcols[:, 32 + c:33 + c], bcols[:, 40 + c:41 + c], ALU.mult, ALU.add))
                pb.step("vector", lambda e, cs=cs: e.tensor_tensor(hbuf[:, cs], tmpf[:, :], scr2[:, :], ALU.mult))
        zdone = pb.prev
        if dbg:
            pb.dma("sync", [(dbgZ_d[c * 128:(c + 1) * 128, :], hbuf[:, c * T:(c + 1) * T])
                            for c in range(NCD)])
            pb.dma("sync", [(dbgX_d[c * 128:(c + 1) * 128, :], x[:, c * T:(c + 1) * T])
                            for c in range(NCD)])
            zdone = pb.prev
        for dc in range(NCD):
            par = dc % 2
            buf = wa if par == 0 else wb
            ldo = pb.dma("gpsimd", [(buf[:, 0:1024].rearrange("p (c o) -> p c o", o=128),
                                     colslice_ap(wout_d, dc * 128, 128, 8))],
                         deps=[rdr.get(("wbuf", par)), zdone])
            for g in range(4):
                sl = slice(g * 512, (g + 1) * 512)
                bank = 2 * par + (g % 2)

                def mmo(e, sl=sl, buf=buf, bank=bank):
                    r = None
                    for c in range(NCD):
                        r = e.matmul(ps[bank][:, :], buf[:, c * 128:(c + 1) * 128],
                                     hbuf[:, c * T + sl.start:c * T + sl.stop],
                                     start=(c == 0), stop=(c == NCD - 1))
                    return r
                s_mm = pb.step("tensor", mmo, deps=[ldo, rdr.get(("ps", bank)), zdone])
                rdr[("wbuf", par)] = s_mm
                cs = slice(dc * T + sl.start, dc * T + sl.stop)
                s_acc = pb.step("vector", lambda e, cs=cs, dc=dc, bank=bank: e.scalar_tensor_tensor(
                    x[:, cs], ps[bank][:, :], bcols[:, 24 + dc:25 + dc], x[:, cs], ALU.add, ALU.add),
                    deps=[s_mm])
                rdr[("ps", bank)] = s_acc
        pb.prev = pb.step("scalar", lambda e: e.nop(), deps=[pb.prev])

        if dbg:
            _d = pb.dma("sync", [(dbgA_d[c * 128:(c + 1) * 128, :], x[:, c * T:(c + 1) * T])
                                 for c in range(NCD)])
            pb.prev = _d
        emit_ln(x)
        for t in range(16):
            tsl = slice(t * 128, (t + 1) * 128)

            def mmlg(e, tsl=tsl):
                r = None
                for c in range(NCD):
                    r = e.matmul(ps[0][:, :8], xln[:, c * T + tsl.start:c * T + tsl.stop],
                                 wgw[:, c * 8:(c + 1) * 8], start=(c == 0), stop=(c == NCD - 1))
                return r
            pb.step("tensor", mmlg)
            L8 = lambda a, b: slice(a, b)
            pb.step("vector", lambda e: e.tensor_copy(tmpf[:, 0:8], ps[0][:, :8]))
            pb.step("vector", lambda e: e.reduce_max(tmpf[:, 8:9], tmpf[:, 0:8], axis=AX.X))
            pb.step("vector", lambda e: e.tensor_scalar(tmpf[:, 16:24], tmpf[:, 0:8], tmpf[:, 8:9], None, ALU.is_ge))
            pb.step("vector", lambda e: e.scalar_tensor_tensor(tmpf[:, 24:32], tmpf[:, 16:24], -1e30, tmpf[:, 0:8], ALU.mult, ALU.add))
            pb.step("vector", lambda e: e.reduce_max(tmpf[:, 10:11], tmpf[:, 24:32], axis=AX.X))
            pb.step("vector", lambda e: e.tensor_scalar(tmpf[:, 9:10], tmpf[:, 8:9], -1.0, None, ALU.mult))
            pb.step("scalar", lambda e: e.activation(tmpf[:, 11:12], tmpf[:, 10:11], AF.Exp, bias=tmpf[:, 9:10]))
            pb.step("vector", lambda e: e.tensor_scalar(tmpf[:, 11:12], tmpf[:, 11:12], 1.0, None, ALU.add))
            pb.step("vector", lambda e: e.reciprocal(tmpf[:, 12:13], tmpf[:, 11:12]))
            pb.step("scalar", lambda e: e.activation(tmpf[:, 24:32], tmpf[:, 0:8], AF.Exp, bias=tmpf[:, 9:10]))
            pb.step("vector", lambda e: e.tensor_scalar(tmpf[:, 16:24], tmpf[:, 0:8], tmpf[:, 10:11], None, ALU.is_ge))
            pb.step("vector", lambda e: e.tensor_tensor(tmpf[:, 24:32], tmpf[:, 16:24], tmpf[:, 24:32], ALU.mult))
            pb.step("vector", lambda e, t=t: e.tensor_scalar(cmbt[:, t * 8:(t + 1) * 8], tmpf[:, 24:32], tmpf[:, 12:13], None, ALU.mult))
        pb.step("tensor", lambda e: e.transpose(psT[:, :128], cmbt[:, :], idf[:, :]))
        pb.step("vector", lambda e: e.tensor_copy(cmbu[:, :], psT[:, :128]))
        pb.dma("sync", [(cmb_dram.ap(), cmbu[:, :])])
        pb.dma("sync", [(cfm8[:, :].rearrange("e (t j) -> e t j", j=128),
                        cmb_dram.ap().rearrange("(t e) j -> e t j", e=8))])
        if dbg:
            pb.dma("sync", [(cmb_d[:, :], cfm8[:, :])])
            pb.dma("sync", [(cmbt_d[:, :], cmbt[:, :])])
        cmb_ready = pb.prev
        sAm = [scr2, scr3]
        sBm = [sqln, scr4]
        prev_dn = cmb_ready
        prev_ue_last = None
        for ex in range(NE):
            cb_evs = []
            for g in range(4):
                sl = slice(g * 512, (g + 1) * 512)
                s_cb = pb.step("tensor", lambda e, ex=ex, sl=sl: e.matmul(
                    ps[4][:, :], sel[:, ex * 128:(ex + 1) * 128], cfm8[:, sl], start=True, stop=True),
                    deps=[cmb_ready, rdr.get(("ps", 4)), prev_ue_last])
                s_ev = pb.step("vector", lambda e, sl=sl: e.tensor_copy(cb[:, sl], ps[4][:, :]),
                               deps=[s_cb, prev_ue_last])
                rdr[("ps", 4)] = s_ev
                cb_evs.append(s_ev)
            cb_done = cb_evs[-1]
            for t2 in range(NT2):
                base = t2 * TT2
                hms = []
                for oc in range(NH):
                    par = oc % 2
                    buf = wa if par == 0 else wb
                    ldgu = pb.dma("gpsimd", [
                        (buf[:, 0:1024].rearrange("p (c o) -> p c o", o=128),
                         colslice_ap(wgT_d[ex], oc * 128, 128, 8)),
                        (buf[:, 1024:2048].rearrange("p (c o) -> p c o", o=128),
                         colslice_ap(wuT_d[ex], oc * 128, 128, 8))],
                        deps=[rdr.get(("wbuf", par))])
                    for hf in range(2):
                        sl = slice(base + hf * 512, base + hf * 512 + 512)
                        b1, b2 = 2 * par, 2 * par + 1

                        def mmg2(e, sl=sl, buf=buf, b1=b1):
                            r = None
                            for c in range(NCD):
                                r = e.matmul(ps[b1][:, :], buf[:, c * 128:(c + 1) * 128],
                                             xln[:, c * T + sl.start:c * T + sl.stop],
                                             start=(c == 0), stop=(c == NCD - 1))
                            return r
                        s_g = pb.step("tensor", mmg2, deps=[ldgu, rdr.get(("ps", b1))])
                        rdr[("wbuf", par)] = s_g

                        def mmu2(e, sl=sl, buf=buf, b2=b2):
                            r = None
                            for c in range(NCD):
                                r = e.matmul(ps[b2][:, :], buf[:, 1024 + c * 128:1024 + (c + 1) * 128],
                                             xln[:, c * T + sl.start:c * T + sl.stop],
                                             start=(c == 0), stop=(c == NCD - 1))
                            return r
                        s_u = pb.step("tensor", mmu2, deps=[ldgu, rdr.get(("ps", b2))])
                        rdr[("wbuf", par)] = s_u
                        s_si = pb.step("scalar", lambda e, par=par, b1=b1: e.activation(
                            sAm[par][:, :], ps[b1][:, :], AF.Silu),
                            deps=[s_g, rdr.get(("sA", par))])
                        rdr[("ps", b1)] = s_si
                        s_ue = pb.step("vector", lambda e, sl=sl, par=par, b2=b2: e.tensor_tensor(
                            sBm[par][:, :], ps[b2][:, :], cb[:, sl], ALU.mult),
                            deps=[s_u, cb_done, rdr.get(("sB", par))])
                        rdr[("ps", b2)] = s_ue
                        prev_ue_last = s_ue
                        s_hm = pb.step("vector", lambda e, oc=oc, hf=hf, par=par: e.tensor_tensor(
                            hmoe[:, oc * TT2 + hf * 512:oc * TT2 + hf * 512 + 512],
                            sAm[par][:, :], sBm[par][:, :], ALU.mult),
                            deps=[s_si, s_ue, prev_dn])
                        rdr[("sA", par)] = s_hm
                        rdr[("sB", par)] = s_hm
                        hms.append(s_hm)
                hm_all = hms[-1]
                for dc in range(NCD):
                    par = dc % 2
                    buf = wa if par == 0 else wb
                    ldd = pb.dma("gpsimd", [(buf[:, :].rearrange("p (c o) -> p c o", o=128),
                                             colslice_ap(wdT_d[ex], dc * 128, 128, 32))],
                                 deps=[rdr.get(("wbuf", par))])
                    for hf in range(2):
                        bank = 5 if hf == 0 else 2 * par

                        def mmd(e, hf=hf, buf=buf, bank=bank):
                            r = None
                            for ic in range(NH):
                                r = e.matmul(ps[bank][:, :], buf[:, ic * 128:(ic + 1) * 128],
                                             hmoe[:, ic * TT2 + hf * 512:ic * TT2 + hf * 512 + 512],
                                             start=(ic == 0), stop=(ic == NH - 1))
                            return r
                        s_d = pb.step("tensor", mmd, deps=[ldd, rdr.get(("ps", bank)), hm_all])
                        rdr[("wbuf", par)] = s_d
                        cs = slice(dc * T + base + hf * 512, dc * T + base + hf * 512 + 512)
                        s_a = pb.step("vector", lambda e, cs=cs, bank=bank: e.tensor_tensor(
                            x[:, cs], x[:, cs], ps[bank][:, :], ALU.add), deps=[s_d])
                        rdr[("ps", bank)] = s_a
                        prev_dn = s_a
        pb.prev = pb.step("scalar", lambda e: e.nop(), deps=[pb.prev, prev_dn])

        stv = pb.counts["sync"] + 16 * NCD
        pb.dma("sync", [(outT_d[c * 128:(c + 1) * 128, :], x[:, c * T:(c + 1) * T]) for c in range(NCD)])

        def fin(e):
            e.wait_ge(sems["sync"], stv)
            return e.nop()
        pb.step("sync", fin)

        pb.emit(block, sems)
    return nc


def prep_inputs_B(inputs, yT_list, core):
    asc = np.ascontiguousarray

    def colpack(vecs):
        out = np.zeros((128, 48), np.float32)
        for j, v in enumerate(vecs):
            out[:, j * 8:(j + 1) * 8] = v.reshape(8, 128).T
        return out
    xT = np.concatenate([yT_list[b][:, core * TB:(core + 1) * TB] for b in range(NB)], axis=1)
    gwc = np.zeros((128, 64), np.float32)
    gw = inputs["L1_ffn_gate_w"]
    for c in range(8):
        gwc[:, c * 8:(c + 1) * 8] = gw[:, c * 128:(c + 1) * 128].T
    return {
        "xT": asc(xT),
        "WinT": asc(inputs["L1_attn_W_in"].T), "WlamT": asc(inputs["L1_attn_W_lam"].T),
        "WgateT": asc(inputs["L1_attn_W_gate"].T), "WoutT": asc(inputs["L1_attn_W_out"].T),
        "bcols": colpack([inputs["L1_attn_b_in"], inputs["L1_attn_b_lam"],
                          inputs["L1_attn_b_gate"], inputs["L1_attn_b_out"],
                          inputs["L1_attn_gamma"], inputs["L1_attn_beta"]]),
        "gwc": asc(gwc),
        "wgT": asc(np.transpose(inputs["L1_ffn_wg"], (0, 2, 1))),
        "wuT": asc(np.transpose(inputs["L1_ffn_wu"], (0, 2, 1))),
        "wdT": asc(np.transpose(inputs["L1_ffn_wd"], (0, 2, 1))),
    }


from concourse.bass_utils import run_bass_kernel_spmd

_CACHE = {}


def kernel(**inputs):
    if "A" not in _CACHE:
        _CACHE["A"] = build_launchA()
        _CACHE["B"] = build_launchB()
    ncA, ncB = _CACHE["A"], _CACHE["B"]

    in_maps_A = [prep_inputs_A(inputs, b) for b in range(8)]
    resA = run_bass_kernel_spmd(ncA, in_maps_A, core_ids=list(range(8)))
    yT_list = [resA.results[b]["yT"] for b in range(8)]

    in_maps_B = [prep_inputs_B(inputs, yT_list, c) for c in range(8)]
    resB = run_bass_kernel_spmd(ncB, in_maps_B, core_ids=list(range(8)))

    out = np.empty((8, 2048, 1024), np.float32)
    for c in range(8):
        out[:, c * TB:(c + 1) * TB, :] = resB.results[c]["outT"].T.reshape(NB, TB, D)
    return out


# version 04
# speedup vs baseline: 1.0372x; 1.0372x over previous
import sys
sys.path.insert(0, "/opt/trn_rl_repo")
import numpy as np
import ml_dtypes
import concourse.bass as bass
import concourse.mybir as mybir

F32 = mybir.dt.float32
BF16 = mybir.dt.bfloat16
AF = mybir.ActivationFunctionType
ALU = mybir.AluOpType
AX = mybir.AxisListType

ENGS = ("tensor", "vector", "scalar", "gpsimd", "sync")


class Step:
    __slots__ = ("eng", "fn", "deps", "idx", "val", "ndma", "_sem_holder")

    def __init__(self, eng, fn, deps, idx, ndma):
        self.eng, self.fn, self.deps, self.idx, self.ndma = eng, fn, deps, idx, ndma
        self.val = None


class PB:
    def __init__(self, nc):
        self.nc = nc
        self.steps = []
        self.counts = {e: 0 for e in ENGS}
        self.prev = None

    def step(self, eng, fn, deps=None, ndma=0):
        if deps is None:
            deps = [self.prev] if self.prev is not None else []
        deps = [d for d in deps if d is not None]
        s = Step(eng, fn, deps, len(self.steps), ndma)
        self.counts[eng] += (16 * ndma) if ndma else 1
        s.val = self.counts[eng]
        self.steps.append(s)
        self.prev = s
        return s

    def dma(self, queue, pairs, deps=None):
        holder = {}

        def fn(e):
            sem = holder["sem"]
            for o, i in pairs:
                e.dma_start(out=o, in_=i).then_inc(sem, 16)
            return None
        s = self.step(queue, fn, deps=deps, ndma=len(pairs))
        s._sem_holder = holder
        return s

    def emit(self, block, sems):
        per_eng = {e: [s for s in self.steps if s.eng == e] for e in ENGS}

        def make_body(eng):
            def body(e):
                watermark = {}
                for s in per_eng[eng]:
                    best = {}
                    for d in s.deps:
                        if d.eng not in best or d.val > best[d.eng].val:
                            best[d.eng] = d
                    for d in best.values():
                        if d.eng == eng == "tensor":
                            continue
                        if d.val <= watermark.get(d.eng, -1):
                            continue
                        e.wait_ge(sems[d.eng], d.val)
                        watermark[d.eng] = d.val
                    if s.ndma:
                        s._sem_holder["sem"] = sems[eng]
                        s.fn(e)
                    else:
                        last = s.fn(e)
                        assert last is not None, f"step {s.idx} on {eng} returned None"
                        last.then_inc(sems[eng], 1)
            return body

        for eng in ENGS:
            if per_eng[eng]:
                getattr(block, eng)(make_body(eng))


D, T = 1024, 2048
NCD = 8
H, DK = 8, 128
CH = 64
NCHUNK = T // CH
TT = 512
NTT = T // TT
DH = 4096
NH = DH // 128
LN_EPS = 1e-5


def colslice_ap(w_d, o0, width, nchunks):
    return w_d.rearrange("(c p) o -> p c o", p=128)[:, :nchunks, o0:o0 + width]


class SMap:

    def __init__(self, nc):
        self.nc = nc
        self.off = (nc.sbuf_base + 31) & ~31
        self.top = nc.sbuf_top
        self.offsets = {}

    def at(self, name, shape, dtype, offset):
        return self.nc.alloc_sbuf_tensor_at(name, shape, dtype, offset=offset)

    def seq(self, name, shape, dtype):
        per = int(np.prod(shape[1:])) * mybir.dt.size(dtype)
        h = self.at(name, shape, dtype, self.off)
        self.offsets[name] = self.off
        self.off = (self.off + per + 31) & ~31
        assert self.off <= self.top, f"SBUF overflow at {name}: {self.off} > {self.top}"
        return h


def build_launchA():
    nc = bass.Bass("TRN2", target_bir_lowering=False)

    xT_d = nc.dram_tensor("xT", [D, T], F32, kind="ExternalInput").ap()
    wq_d = nc.dram_tensor("Wq", [D, D], F32, kind="ExternalInput").ap()
    wk_d = nc.dram_tensor("Wk", [D, D], F32, kind="ExternalInput").ap()
    wv_d = nc.dram_tensor("Wv", [D, D], F32, kind="ExternalInput").ap()
    wg_d = nc.dram_tensor("Wg", [D, D], F32, kind="ExternalInput").ap()
    wo_d = nc.dram_tensor("Wo", [D, D], F32, kind="ExternalInput").ap()
    w1_d = nc.dram_tensor("w1T", [D, DH], F32, kind="ExternalInput").ap()
    w2_d = nc.dram_tensor("w2T", [D, DH], F32, kind="ExternalInput").ap()
    h1_d = nc.dram_tensor("h1T", [DH, DH], F32, kind="ExternalInput").ap()
    h2_d = nc.dram_tensor("h2T", [DH, DH], F32, kind="ExternalInput").ap()
    w3_d = nc.dram_tensor("w3T", [DH, D], F32, kind="ExternalInput").ap()
    s1_d = nc.dram_tensor("s1", [128, NH], F32, kind="ExternalInput").ap()
    s2_d = nc.dram_tensor("s2", [128, NH], F32, kind="ExternalInput").ap()
    sh1_d = nc.dram_tensor("sh1", [128, NH], F32, kind="ExternalInput").ap()
    sh2_d = nc.dram_tensor("sh2", [128, NH], F32, kind="ExternalInput").ap()
    s3_d = nc.dram_tensor("s3", [128, NCD], F32, kind="ExternalInput").ap()
    yT_d = nc.dram_tensor("yT", [D, T], F32, kind="ExternalOutput").ap()

    tri_d = nc.inline_tensor(np.triu(np.ones((CH, CH), dtype=np.float32)), name="tric")
    id_d = nc.inline_tensor(np.eye(128, dtype=ml_dtypes.bfloat16), name="idc")

    sm = SMap(nc)
    x = sm.seq("x", [128, NCD * T], F32)
    xln = sm.seq("xln", [128, NCD * T], BF16)
    ofm = sm.seq("ofm", [128, H * T], BF16)
    wa = sm.seq("wa", [128, 4096], BF16)
    rowa = sm.seq("rowa", [1, 512], F32)
    rowb = sm.seq("rowb", [1, 512], F32)
    rowc = sm.seq("rowc", [1, 512], F32)
    tmpf = sm.seq("tmpf", [128, 512], F32)
    tmpf2 = sm.seq("tmpf2", [128, 512], F32)
    scr2 = sm.seq("scr2", [128, 512], BF16)
    sqln = sm.seq("sqln", [128, 512], BF16)
    sq2 = sm.seq("sq2", [128, 512], BF16)
    sq3 = sm.seq("sq3", [128, 512], BF16)
    ones1 = sm.seq("ones1", [1, 128], F32)
    onesc = sm.seq("onesc", [128, 1], BF16)
    tri = sm.seq("tri", [CH, CH], F32)
    idm = sm.seq("idm", [128, 128], BF16)
    scol = sm.seq("scol", [128, 2 * NH + NCD], F32)
    scol2 = sm.seq("scol2", [128, 2 * NH], F32)
    cns = sm.seq("cns", [1, 8], F32)
    U = sm.off
    vtm = sm.at("vtm", [64, NCHUNK * 128], BF16, U + 0)
    qh = sm.at("qh", [128, T], BF16, U + 8192)
    kh = sm.at("kh", [128, T], BF16, U + 12288)
    gh = sm.at("gh", [128, T], F32, U + 16384)
    ke = sm.at("ke", [128, T], BF16, U + 16384)
    ah = sm.at("ah", [128, T], F32, U + 24576)
    qs = sm.at("qs", [128, T], BF16, U + 32768)
    ks = sm.at("ks", [128, T], BF16, U + 36864)
    sst = sm.at("sst", [128, 128], F32, U + 40960)
    pm = sm.at("pm", [64, 2 * CH], BF16, U + 41472)
    sstb = sm.at("sstb", [128, 128], BF16, U + 41728)
    rahg = sm.at("rahg", [128, 512], F32, U + 41984)
    whead = sm.at("whead", [128, 4096], BF16, U + 48032)
    ketm = sm.at("ketm", [64, NCHUNK * 128], BF16, U + 48032)
    wb = sm.at("wb", [128, 4096], BF16, U + 48032)
    GLA_END = U + 48032 + 8192
    hb = sm.at("hb", [128, NH * TT], BF16, U)
    g1 = sm.at("g1", [128, NH * TT], BF16, sm.offsets["ofm"])
    assert GLA_END <= sm.top and U + NH * TT * 2 <= sm.top, (U, GLA_END, sm.top)

    from contextlib import ExitStack
    with ExitStack() as es:
        E = es.enter_context
        ps = [E(nc.psum_tensor(f"ps{i}", [128, 512], F32)) for i in range(6)]
        psT = E(nc.psum_tensor("psT", [128, 1024], BF16))
        psP = E(nc.psum_tensor("psP", [64, 512], F32))
        sems = {e: E(nc.semaphore("sem_" + e)) for e in ENGS}
        block = E(nc.Block())

        pb = PB(nc)

        ld0 = pb.dma("gpsimd", [(x[:, c * T:(c + 1) * T], xT_d[c * 128:(c + 1) * 128, :])
                                for c in range(NCD)], deps=[])
        ldc = pb.dma("sync", [(tri[:, :], tri_d.ap()), (idm[:, :], id_d.ap()),
                              (scol[:, 0:NH], s1_d[:, :]), (scol[:, NH:2 * NH], s2_d[:, :]),
                              (scol[:, 2 * NH:2 * NH + NCD], s3_d[:, :]),
                              (scol2[:, 0:NH], sh1_d[:, :]), (scol2[:, NH:2 * NH], sh2_d[:, :])],
                     deps=[])

        def ones_fn(e):
            e.memset(ones1[:, :], 1.0)
            e.memset(cns[:1, :], LN_EPS)
            return e.memset(onesc[:, :], 1.0)
        pb.step("vector", ones_fn, deps=[])
        pb.step("scalar", lambda e: e.nop(), deps=[ld0, ldc, pb.prev])

        tmpq = [sq2, sq3]
        tmpa = [tmpf, tmpf2]
        def emit_ln(src_t):
            entry = pb.prev
            rl = {}
            bcast = {}
            for g in range(4):
                sl = slice(g * 512, (g + 1) * 512)
                for c in range(NCD):
                    cs = slice(c * T + sl.start, c * T + sl.stop)
                    par = c % 2
                    cbuf = scr2 if par == 0 else sqln
                    s_cast = pb.step("vector", lambda e, cs=cs, cbuf=cbuf: e.tensor_copy(cbuf[:, :512], src_t[:, cs]),
                                     deps=[rl.get(("cb", par)), entry])
                    s_mm1 = pb.step("tensor", lambda e, c=c, cbuf=cbuf: e.matmul(
                        ps[0][:1, :], onesc[:, :], cbuf[:, :512], start=(c == 0), stop=(c == NCD - 1)),
                        deps=[s_cast, rl.get(("ps", 0))])
                    s_sq = pb.step("vector", lambda e, cbuf=cbuf, par=par: e.tensor_tensor(
                        tmpq[par][:, :], cbuf[:, :512], cbuf[:, :512], ALU.mult),
                        deps=[s_cast, rl.get(("sq", par))])
                    s_mm2 = pb.step("tensor", lambda e, c=c, par=par: e.matmul(
                        ps[1][:1, :], onesc[:, :], tmpq[par][:, :], start=(c == 0), stop=(c == NCD - 1)),
                        deps=[s_sq, rl.get(("ps", 1))])
                    rl[("cb", par)] = s_mm1
                    rl[("sq", par)] = s_mm2
                r1 = pb.step("vector", lambda e: e.tensor_scalar(rowa[:1, :], ps[0][:1, :], 1.0 / D, None, ALU.mult),
                             deps=[s_mm1, rl.get("rows")])
                rl[("ps", 0)] = r1
                r2 = pb.step("vector", lambda e: e.tensor_scalar(rowb[:1, :], ps[1][:1, :], 1.0 / D, None, ALU.mult),
                             deps=[s_mm2, r1, rl.get("bA")])
                rl[("ps", 1)] = r2
                r3 = pb.step("vector", lambda e: e.tensor_tensor(rowc[:1, :], rowa[:1, :], rowa[:1, :], ALU.mult),
                             deps=[r2, rl.get("bB")])
                r4 = pb.step("vector", lambda e: e.tensor_tensor(rowb[:1, :], rowb[:1, :], rowc[:1, :], ALU.subtract), deps=[r3])
                r5 = pb.step("scalar", lambda e: e.activation(rowb[:1, :], rowb[:1, :], AF.Ln, bias=cns[:1, :1]), deps=[r4])
                r6 = pb.step("scalar", lambda e: e.activation(rowb[:1, :], rowb[:1, :], AF.Exp, scale=-0.5), deps=[r5])
                r7 = pb.step("vector", lambda e: e.scalar_tensor_tensor(rowc[:1, :], rowa[:1, :], -1.0, rowb[:1, :], ALU.mult, ALU.mult), deps=[r6])
                rl["rows"] = r7
                bA = pb.step("tensor", lambda e: e.matmul(ps[2][:, :], ones1[:, :], rowb[:1, :], start=True, stop=True),
                             deps=[r6, r7, rl.get(("ps", 2))])
                bB = pb.step("tensor", lambda e: e.matmul(ps[3][:, :], ones1[:, :], rowc[:1, :], start=True, stop=True),
                             deps=[r7, rl.get(("ps", 3))])
                rl["bA"] = bA
                rl["bB"] = bB
                ap_last = None
                for c in range(NCD):
                    cs = slice(c * T + sl.start, c * T + sl.stop)
                    par = c % 2
                    a1 = pb.step("vector", lambda e, cs=cs, par=par: e.tensor_tensor(
                        tmpa[par][:, :], src_t[:, cs], ps[2][:, :], ALU.mult),
                        deps=[bA, rl.get(("ta", par))])
                    a2 = pb.step("vector", lambda e, cs=cs, par=par: e.tensor_tensor(
                        xln[:, cs], tmpa[par][:, :], ps[3][:, :], ALU.add),
                        deps=[bB, a1])
                    rl[("ta", par)] = a2
                    ap_last = a2
                rl[("ps", 2)] = ap_last
                rl[("ps", 3)] = ap_last
            pb.prev = pb.step("scalar", lambda e: e.nop(), deps=[ap_last])

        emit_ln(x)

        for h in range(H):
            pb.dma("gpsimd", [
                (whead[:, 0:1024].rearrange("p (c o) -> p c o", o=128), colslice_ap(wq_d, h * 128, 128, 8)),
                (whead[:, 1024:2048].rearrange("p (c o) -> p c o", o=128), colslice_ap(wk_d, h * 128, 128, 8)),
                (whead[:, 2048:3072].rearrange("p (c o) -> p c o", o=128), colslice_ap(wv_d, h * 128, 128, 8)),
                (whead[:, 3072:4096].rearrange("p (c o) -> p c o", o=128), colslice_ap(wg_d, h * 128, 128, 8))])
            lw = pb.prev
            grdr = {}
            for off, dst, mode in ((0, qh, "c"), (1024, kh, "c"), (3072, gh, "s")):
                for g in range(4):
                    sl = slice(g * 512, (g + 1) * 512)
                    bank = g % 2

                    def mmfn(e, off=off, sl=sl, bank=bank):
                        r = None
                        for c in range(NCD):
                            r = e.matmul(ps[bank][:, :], whead[:, off + c * 128:off + (c + 1) * 128],
                                         xln[:, c * T + sl.start:c * T + sl.stop],
                                         start=(c == 0), stop=(c == NCD - 1))
                        return r
                    s_mm = pb.step("tensor", mmfn, deps=[lw, grdr.get(("ps", bank))])
                    if mode == "c":
                        s_ev = pb.step("vector", lambda e, dst=dst, sl=sl, bank=bank: e.tensor_copy(dst[:, sl], ps[bank][:, :]),
                                       deps=[s_mm])
                    else:
                        s_ev = pb.step("scalar", lambda e, dst=dst, sl=sl, bank=bank: e.activation(dst[:, sl], ps[bank][:, :], AF.Sigmoid),
                                       deps=[s_mm])
                    grdr[("ps", bank)] = s_ev
            for cc in range(NCHUNK):
                tsl = slice(cc * CH, (cc + 1) * CH)
                bank = 2 + cc % 2

                def vfn(e, tsl=tsl, bank=bank):
                    r = None
                    for c in range(NCD):
                        r = e.matmul(ps[bank][:64, :128], xln[:, c * T + tsl.start:c * T + tsl.stop],
                                     whead[:, 2048 + c * 128:2048 + (c + 1) * 128],
                                     start=(c == 0), stop=(c == NCD - 1))
                    return r
                s_mm = pb.step("tensor", vfn, deps=[lw, grdr.get(("ps", bank))])
                s_ev = pb.step("vector", lambda e, cc=cc, bank=bank: e.tensor_copy(vtm[:, cc * 128:(cc + 1) * 128], ps[bank][:64, :128]),
                               deps=[s_mm])
                grdr[("ps", bank)] = s_ev
            pb.prev = pb.step("scalar", lambda e: e.nop(), deps=[pb.prev, s_ev])
            pb.step("scalar", lambda e: e.activation(gh[:, :], gh[:, :], AF.Ln))
            for cc in range(NCHUNK):
                sl = slice(cc * CH, (cc + 1) * CH)
                pb.step("vector", lambda e, sl=sl: e.tensor_tensor_scan(gh[:, sl], gh[:, sl], gh[:, sl], 0.0, ALU.add, ALU.bypass))
            pb.step("scalar", lambda e: e.activation(ah[:, :], gh[:, :], AF.Exp))
            eA = pb.prev
            for g in range(4):
                sl = slice(g * 512, (g + 1) * 512)
                pb.step("vector", lambda e, sl=sl: e.tensor_tensor(sq2[:, :], qh[:, sl], qh[:, sl], ALU.mult))
                pb.step("tensor", lambda e: e.matmul(ps[0][:1, :], onesc[:, :], sq2[:, :], start=True, stop=True))
                pb.step("vector", lambda e: e.tensor_copy(rowa[:1, :], ps[0][:1, :]))
                pb.step("scalar", lambda e: e.activation(rowa[:1, :], rowa[:1, :], AF.Ln))
                pb.step("scalar", lambda e: e.activation(rowa[:1, :], rowa[:1, :], AF.Exp, scale=-0.5))
                pb.step("tensor", lambda e: e.matmul(ps[2][:, :], ones1[:, :], rowa[:1, :], start=True, stop=True))
                pb.step("vector", lambda e, sl=sl: e.tensor_tensor(tmpf[:, :], ah[:, sl], ps[2][:, :], ALU.mult),
                        deps=[eA, pb.prev])
                pb.step("vector", lambda e, sl=sl: e.tensor_tensor(qs[:, sl], tmpf[:, :], qh[:, sl], ALU.mult))
            for g in range(4):
                sl = slice(g * 512, (g + 1) * 512)
                pb.step("vector", lambda e, sl=sl: e.tensor_tensor(sq2[:, :], kh[:, sl], kh[:, sl], ALU.mult))
                pb.step("tensor", lambda e: e.matmul(ps[0][:1, :], onesc[:, :], sq2[:, :], start=True, stop=True))
                pb.step("vector", lambda e: e.tensor_copy(rowa[:1, :], ps[0][:1, :]))
                pb.step("scalar", lambda e: e.activation(rowa[:1, :], rowa[:1, :], AF.Ln))
                pb.step("scalar", lambda e: e.activation(rowa[:1, :], rowa[:1, :], AF.Exp, scale=-0.5))
                pb.step("tensor", lambda e: e.matmul(ps[2][:, :], ones1[:, :], rowa[:1, :], start=True, stop=True))
                pb.step("scalar", lambda e, sl=sl: e.activation(rahg[:, :], gh[:, sl], AF.Exp, scale=-1.0))
                pb.step("vector", lambda e: e.tensor_tensor(rahg[:, :], rahg[:, :], ps[2][:, :], ALU.mult))
                pb.step("vector", lambda e, sl=sl: e.tensor_tensor(ks[:, sl], rahg[:, :], kh[:, sl], ALU.mult))
            for cc in range(NCHUNK):
                sl = slice(cc * CH, (cc + 1) * CH)
                col = slice(cc * CH + CH - 1, cc * CH + CH)
                pb.step("vector", lambda e, sl=sl, col=col: e.tensor_scalar(ke[:, sl], ks[:, sl], ah[:, col], None, ALU.mult))
            for cc in range(NCHUNK):
                sl = slice(cc * CH, (cc + 1) * CH)
                pb.step("tensor", lambda e, sl=sl: e.transpose(psT[:64, :128], ke[:, sl], idm[:, :]))
                pb.step("vector", lambda e, cc=cc: e.tensor_copy(ketm[:, cc * 128:(cc + 1) * 128], psT[:64, :128]))
            pb.step("vector", lambda e: e.memset(sst[:, :], 0.0))
            pb.step("vector", lambda e: e.memset(sstb[:, :], 0.0))
            crdr = {}
            elem_done = pb.prev
            s_scopy = {}
            s_supd = {}
            s_ofn = {}
            for cc in range(NCHUNK):
                sl = slice(cc * CH, (cc + 1) * CH)
                col = slice(cc * CH + CH - 1, cc * CH + CH)
                pbank = 2 + cc % 2
                tbank = 5 if cc % 2 == 0 else 0
                slot = (cc % 2) * 64
                s_pmm = pb.step("tensor", lambda e, sl=sl, pbank=pbank: e.matmul(
                    ps[pbank][:64, :64], ks[:, sl], qs[:, sl], start=True, stop=True),
                    deps=[elem_done, crdr.get(("ps", pbank)), pb.prev])
                s_msk = pb.step("vector", lambda e, pbank=pbank, slot=slot: e.tensor_tensor(
                    pm[:, slot:slot + 64], ps[pbank][:64, :64], tri[:, :], ALU.mult),
                    deps=[s_pmm, crdr.get(("pm", slot))])
                crdr[("ps", pbank)] = s_msk

                def ofn(e, sl=sl, cc=cc, slot=slot):
                    e.matmul(ps[4][:, :64], vtm[:, cc * 128:(cc + 1) * 128], pm[:, slot:slot + 64], start=True, stop=False)
                    return e.matmul(ps[4][:, :64], sstb[:, :], qs[:, sl], start=False, stop=True)
                s_ofn[cc] = pb.step("tensor", ofn, deps=[s_msk, s_scopy.get(cc - 1), crdr.get(("ps", 4))])
                crdr[("pm", slot)] = s_ofn[cc]
                s_oev = pb.step("vector", lambda e, sl=sl, h=h: e.tensor_copy(
                    ofm[:, h * T + sl.start:h * T + sl.stop], ps[4][:, :64]), deps=[s_ofn[cc]])
                crdr[("ps", 4)] = s_oev

                def tfn(e, cc=cc, tbank=tbank):
                    return e.matmul(ps[tbank][:, :128], ketm[:, cc * 128:(cc + 1) * 128],
                                    vtm[:, cc * 128:(cc + 1) * 128], start=True, stop=True)
                s_tmm = pb.step("tensor", tfn, deps=[elem_done, crdr.get(("ps", tbank))])
                s_supd[cc] = pb.step("vector", lambda e, col=col, tbank=tbank: e.scalar_tensor_tensor(
                    sst[:, :], sst[:, :], ah[:, col], ps[tbank][:, :128], ALU.mult, ALU.add),
                    deps=[s_tmm, s_scopy.get(cc - 1), s_ofn[cc]])
                crdr[("ps", tbank)] = s_supd[cc]
                s_scopy[cc] = pb.step("vector", lambda e: e.tensor_copy(sstb[:, :], sst[:, :]),
                                      deps=[s_supd[cc], s_ofn[cc]])
            pb.prev = pb.step("scalar", lambda e: e.nop(), deps=[s_scopy[NCHUNK - 1], pb.prev])

        rdr = {}
        gla_done = pb.prev
        ldW = {}
        mmW = {}
        accW = {}
        for dc in range(NCD):
            par = dc % 2
            buf = wa if par == 0 else wb
            ldW[dc] = pb.dma("gpsimd", [(buf[:, 0:1024].rearrange("p (c o) -> p c o", o=128),
                                         colslice_ap(wo_d, dc * 128, 128, 8))],
                             deps=[rdr.get(("wbuf", par)), gla_done])
            for g in range(4):
                sl = slice(g * 512, (g + 1) * 512)
                bank = 2 * par + (g % 2)

                def mmfn(e, sl=sl, buf=buf, bank=bank):
                    r = None
                    for c in range(NCD):
                        r = e.matmul(ps[bank][:, :], buf[:, c * 128:(c + 1) * 128],
                                     ofm[:, c * T + sl.start:c * T + sl.stop],
                                     start=(c == 0), stop=(c == NCD - 1))
                    return r
                mmW[(dc, g)] = pb.step("tensor", mmfn, deps=[ldW[dc], rdr.get(("ps", bank)), gla_done])
                rdr[("wbuf", par)] = mmW[(dc, g)]
                cs = slice(dc * T + sl.start, dc * T + sl.stop)
                accW[(dc, g)] = pb.step("vector", lambda e, cs=cs, bank=bank: e.tensor_tensor(
                    x[:, cs], x[:, cs], ps[bank][:, :], ALU.add), deps=[mmW[(dc, g)]])
                rdr[("ps", bank)] = accW[(dc, g)]
        wo_done = pb.step("scalar", lambda e: e.nop(),
                          deps=[accW[(NCD - 1, 3)], accW[(NCD - 2, 3)], accW[(NCD - 1, 2)]])

        pb.prev = wo_done
        emit_ln(x)
        ln2_done = pb.prev

        ofm_free = wo_done
        hb_free = gla_done
        prev_tt_w3 = ln2_done
        prev_tt_h2 = None
        sA = [scr2, sqln]
        for tt in range(NTT):
            tsl = slice(tt * TT, (tt + 1) * TT)
            mt1, mt2, sil, hm, ld12 = {}, {}, {}, {}, {}
            for oc in range(NH):
                par = oc % 2
                buf = wa if par == 0 else wb
                ld12[oc] = pb.dma("gpsimd", [
                    (buf[:, 0:1024].rearrange("p (c o) -> p c o", o=128),
                     colslice_ap(w1_d, oc * 128, 128, 8)),
                    (buf[:, 1024:2048].rearrange("p (c o) -> p c o", o=128),
                     colslice_ap(w2_d, oc * 128, 128, 8))],
                    deps=[rdr.get(("wbuf", par)), ln2_done])
                b1, b2 = 2 * par, 2 * par + 1

                def m1f(e, tsl=tsl, buf=buf, b1=b1):
                    r = None
                    for c in range(NCD):
                        r = e.matmul(ps[b1][:, :], buf[:, c * 128:(c + 1) * 128],
                                     xln[:, c * T + tsl.start:c * T + tsl.stop],
                                     start=(c == 0), stop=(c == NCD - 1))
                    return r
                mt1[oc] = pb.step("tensor", m1f, deps=[ld12[oc], rdr.get(("ps", b1)), ln2_done])
                rdr[("wbuf", par)] = mt1[oc]

                def m2f(e, tsl=tsl, buf=buf, b2=b2):
                    r = None
                    for c in range(NCD):
                        r = e.matmul(ps[b2][:, :], buf[:, 1024 + c * 128:1024 + (c + 1) * 128],
                                     xln[:, c * T + tsl.start:c * T + tsl.stop],
                                     start=(c == 0), stop=(c == NCD - 1))
                    return r
                mt2[oc] = pb.step("tensor", m2f, deps=[ld12[oc], rdr.get(("ps", b2))])
                rdr[("wbuf", par)] = mt2[oc]
                sil[oc] = pb.step("scalar", lambda e, oc=oc, par=par, b1=b1: e.activation(
                    sA[par][:, :], ps[b1][:, :], AF.Silu, scale=scol[:, oc:oc + 1]),
                    deps=[mt1[oc], rdr.get(("sA", par))])
                rdr[("ps", b1)] = sil[oc]
                hm[oc] = pb.step("vector", lambda e, oc=oc, par=par, b2=b2: e.scalar_tensor_tensor(
                    hb[:, oc * TT:(oc + 1) * TT], ps[b2][:, :], scol[:, NH + oc:NH + oc + 1],
                    sA[par][:, :], ALU.mult, ALU.mult),
                    deps=[mt2[oc], sil[oc], prev_tt_w3, hb_free])
                rdr[("ps", b2)] = hm[oc]
                rdr[("sA", par)] = hm[oc]
            hm_last = hm[NH - 1]
            mmh, gel, ldh = {}, {}, {}
            for oc in range(NH):
                par = oc % 2
                buf = wa if par == 0 else wb
                ldh[oc] = pb.dma("gpsimd", [(buf[:, :].rearrange("p (c o) -> p c o", o=128),
                                             colslice_ap(h1_d, oc * 128, 128, 32))],
                                 deps=[rdr.get(("wbuf", par))])
                bank = par

                def mhf(e, buf=buf, bank=bank):
                    r = None
                    for ic in range(NH):
                        r = e.matmul(ps[bank][:, :], buf[:, ic * 128:(ic + 1) * 128],
                                     hb[:, ic * TT:(ic + 1) * TT],
                                     start=(ic == 0), stop=(ic == NH - 1))
                    return r
                mmh[oc] = pb.step("tensor", mhf, deps=[ldh[oc], rdr.get(("ps", bank)), hm_last])
                rdr[("wbuf", par)] = mmh[oc]
                gel[oc] = pb.step("scalar", lambda e, oc=oc, bank=bank: e.activation(
                    g1[:, oc * TT:(oc + 1) * TT], ps[bank][:, :], AF.Gelu, scale=scol2[:, oc:oc + 1]),
                    deps=[mmh[oc], ofm_free, prev_tt_h2])
                rdr[("ps", bank)] = gel[oc]
            gel_last = gel[NH - 1]
            mmh2, gel2, ldh2 = {}, {}, {}
            for oc in range(NH):
                par = oc % 2
                buf = wa if par == 0 else wb
                ldh2[oc] = pb.dma("gpsimd", [(buf[:, :].rearrange("p (c o) -> p c o", o=128),
                                              colslice_ap(h2_d, oc * 128, 128, 32))],
                                  deps=[rdr.get(("wbuf", par))])
                bank = 2 + par

                def mh2f(e, buf=buf, bank=bank):
                    r = None
                    for ic in range(NH):
                        r = e.matmul(ps[bank][:, :], buf[:, ic * 128:(ic + 1) * 128],
                                     g1[:, ic * TT:(ic + 1) * TT],
                                     start=(ic == 0), stop=(ic == NH - 1))
                    return r
                mmh2[oc] = pb.step("tensor", mh2f, deps=[ldh2[oc], rdr.get(("ps", bank)), gel_last])
                rdr[("wbuf", par)] = mmh2[oc]
                gel2[oc] = pb.step("scalar", lambda e, oc=oc, bank=bank: e.activation(
                    hb[:, oc * TT:(oc + 1) * TT], ps[bank][:, :], AF.Gelu, scale=scol2[:, NH + oc:NH + oc + 1]),
                    deps=[mmh2[oc]])
                rdr[("ps", bank)] = gel2[oc]
            gel2_last = gel2[NH - 1]
            prev_tt_h2 = gel2_last
            mmy, accy, ldy = {}, {}, {}
            for dc in range(NCD):
                par = dc % 2
                buf = wa if par == 0 else wb
                ldy[dc] = pb.dma("gpsimd", [(buf[:, :].rearrange("p (c o) -> p c o", o=128),
                                             colslice_ap(w3_d, dc * 128, 128, 32))],
                                 deps=[rdr.get(("wbuf", par))])
                bank = 4 + par

                def myf(e, buf=buf, bank=bank):
                    r = None
                    for ic in range(NH):
                        r = e.matmul(ps[bank][:, :], buf[:, ic * 128:(ic + 1) * 128],
                                     hb[:, ic * TT:(ic + 1) * TT],
                                     start=(ic == 0), stop=(ic == NH - 1))
                    return r
                mmy[dc] = pb.step("tensor", myf, deps=[ldy[dc], rdr.get(("ps", bank)), gel2_last])
                rdr[("wbuf", par)] = mmy[dc]
                cs = slice(dc * T + tsl.start, dc * T + tsl.stop)
                accy[dc] = pb.step("vector", lambda e, cs=cs, dc=dc, bank=bank: e.scalar_tensor_tensor(
                    x[:, cs], ps[bank][:, :], scol[:, 2 * NH + dc:2 * NH + dc + 1], x[:, cs], ALU.mult, ALU.add),
                    deps=[mmy[dc]])
                rdr[("ps", bank)] = accy[dc]
            prev_tt_w3 = pb.step("scalar", lambda e: e.nop(),
                                 deps=[accy[NCD - 1], accy[NCD - 2]])
            pb.prev = prev_tt_w3

        stv = pb.counts["sync"] + 16 * NCD
        pb.dma("sync", [(yT_d[c * 128:(c + 1) * 128, :], x[:, c * T:(c + 1) * T]) for c in range(NCD)])

        def fin(e):
            e.wait_ge(sems["sync"], stv)
            return e.nop()
        pb.step("sync", fin)

        pb.emit(block, sems)
    return nc


def prep_inputs_A(inputs, b):
    asc = np.ascontiguousarray
    return {
        "xT": asc(inputs["L0_attn_x"][b].T),
        "Wq": asc(inputs["L0_attn_Wq"]), "Wk": asc(inputs["L0_attn_Wk"]),
        "Wv": asc(inputs["L0_attn_Wv"]), "Wg": asc(inputs["L0_attn_Wg"]),
        "Wo": asc(inputs["L0_attn_Wo"]),
        "w1T": asc(inputs["L0_ffn_w1_w"].T), "w2T": asc(inputs["L0_ffn_w2_w"].T),
        "h1T": asc(inputs["L0_ffn_h1_w"].T), "h2T": asc(inputs["L0_ffn_h2_w"].T),
        "w3T": asc(inputs["L0_ffn_w3_w"].T),
        "s1": asc(inputs["L0_ffn_w1_s"].reshape(NH, 128).T),
        "s2": asc(inputs["L0_ffn_w2_s"].reshape(NH, 128).T),
        "sh1": asc(inputs["L0_ffn_h1_s"].reshape(NH, 128).T),
        "sh2": asc(inputs["L0_ffn_h2_s"].reshape(NH, 128).T),
        "s3": asc(inputs["L0_ffn_w3_s"].reshape(NCD, 128).T),
    }


TT2 = 1024
NT2 = T // TT2
NE = 8
NB = 8
TB = 256


def build_launchB(dbg=False):
    nc = bass.Bass("TRN2", target_bir_lowering=False)

    xT_d = nc.dram_tensor("xT", [D, T], F32, kind="ExternalInput").ap()
    win_d = nc.dram_tensor("WinT", [D, D], F32, kind="ExternalInput").ap()
    wlam_d = nc.dram_tensor("WlamT", [D, D], F32, kind="ExternalInput").ap()
    wgate_d = nc.dram_tensor("WgateT", [D, D], F32, kind="ExternalInput").ap()
    wout_d = nc.dram_tensor("WoutT", [D, D], F32, kind="ExternalInput").ap()
    bcols_d = nc.dram_tensor("bcols", [128, 48], F32, kind="ExternalInput").ap()
    gwc_d = nc.dram_tensor("gwc", [128, 64], F32, kind="ExternalInput").ap()
    wgT_d = nc.dram_tensor("wgT", [NE, D, DH], F32, kind="ExternalInput").ap()
    wuT_d = nc.dram_tensor("wuT", [NE, D, DH], F32, kind="ExternalInput").ap()
    wdT_d = nc.dram_tensor("wdT", [NE, DH, D], F32, kind="ExternalInput").ap()
    outT_d = nc.dram_tensor("outT", [D, T], F32, kind="ExternalOutput").ap()
    if dbg:
        dbgA_d = nc.dram_tensor("dbgA", [D, T], F32, kind="ExternalOutput").ap()
        dbgL_d = nc.dram_tensor("dbgL", [D, T], BF16, kind="ExternalOutput").ap()
        dbgH_d = nc.dram_tensor("dbgH", [D, T], BF16, kind="ExternalOutput").ap()
        cmb_d = nc.dram_tensor("cmbo", [8, T], F32, kind="ExternalOutput").ap()
        cmbt_d = nc.dram_tensor("cmbt_o", [128, 128], F32, kind="ExternalOutput").ap()
        dbgZ_d = nc.dram_tensor("dbgZ", [D, T], BF16, kind="ExternalOutput").ap()
        dbgX_d = nc.dram_tensor("dbgX", [D, T], F32, kind="ExternalOutput").ap()

    idf_d = nc.inline_tensor(np.eye(128, dtype=np.float32), name="idf")
    sel_np = np.zeros((8, 1024), dtype=np.float32)
    for e in range(8):
        sel_np[e, e * 128:(e + 1) * 128] = 1.0
    sel_d = nc.inline_tensor(sel_np, name="selc")
    cmb_dram = nc.dram_tensor("cmbscr", [128, 128], F32)

    sm = SMap(nc)
    x = sm.seq("x", [128, NCD * T], F32)
    xln = sm.seq("xln", [128, NCD * T], BF16)
    wa = sm.seq("wa", [128, 4096], BF16)
    wb = sm.seq("wb", [128, 4096], BF16)
    scr3 = sm.seq("scr3", [128, 512], BF16)
    scr4 = sm.seq("scr4", [128, 512], BF16)
    rowa = sm.seq("rowa", [1, 512], F32)
    rowb = sm.seq("rowb", [1, 512], F32)
    rowc = sm.seq("rowc", [1, 512], F32)
    tmpf = sm.seq("tmpf", [128, 512], F32)
    scr2 = sm.seq("scr2", [128, 512], BF16)
    sqln = sm.seq("sqln", [128, 512], BF16)
    ones1 = sm.seq("ones1", [1, 128], F32)
    onesc = sm.seq("onesc", [128, 1], BF16)
    idf = sm.seq("idf", [128, 128], F32)
    sel = sm.seq("sel", [8, 1024], F32)
    bcols = sm.seq("bcols", [128, 48], F32)
    wgw = sm.seq("wgw", [128, 64], BF16)
    cmbt = sm.seq("cmbt", [128, 128], F32)
    cmbu = sm.seq("cmbu", [128, 128], F32)
    cfm8 = sm.seq("cfm8", [8, T], F32)
    cns = sm.seq("cns", [1, 8], F32)
    UB = sm.off
    lam = sm.at("lam", [128, NCD * T], BF16, UB)
    hbuf = sm.at("hbuf", [128, NCD * T], BF16, UB + 32768)
    hmoe = sm.at("hmoe", [128, NH * TT2], BF16, UB)
    cb = sm.at("cb", [128, T], BF16, sm.offsets["rowa"])
    assert UB + 65536 <= sm.top, (UB, sm.top)

    from contextlib import ExitStack
    with ExitStack() as es:
        E = es.enter_context
        ps = [E(nc.psum_tensor(f"ps{i}", [128, 512], F32)) for i in range(6)]
        psT = E(nc.psum_tensor("psT", [128, 512], F32))
        psX = E(nc.psum_tensor("psX", [128, 512], F32))
        sems = {e: E(nc.semaphore("sem_" + e)) for e in ENGS}
        block = E(nc.Block())

        pb = PB(nc)

        pb.dma("gpsimd", [(x[:, c * T:(c + 1) * T], xT_d[c * 128:(c + 1) * 128, :])
                          for c in range(NCD)], deps=[])
        ldc = pb.dma("sync", [(idf[:, :], idf_d.ap()), (sel[:, :], sel_d.ap()),
                              (bcols[:, :], bcols_d[:, :])], deps=[])
        ldw = pb.dma("gpsimd", [(wgw[:, :], gwc_d[:, :])], deps=[])

        def ones_fn(e):
            e.memset(ones1[:, :], 1.0)
            e.memset(cns[:1, :], LN_EPS)
            return e.memset(onesc[:, :], 1.0)
        pb.step("vector", ones_fn, deps=[])
        pb.step("scalar", lambda e: e.nop(), deps=[pb.steps[0], ldc, ldw, pb.prev])

        def emit_ln(src, src_is_bf16=False):
            for g in range(4):
                sl = slice(g * 512, (g + 1) * 512)
                for c in range(NCD):
                    cs = slice(c * T + sl.start, c * T + sl.stop)
                    if src_is_bf16:
                        pb.step("tensor", lambda e, c=c, cs=cs: e.matmul(
                            ps[0][:1, :], onesc[:, :], src[:, cs], start=(c == 0), stop=(c == NCD - 1)))
                        pb.step("scalar", lambda e, cs=cs: e.activation(sqln[:, :], src[:, cs], AF.Square))
                    else:
                        pb.step("vector", lambda e, cs=cs: e.tensor_copy(scr2[:, :], src[:, cs]))
                        cast = pb.prev
                        pb.step("tensor", lambda e, c=c: e.matmul(
                            ps[0][:1, :], onesc[:, :], scr2[:, :], start=(c == 0), stop=(c == NCD - 1)))
                        pb.step("scalar", lambda e: e.activation(sqln[:, :], scr2[:, :], AF.Square),
                                deps=[cast, pb.prev])
                    pb.step("tensor", lambda e, c=c: e.matmul(
                        ps[1][:1, :], onesc[:, :], sqln[:, :], start=(c == 0), stop=(c == NCD - 1)))
                pb.step("vector", lambda e: e.tensor_scalar(rowa[:1, :], ps[0][:1, :], 1.0 / D, None, ALU.mult))
                pb.step("vector", lambda e: e.tensor_scalar(rowb[:1, :], ps[1][:1, :], 1.0 / D, None, ALU.mult))
                pb.step("vector", lambda e: e.tensor_tensor(rowc[:1, :], rowa[:1, :], rowa[:1, :], ALU.mult))
                pb.step("vector", lambda e: e.tensor_tensor(rowb[:1, :], rowb[:1, :], rowc[:1, :], ALU.subtract))
                pb.step("scalar", lambda e: e.activation(rowb[:1, :], rowb[:1, :], AF.Ln, bias=cns[:1, :1]))
                pb.step("scalar", lambda e: e.activation(rowb[:1, :], rowb[:1, :], AF.Exp, scale=-0.5))
                pb.step("vector", lambda e: e.scalar_tensor_tensor(rowc[:1, :], rowa[:1, :], -1.0, rowb[:1, :], ALU.mult, ALU.mult))
                pb.step("tensor", lambda e: e.matmul(ps[2][:, :], ones1[:, :], rowb[:1, :], start=True, stop=True))
                bA = pb.prev
                pb.step("tensor", lambda e: e.matmul(ps[3][:, :], ones1[:, :], rowc[:1, :], start=True, stop=True))
                bB = pb.prev
                for c in range(NCD):
                    cs = slice(c * T + sl.start, c * T + sl.stop)
                    pb.step("vector", lambda e, cs=cs: e.tensor_tensor(tmpf[:, :], src[:, cs], ps[2][:, :], ALU.mult),
                            deps=[bA, pb.prev])
                    pb.step("vector", lambda e, cs=cs: e.tensor_tensor(xln[:, cs], tmpf[:, :], ps[3][:, :], ALU.add),
                            deps=[bB, pb.prev])

        emit_ln(x)
        rdr = {}
        ln3_done = pb.prev
        sA = [scr2, scr3]
        sB = [tmpf]
        for oc in range(NCD):
            par = oc % 2
            buf = wa if par == 0 else wb
            ldL = pb.dma("gpsimd", [(buf[:, 0:1024].rearrange("p (c o) -> p c o", o=128),
                                     colslice_ap(wlam_d, oc * 128, 128, 8)),
                                    (buf[:, 1024:2048].rearrange("p (c o) -> p c o", o=128),
                                     colslice_ap(win_d, oc * 128, 128, 8))],
                         deps=[rdr.get(("wbuf", par)), ln3_done])
            for g in range(4):
                sl = slice(g * 512, (g + 1) * 512)
                b1, b2 = 2 * par, 2 * par + 1

                def mml(e, sl=sl, buf=buf, b1=b1):
                    r = None
                    for c in range(NCD):
                        r = e.matmul(ps[b1][:, :], buf[:, c * 128:(c + 1) * 128],
                                     xln[:, c * T + sl.start:c * T + sl.stop],
                                     start=(c == 0), stop=(c == NCD - 1))
                    return r
                s_mml = pb.step("tensor", mml, deps=[ldL, rdr.get(("ps", b1)), ln3_done])
                rdr[("wbuf", par)] = s_mml
                s_sig = pb.step("scalar", lambda e, oc=oc, sl=sl, b1=b1: e.activation(
                    lam[:, oc * T + sl.start:oc * T + sl.stop], ps[b1][:, :], AF.Sigmoid,
                    bias=bcols[:, 8 + oc:9 + oc]), deps=[s_mml])
                rdr[("ps", b1)] = s_sig

                def mmi(e, sl=sl, buf=buf, b2=b2):
                    r = None
                    for c in range(NCD):
                        r = e.matmul(ps[b2][:, :], buf[:, 1024 + c * 128:1024 + (c + 1) * 128],
                                     xln[:, c * T + sl.start:c * T + sl.stop],
                                     start=(c == 0), stop=(c == NCD - 1))
                    return r
                s_mmi = pb.step("tensor", mmi, deps=[ldL, rdr.get(("ps", b2))])
                rdr[("wbuf", par)] = s_mmi
                s_sil = pb.step("scalar", lambda e, oc=oc, par=par, b2=b2: e.activation(
                    sA[par][:, :], ps[b2][:, :], AF.Silu, bias=bcols[:, oc:1 + oc]),
                    deps=[s_mmi, rdr.get(("sA", par))])
                rdr[("ps", b2)] = s_sil
                s_om = pb.step("vector", lambda e, oc=oc, sl=sl: e.tensor_scalar(
                    tmpf[:, :], lam[:, oc * T + sl.start:oc * T + sl.stop], -1.0, 1.0, ALU.mult, ALU.add),
                    deps=[s_sig, rdr.get("tmpf")])
                s_hm = pb.step("vector", lambda e, oc=oc, sl=sl, par=par: e.tensor_tensor(
                    hbuf[:, oc * T + sl.start:oc * T + sl.stop], tmpf[:, :], sA[par][:, :], ALU.mult),
                    deps=[s_om, s_sil])
                rdr[("sA", par)] = s_hm
                rdr["tmpf"] = s_hm
        pb.prev = pb.step("scalar", lambda e: e.nop(), deps=[pb.prev])
        for c in range(NCD):
            for b in range(1, NB):
                s_prev = slice(c * T + (b - 1) * TB, c * T + b * TB)
                s_cur = slice(c * T + b * TB, c * T + (b + 1) * TB)
                pb.step("vector", lambda e, s_prev=s_prev, s_cur=s_cur: e.tensor_tensor(
                    tmpf[:, :TB], lam[:, s_cur], hbuf[:, s_prev], ALU.mult))
                pb.step("vector", lambda e, s_cur=s_cur: e.tensor_tensor(
                    hbuf[:, s_cur], hbuf[:, s_cur], tmpf[:, :TB], ALU.add))
        if dbg:
            pb.dma("sync", [(dbgL_d[c * 128:(c + 1) * 128, :], lam[:, c * T:(c + 1) * T])
                            for c in range(NCD)])
            pb.dma("sync", [(dbgH_d[c * 128:(c + 1) * 128, :], hbuf[:, c * T:(c + 1) * T])
                            for c in range(NCD)])
        for g in range(4):
            sl = slice(g * 512, (g + 1) * 512)
            for c in range(NCD):
                cs = slice(c * T + sl.start, c * T + sl.stop)
                pb.step("tensor", lambda e, c=c, cs=cs: e.matmul(
                    ps[0][:1, :], onesc[:, :], hbuf[:, cs], start=(c == 0), stop=(c == NCD - 1)))
                pb.step("scalar", lambda e, cs=cs: e.activation(sqln[:, :], hbuf[:, cs], AF.Square))
                pb.step("tensor", lambda e, c=c: e.matmul(
                    ps[1][:1, :], onesc[:, :], sqln[:, :], start=(c == 0), stop=(c == NCD - 1)))
            pb.step("vector", lambda e: e.tensor_scalar(rowa[:1, :], ps[0][:1, :], 1.0 / D, None, ALU.mult))
            pb.step("vector", lambda e: e.tensor_scalar(rowb[:1, :], ps[1][:1, :], 1.0 / D, None, ALU.mult))
            pb.step("vector", lambda e: e.tensor_tensor(rowc[:1, :], rowa[:1, :], rowa[:1, :], ALU.mult))
            pb.step("vector", lambda e: e.tensor_tensor(rowb[:1, :], rowb[:1, :], rowc[:1, :], ALU.subtract))
            pb.step("scalar", lambda e: e.activation(rowb[:1, :], rowb[:1, :], AF.Ln, bias=cns[:1, :1]))
            pb.step("scalar", lambda e: e.activation(rowb[:1, :], rowb[:1, :], AF.Exp, scale=-0.5))
            pb.step("vector", lambda e: e.scalar_tensor_tensor(rowc[:1, :], rowa[:1, :], -1.0, rowb[:1, :], ALU.mult, ALU.mult))
            pb.step("tensor", lambda e: e.matmul(ps[2][:, :], ones1[:, :], rowb[:1, :], start=True, stop=True))
            bA = pb.prev
            pb.step("tensor", lambda e: e.matmul(ps[3][:, :], ones1[:, :], rowc[:1, :], start=True, stop=True))
            bB = pb.prev
            for c in range(NCD):
                cs = slice(c * T + sl.start, c * T + sl.stop)
                pb.dma("gpsimd", [(wa[:, 0:1024].rearrange("p (cc o) -> p cc o", o=128),
                                   colslice_ap(wgate_d, c * 128, 128, 8))])
                lw = pb.prev

                def mmg(e, sl=sl):
                    r = None
                    for cc in range(NCD):
                        r = e.matmul(ps[0][:, :], wa[:, cc * 128:(cc + 1) * 128],
                                     xln[:, cc * T + sl.start:cc * T + sl.stop],
                                     start=(cc == 0), stop=(cc == NCD - 1))
                    return r
                pb.step("tensor", mmg, deps=[lw, pb.prev])
                pb.step("scalar", lambda e, c=c: e.activation(scr2[:, :], ps[0][:, :], AF.Silu, bias=bcols[:, 16 + c:17 + c]))
                pb.step("vector", lambda e, cs=cs: e.tensor_tensor(tmpf[:, :], hbuf[:, cs], ps[2][:, :], ALU.mult),
                        deps=[bA, pb.prev])
                pb.step("vector", lambda e: e.tensor_tensor(tmpf[:, :], tmpf[:, :], ps[3][:, :], ALU.add),
                        deps=[bB, pb.prev])
                pb.step("vector", lambda e, c=c: e.tensor_scalar(
                    tmpf[:, :], tmpf[:, :], bcols[:, 32 + c:33 + c], bcols[:, 40 + c:41 + c], ALU.mult, ALU.add))
                pb.step("vector", lambda e, cs=cs: e.tensor_tensor(hbuf[:, cs], tmpf[:, :], scr2[:, :], ALU.mult))
        zdone = pb.prev
        if dbg:
            pb.dma("sync", [(dbgZ_d[c * 128:(c + 1) * 128, :], hbuf[:, c * T:(c + 1) * T])
                            for c in range(NCD)])
            pb.dma("sync", [(dbgX_d[c * 128:(c + 1) * 128, :], x[:, c * T:(c + 1) * T])
                            for c in range(NCD)])
            zdone = pb.prev
        for dc in range(NCD):
            par = dc % 2
            buf = wa if par == 0 else wb
            ldo = pb.dma("gpsimd", [(buf[:, 0:1024].rearrange("p (c o) -> p c o", o=128),
                                     colslice_ap(wout_d, dc * 128, 128, 8))],
                         deps=[rdr.get(("wbuf", par)), zdone])
            for g in range(4):
                sl = slice(g * 512, (g + 1) * 512)
                bank = 2 * par + (g % 2)

                def mmo(e, sl=sl, buf=buf, bank=bank):
                    r = None
                    for c in range(NCD):
                        r = e.matmul(ps[bank][:, :], buf[:, c * 128:(c + 1) * 128],
                                     hbuf[:, c * T + sl.start:c * T + sl.stop],
                                     start=(c == 0), stop=(c == NCD - 1))
                    return r
                s_mm = pb.step("tensor", mmo, deps=[ldo, rdr.get(("ps", bank)), zdone])
                rdr[("wbuf", par)] = s_mm
                cs = slice(dc * T + sl.start, dc * T + sl.stop)
                s_acc = pb.step("vector", lambda e, cs=cs, dc=dc, bank=bank: e.scalar_tensor_tensor(
                    x[:, cs], ps[bank][:, :], bcols[:, 24 + dc:25 + dc], x[:, cs], ALU.add, ALU.add),
                    deps=[s_mm])
                rdr[("ps", bank)] = s_acc
        pb.prev = pb.step("scalar", lambda e: e.nop(), deps=[pb.prev])

        if dbg:
            _d = pb.dma("sync", [(dbgA_d[c * 128:(c + 1) * 128, :], x[:, c * T:(c + 1) * T])
                                 for c in range(NCD)])
            pb.prev = _d
        emit_ln(x)
        for t in range(16):
            tsl = slice(t * 128, (t + 1) * 128)

            def mmlg(e, tsl=tsl):
                r = None
                for c in range(NCD):
                    r = e.matmul(ps[0][:, :8], xln[:, c * T + tsl.start:c * T + tsl.stop],
                                 wgw[:, c * 8:(c + 1) * 8], start=(c == 0), stop=(c == NCD - 1))
                return r
            pb.step("tensor", mmlg)
            L8 = lambda a, b: slice(a, b)
            pb.step("vector", lambda e: e.tensor_copy(tmpf[:, 0:8], ps[0][:, :8]))
            pb.step("vector", lambda e: e.reduce_max(tmpf[:, 8:9], tmpf[:, 0:8], axis=AX.X))
            pb.step("vector", lambda e: e.tensor_scalar(tmpf[:, 16:24], tmpf[:, 0:8], tmpf[:, 8:9], None, ALU.is_ge))
            pb.step("vector", lambda e: e.scalar_tensor_tensor(tmpf[:, 24:32], tmpf[:, 16:24], -1e30, tmpf[:, 0:8], ALU.mult, ALU.add))
            pb.step("vector", lambda e: e.reduce_max(tmpf[:, 10:11], tmpf[:, 24:32], axis=AX.X))
            pb.step("vector", lambda e: e.tensor_scalar(tmpf[:, 9:10], tmpf[:, 8:9], -1.0, None, ALU.mult))
            pb.step("scalar", lambda e: e.activation(tmpf[:, 11:12], tmpf[:, 10:11], AF.Exp, bias=tmpf[:, 9:10]))
            pb.step("vector", lambda e: e.tensor_scalar(tmpf[:, 11:12], tmpf[:, 11:12], 1.0, None, ALU.add))
            pb.step("vector", lambda e: e.reciprocal(tmpf[:, 12:13], tmpf[:, 11:12]))
            pb.step("scalar", lambda e: e.activation(tmpf[:, 24:32], tmpf[:, 0:8], AF.Exp, bias=tmpf[:, 9:10]))
            pb.step("vector", lambda e: e.tensor_scalar(tmpf[:, 16:24], tmpf[:, 0:8], tmpf[:, 10:11], None, ALU.is_ge))
            pb.step("vector", lambda e: e.tensor_tensor(tmpf[:, 24:32], tmpf[:, 16:24], tmpf[:, 24:32], ALU.mult))
            pb.step("vector", lambda e, t=t: e.tensor_scalar(cmbt[:, t * 8:(t + 1) * 8], tmpf[:, 24:32], tmpf[:, 12:13], None, ALU.mult))
        pb.step("tensor", lambda e: e.transpose(psT[:, :128], cmbt[:, :], idf[:, :]))
        pb.step("vector", lambda e: e.tensor_copy(cmbu[:, :], psT[:, :128]))
        pb.dma("sync", [(cmb_dram.ap(), cmbu[:, :])])
        pb.dma("sync", [(cfm8[:, :].rearrange("e (t j) -> e t j", j=128),
                        cmb_dram.ap().rearrange("(t e) j -> e t j", e=8))])
        if dbg:
            pb.dma("sync", [(cmb_d[:, :], cfm8[:, :])])
            pb.dma("sync", [(cmbt_d[:, :], cmbt[:, :])])
        cmb_ready = pb.prev
        sAm = [scr2, scr3]
        sBm = [sqln, scr4]
        prev_dn = cmb_ready
        prev_ue_last = None
        for ex in range(NE):
            cb_evs = []
            for g in range(4):
                sl = slice(g * 512, (g + 1) * 512)
                s_cb = pb.step("tensor", lambda e, ex=ex, sl=sl: e.matmul(
                    ps[4][:, :], sel[:, ex * 128:(ex + 1) * 128], cfm8[:, sl], start=True, stop=True),
                    deps=[cmb_ready, rdr.get(("ps", 4)), prev_ue_last])
                s_ev = pb.step("vector", lambda e, sl=sl: e.tensor_copy(cb[:, sl], ps[4][:, :]),
                               deps=[s_cb, prev_ue_last])
                rdr[("ps", 4)] = s_ev
                cb_evs.append(s_ev)
            cb_done = cb_evs[-1]
            for t2 in range(NT2):
                base = t2 * TT2
                hms = []
                for oc in range(NH):
                    par = oc % 2
                    buf = wa if par == 0 else wb
                    ldgu = pb.dma("gpsimd", [
                        (buf[:, 0:1024].rearrange("p (c o) -> p c o", o=128),
                         colslice_ap(wgT_d[ex], oc * 128, 128, 8)),
                        (buf[:, 1024:2048].rearrange("p (c o) -> p c o", o=128),
                         colslice_ap(wuT_d[ex], oc * 128, 128, 8))],
                        deps=[rdr.get(("wbuf", par))])
                    for hf in range(2):
                        sl = slice(base + hf * 512, base + hf * 512 + 512)
                        b1, b2 = 2 * par, 2 * par + 1

                        def mmg2(e, sl=sl, buf=buf, b1=b1):
                            r = None
                            for c in range(NCD):
                                r = e.matmul(ps[b1][:, :], buf[:, c * 128:(c + 1) * 128],
                                             xln[:, c * T + sl.start:c * T + sl.stop],
                                             start=(c == 0), stop=(c == NCD - 1))
                            return r
                        s_g = pb.step("tensor", mmg2, deps=[ldgu, rdr.get(("ps", b1))])
                        rdr[("wbuf", par)] = s_g

                        def mmu2(e, sl=sl, buf=buf, b2=b2):
                            r = None
                            for c in range(NCD):
                                r = e.matmul(ps[b2][:, :], buf[:, 1024 + c * 128:1024 + (c + 1) * 128],
                                             xln[:, c * T + sl.start:c * T + sl.stop],
                                             start=(c == 0), stop=(c == NCD - 1))
                            return r
                        s_u = pb.step("tensor", mmu2, deps=[ldgu, rdr.get(("ps", b2))])
                        rdr[("wbuf", par)] = s_u
                        s_si = pb.step("scalar", lambda e, par=par, b1=b1: e.activation(
                            sAm[par][:, :], ps[b1][:, :], AF.Silu),
                            deps=[s_g, rdr.get(("sA", par))])
                        rdr[("ps", b1)] = s_si
                        s_ue = pb.step("vector", lambda e, sl=sl, par=par, b2=b2: e.tensor_tensor(
                            sBm[par][:, :], ps[b2][:, :], cb[:, sl], ALU.mult),
                            deps=[s_u, cb_done, rdr.get(("sB", par))])
                        rdr[("ps", b2)] = s_ue
                        prev_ue_last = s_ue
                        s_hm = pb.step("vector", lambda e, oc=oc, hf=hf, par=par: e.tensor_tensor(
                            hmoe[:, oc * TT2 + hf * 512:oc * TT2 + hf * 512 + 512],
                            sAm[par][:, :], sBm[par][:, :], ALU.mult),
                            deps=[s_si, s_ue, prev_dn])
                        rdr[("sA", par)] = s_hm
                        rdr[("sB", par)] = s_hm
                        hms.append(s_hm)
                hm_all = hms[-1]
                for dc in range(NCD):
                    par = dc % 2
                    buf = wa if par == 0 else wb
                    ldd = pb.dma("gpsimd", [(buf[:, :].rearrange("p (c o) -> p c o", o=128),
                                             colslice_ap(wdT_d[ex], dc * 128, 128, 32))],
                                 deps=[rdr.get(("wbuf", par))])
                    for hf in range(2):
                        bank = 5 if hf == 0 else 2 * par

                        def mmd(e, hf=hf, buf=buf, bank=bank):
                            r = None
                            for ic in range(NH):
                                r = e.matmul(ps[bank][:, :], buf[:, ic * 128:(ic + 1) * 128],
                                             hmoe[:, ic * TT2 + hf * 512:ic * TT2 + hf * 512 + 512],
                                             start=(ic == 0), stop=(ic == NH - 1))
                            return r
                        s_d = pb.step("tensor", mmd, deps=[ldd, rdr.get(("ps", bank)), hm_all])
                        rdr[("wbuf", par)] = s_d
                        cs = slice(dc * T + base + hf * 512, dc * T + base + hf * 512 + 512)
                        s_a = pb.step("vector", lambda e, cs=cs, bank=bank: e.tensor_tensor(
                            x[:, cs], x[:, cs], ps[bank][:, :], ALU.add), deps=[s_d])
                        rdr[("ps", bank)] = s_a
                        prev_dn = s_a
        pb.prev = pb.step("scalar", lambda e: e.nop(), deps=[pb.prev, prev_dn])

        stv = pb.counts["sync"] + 16 * NCD
        pb.dma("sync", [(outT_d[c * 128:(c + 1) * 128, :], x[:, c * T:(c + 1) * T]) for c in range(NCD)])

        def fin(e):
            e.wait_ge(sems["sync"], stv)
            return e.nop()
        pb.step("sync", fin)

        pb.emit(block, sems)
    return nc


def prep_inputs_B(inputs, yT_list, core):
    asc = np.ascontiguousarray

    def colpack(vecs):
        out = np.zeros((128, 48), np.float32)
        for j, v in enumerate(vecs):
            out[:, j * 8:(j + 1) * 8] = v.reshape(8, 128).T
        return out
    xT = np.concatenate([yT_list[b][:, core * TB:(core + 1) * TB] for b in range(NB)], axis=1)
    gwc = np.zeros((128, 64), np.float32)
    gw = inputs["L1_ffn_gate_w"]
    for c in range(8):
        gwc[:, c * 8:(c + 1) * 8] = gw[:, c * 128:(c + 1) * 128].T
    return {
        "xT": asc(xT),
        "WinT": asc(inputs["L1_attn_W_in"].T), "WlamT": asc(inputs["L1_attn_W_lam"].T),
        "WgateT": asc(inputs["L1_attn_W_gate"].T), "WoutT": asc(inputs["L1_attn_W_out"].T),
        "bcols": colpack([inputs["L1_attn_b_in"], inputs["L1_attn_b_lam"],
                          inputs["L1_attn_b_gate"], inputs["L1_attn_b_out"],
                          inputs["L1_attn_gamma"], inputs["L1_attn_beta"]]),
        "gwc": asc(gwc),
        "wgT": asc(np.transpose(inputs["L1_ffn_wg"], (0, 2, 1))),
        "wuT": asc(np.transpose(inputs["L1_ffn_wu"], (0, 2, 1))),
        "wdT": asc(np.transpose(inputs["L1_ffn_wd"], (0, 2, 1))),
    }


from concourse.bass_utils import run_bass_kernel_spmd

_CACHE = {}


def kernel(**inputs):
    if "A" not in _CACHE:
        _CACHE["A"] = build_launchA()
        _CACHE["B"] = build_launchB()
    ncA, ncB = _CACHE["A"], _CACHE["B"]

    in_maps_A = [prep_inputs_A(inputs, b) for b in range(8)]
    resA = run_bass_kernel_spmd(ncA, in_maps_A, core_ids=list(range(8)))
    yT_list = [resA.results[b]["yT"] for b in range(8)]

    in_maps_B = [prep_inputs_B(inputs, yT_list, c) for c in range(8)]
    resB = run_bass_kernel_spmd(ncB, in_maps_B, core_ids=list(range(8)))

    out = np.empty((8, 2048, 1024), np.float32)
    for c in range(8):
        out[:, c * TB:(c + 1) * TB, :] = resB.results[c]["outT"].T.reshape(NB, TB, D)
    return out


# version 05
# speedup vs baseline: 1.0843x; 1.0454x over previous
import sys
sys.path.insert(0, "/opt/trn_rl_repo")
import numpy as np
import ml_dtypes
import concourse.bass as bass
import concourse.mybir as mybir

F32 = mybir.dt.float32
BF16 = mybir.dt.bfloat16
AF = mybir.ActivationFunctionType
ALU = mybir.AluOpType
AX = mybir.AxisListType

ENGS = ("tensor", "vector", "scalar", "gpsimd", "sync")


class Step:
    __slots__ = ("eng", "fn", "deps", "idx", "val", "ndma", "_sem_holder")

    def __init__(self, eng, fn, deps, idx, ndma):
        self.eng, self.fn, self.deps, self.idx, self.ndma = eng, fn, deps, idx, ndma
        self.val = None


class PB:
    def __init__(self, nc):
        self.nc = nc
        self.steps = []
        self.counts = {e: 0 for e in ENGS}
        self.prev = None

    def step(self, eng, fn, deps=None, ndma=0):
        if deps is None:
            deps = [self.prev] if self.prev is not None else []
        deps = [d for d in deps if d is not None]
        s = Step(eng, fn, deps, len(self.steps), ndma)
        self.counts[eng] += (16 * ndma) if ndma else 1
        s.val = self.counts[eng]
        self.steps.append(s)
        self.prev = s
        return s

    def dma(self, queue, pairs, deps=None):
        holder = {}

        def fn(e):
            sem = holder["sem"]
            for o, i in pairs:
                e.dma_start(out=o, in_=i).then_inc(sem, 16)
            return None
        s = self.step(queue, fn, deps=deps, ndma=len(pairs))
        s._sem_holder = holder
        return s

    def emit(self, block, sems):
        per_eng = {e: [s for s in self.steps if s.eng == e] for e in ENGS}

        def make_body(eng):
            def body(e):
                watermark = {}
                for s in per_eng[eng]:
                    best = {}
                    for d in s.deps:
                        if d.eng not in best or d.val > best[d.eng].val:
                            best[d.eng] = d
                    for d in best.values():
                        if d.eng == eng == "tensor":
                            continue
                        if d.val <= watermark.get(d.eng, -1):
                            continue
                        e.wait_ge(sems[d.eng], d.val)
                        watermark[d.eng] = d.val
                    if s.ndma:
                        s._sem_holder["sem"] = sems[eng]
                        s.fn(e)
                    else:
                        last = s.fn(e)
                        assert last is not None, f"step {s.idx} on {eng} returned None"
                        last.then_inc(sems[eng], 1)
            return body

        for eng in ENGS:
            if per_eng[eng]:
                getattr(block, eng)(make_body(eng))


D, T = 1024, 2048
NCD = 8
H, DK = 8, 128
CH = 64
NCHUNK = T // CH
TT = 512
NTT = T // TT
DH = 4096
NH = DH // 128
LN_EPS = 1e-5


def colslice_ap(w_d, o0, width, nchunks):
    return w_d.rearrange("(c p) o -> p c o", p=128)[:, :nchunks, o0:o0 + width]


class SMap:

    def __init__(self, nc):
        self.nc = nc
        self.off = (nc.sbuf_base + 31) & ~31
        self.top = nc.sbuf_top
        self.offsets = {}

    def at(self, name, shape, dtype, offset):
        return self.nc.alloc_sbuf_tensor_at(name, shape, dtype, offset=offset)

    def seq(self, name, shape, dtype):
        per = int(np.prod(shape[1:])) * mybir.dt.size(dtype)
        h = self.at(name, shape, dtype, self.off)
        self.offsets[name] = self.off
        self.off = (self.off + per + 31) & ~31
        assert self.off <= self.top, f"SBUF overflow at {name}: {self.off} > {self.top}"
        return h


def build_launchA():
    nc = bass.Bass("TRN2", target_bir_lowering=False)

    xT_d = nc.dram_tensor("xT", [D, T], F32, kind="ExternalInput").ap()
    wq_d = nc.dram_tensor("Wq", [D, D], F32, kind="ExternalInput").ap()
    wk_d = nc.dram_tensor("Wk", [D, D], F32, kind="ExternalInput").ap()
    wv_d = nc.dram_tensor("Wv", [D, D], F32, kind="ExternalInput").ap()
    wg_d = nc.dram_tensor("Wg", [D, D], F32, kind="ExternalInput").ap()
    wo_d = nc.dram_tensor("Wo", [D, D], F32, kind="ExternalInput").ap()
    w1_d = nc.dram_tensor("w1T", [D, DH], F32, kind="ExternalInput").ap()
    w2_d = nc.dram_tensor("w2T", [D, DH], F32, kind="ExternalInput").ap()
    h1_d = nc.dram_tensor("h1T", [DH, DH], F32, kind="ExternalInput").ap()
    h2_d = nc.dram_tensor("h2T", [DH, DH], F32, kind="ExternalInput").ap()
    w3_d = nc.dram_tensor("w3T", [DH, D], F32, kind="ExternalInput").ap()
    s1_d = nc.dram_tensor("s1", [128, NH], F32, kind="ExternalInput").ap()
    s2_d = nc.dram_tensor("s2", [128, NH], F32, kind="ExternalInput").ap()
    sh1_d = nc.dram_tensor("sh1", [128, NH], F32, kind="ExternalInput").ap()
    sh2_d = nc.dram_tensor("sh2", [128, NH], F32, kind="ExternalInput").ap()
    s3_d = nc.dram_tensor("s3", [128, NCD], F32, kind="ExternalInput").ap()
    yT_d = nc.dram_tensor("yT", [D, T], F32, kind="ExternalOutput").ap()

    tri_d = nc.inline_tensor(np.triu(np.ones((CH, CH), dtype=np.float32)), name="tric")
    id_d = nc.inline_tensor(np.eye(128, dtype=ml_dtypes.bfloat16), name="idc")

    sm = SMap(nc)
    x = sm.seq("x", [128, NCD * T], F32)
    xln = sm.seq("xln", [128, NCD * T], BF16)
    ofm = sm.seq("ofm", [128, H * T], BF16)
    wa = sm.seq("wa", [128, 4096], BF16)
    rowa = sm.seq("rowa", [1, 512], F32)
    rowb = sm.seq("rowb", [1, 512], F32)
    rowc = sm.seq("rowc", [1, 512], F32)
    tmpf = sm.seq("tmpf", [128, 512], F32)
    tmpf2 = sm.seq("tmpf2", [128, 512], F32)
    scr2 = sm.seq("scr2", [128, 512], BF16)
    sqln = sm.seq("sqln", [128, 512], BF16)
    sq2 = sm.seq("sq2", [128, 512], BF16)
    sq3 = sm.seq("sq3", [128, 512], BF16)
    ones1 = sm.seq("ones1", [1, 128], F32)
    onesc = sm.seq("onesc", [128, 1], BF16)
    tri = sm.seq("tri", [CH, CH], F32)
    idm = sm.seq("idm", [128, 128], BF16)
    scol = sm.seq("scol", [128, 2 * NH + NCD], F32)
    scol2 = sm.seq("scol2", [128, 2 * NH], F32)
    cns = sm.seq("cns", [1, 8], F32)
    U = sm.off
    vtm = sm.at("vtm", [64, NCHUNK * 128], BF16, U + 0)
    qh = sm.at("qh", [128, T], BF16, U + 8192)
    kh = sm.at("kh", [128, T], BF16, U + 12288)
    gh = sm.at("gh", [128, T], F32, U + 16384)
    ke = sm.at("ke", [128, T], BF16, U + 16384)
    ah = sm.at("ah", [128, T], F32, U + 24576)
    qs = sm.at("qs", [128, T], BF16, U + 32768)
    ks = sm.at("ks", [128, T], BF16, U + 36864)
    sst = sm.at("sst", [128, 128], F32, U + 40960)
    pm = sm.at("pm", [64, 2 * CH], BF16, U + 41472)
    sstb = sm.at("sstb", [128, 128], BF16, U + 41728)
    rahg = sm.at("rahg", [128, 512], F32, U + 41984)
    whead = sm.at("whead", [128, 4096], BF16, U + 48032)
    ketm = sm.at("ketm", [64, NCHUNK * 128], BF16, U + 48032)
    wb = sm.at("wb", [128, 4096], BF16, U + 48032)
    GLA_END = U + 48032 + 8192
    hb = sm.at("hb", [128, NH * TT], BF16, U)
    g1 = sm.at("g1", [128, NH * TT], BF16, sm.offsets["ofm"])
    assert GLA_END <= sm.top and U + NH * TT * 2 <= sm.top, (U, GLA_END, sm.top)

    from contextlib import ExitStack
    with ExitStack() as es:
        E = es.enter_context
        ps = [E(nc.psum_tensor(f"ps{i}", [128, 512], F32)) for i in range(6)]
        psT = E(nc.psum_tensor("psT", [128, 1024], BF16))
        psP = E(nc.psum_tensor("psP", [64, 512], F32))
        sems = {e: E(nc.semaphore("sem_" + e)) for e in ENGS}
        block = E(nc.Block())

        pb = PB(nc)

        ld0 = pb.dma("gpsimd", [(x[:, c * T:(c + 1) * T], xT_d[c * 128:(c + 1) * 128, :])
                                for c in range(NCD)], deps=[])
        ldc = pb.dma("sync", [(tri[:, :], tri_d.ap()), (idm[:, :], id_d.ap()),
                              (scol[:, 0:NH], s1_d[:, :]), (scol[:, NH:2 * NH], s2_d[:, :]),
                              (scol[:, 2 * NH:2 * NH + NCD], s3_d[:, :]),
                              (scol2[:, 0:NH], sh1_d[:, :]), (scol2[:, NH:2 * NH], sh2_d[:, :])],
                     deps=[])

        def ones_fn(e):
            e.memset(ones1[:, :], 1.0)
            e.memset(cns[:1, :], LN_EPS)
            return e.memset(onesc[:, :], 1.0)
        pb.step("vector", ones_fn, deps=[])
        pb.step("scalar", lambda e: e.nop(), deps=[ld0, ldc, pb.prev])

        tmpq = [sq2, sq3]
        tmpa = [tmpf, tmpf2]
        def emit_ln(src_t):
            entry = pb.prev
            rl = {}
            bcast = {}
            for g in range(4):
                sl = slice(g * 512, (g + 1) * 512)
                for c in range(NCD):
                    cs = slice(c * T + sl.start, c * T + sl.stop)
                    par = c % 2
                    cbuf = scr2 if par == 0 else sqln
                    s_cast = pb.step("vector", lambda e, cs=cs, cbuf=cbuf: e.tensor_copy(cbuf[:, :512], src_t[:, cs]),
                                     deps=[rl.get(("cb", par)), entry])
                    s_mm1 = pb.step("tensor", lambda e, c=c, cbuf=cbuf: e.matmul(
                        ps[0][:1, :], onesc[:, :], cbuf[:, :512], start=(c == 0), stop=(c == NCD - 1)),
                        deps=[s_cast, rl.get(("ps", 0))])
                    s_sq = pb.step("vector", lambda e, cbuf=cbuf, par=par: e.tensor_tensor(
                        tmpq[par][:, :], cbuf[:, :512], cbuf[:, :512], ALU.mult),
                        deps=[s_cast, rl.get(("sq", par))])
                    s_mm2 = pb.step("tensor", lambda e, c=c, par=par: e.matmul(
                        ps[1][:1, :], onesc[:, :], tmpq[par][:, :], start=(c == 0), stop=(c == NCD - 1)),
                        deps=[s_sq, rl.get(("ps", 1))])
                    rl[("cb", par)] = s_mm1
                    rl[("sq", par)] = s_mm2
                r1 = pb.step("vector", lambda e: e.tensor_scalar(rowa[:1, :], ps[0][:1, :], 1.0 / D, None, ALU.mult),
                             deps=[s_mm1, rl.get("rows")])
                rl[("ps", 0)] = r1
                r2 = pb.step("vector", lambda e: e.tensor_scalar(rowb[:1, :], ps[1][:1, :], 1.0 / D, None, ALU.mult),
                             deps=[s_mm2, r1, rl.get("bA")])
                rl[("ps", 1)] = r2
                r3 = pb.step("vector", lambda e: e.tensor_tensor(rowc[:1, :], rowa[:1, :], rowa[:1, :], ALU.mult),
                             deps=[r2, rl.get("bB")])
                r4 = pb.step("vector", lambda e: e.tensor_tensor(rowb[:1, :], rowb[:1, :], rowc[:1, :], ALU.subtract), deps=[r3])
                r5 = pb.step("scalar", lambda e: e.activation(rowb[:1, :], rowb[:1, :], AF.Ln, bias=cns[:1, :1]), deps=[r4])
                r6 = pb.step("scalar", lambda e: e.activation(rowb[:1, :], rowb[:1, :], AF.Exp, scale=-0.5), deps=[r5])
                r7 = pb.step("vector", lambda e: e.scalar_tensor_tensor(rowc[:1, :], rowa[:1, :], -1.0, rowb[:1, :], ALU.mult, ALU.mult), deps=[r6])
                rl["rows"] = r7
                bA = pb.step("tensor", lambda e: e.matmul(ps[2][:, :], ones1[:, :], rowb[:1, :], start=True, stop=True),
                             deps=[r6, r7, rl.get(("ps", 2))])
                bB = pb.step("tensor", lambda e: e.matmul(ps[3][:, :], ones1[:, :], rowc[:1, :], start=True, stop=True),
                             deps=[r7, rl.get(("ps", 3))])
                rl["bA"] = bA
                rl["bB"] = bB
                ap_last = None
                for c in range(NCD):
                    cs = slice(c * T + sl.start, c * T + sl.stop)
                    par = c % 2
                    a1 = pb.step("vector", lambda e, cs=cs, par=par: e.tensor_tensor(
                        tmpa[par][:, :], src_t[:, cs], ps[2][:, :], ALU.mult),
                        deps=[bA, rl.get(("ta", par))])
                    a2 = pb.step("vector", lambda e, cs=cs, par=par: e.tensor_tensor(
                        xln[:, cs], tmpa[par][:, :], ps[3][:, :], ALU.add),
                        deps=[bB, a1])
                    rl[("ta", par)] = a2
                    ap_last = a2
                rl[("ps", 2)] = ap_last
                rl[("ps", 3)] = ap_last
            pb.prev = pb.step("scalar", lambda e: e.nop(), deps=[ap_last])

        emit_ln(x)

        for h in range(H):
            pb.dma("gpsimd", [
                (whead[:, 0:1024].rearrange("p (c o) -> p c o", o=128), colslice_ap(wq_d, h * 128, 128, 8)),
                (whead[:, 1024:2048].rearrange("p (c o) -> p c o", o=128), colslice_ap(wk_d, h * 128, 128, 8)),
                (whead[:, 2048:3072].rearrange("p (c o) -> p c o", o=128), colslice_ap(wv_d, h * 128, 128, 8)),
                (whead[:, 3072:4096].rearrange("p (c o) -> p c o", o=128), colslice_ap(wg_d, h * 128, 128, 8))])
            lw = pb.prev
            grdr = {}
            for off, dst, mode in ((0, qh, "c"), (1024, kh, "c"), (3072, gh, "s")):
                for g in range(4):
                    sl = slice(g * 512, (g + 1) * 512)
                    bank = g % 2

                    def mmfn(e, off=off, sl=sl, bank=bank):
                        r = None
                        for c in range(NCD):
                            r = e.matmul(ps[bank][:, :], whead[:, off + c * 128:off + (c + 1) * 128],
                                         xln[:, c * T + sl.start:c * T + sl.stop],
                                         start=(c == 0), stop=(c == NCD - 1))
                        return r
                    s_mm = pb.step("tensor", mmfn, deps=[lw, grdr.get(("ps", bank))])
                    if mode == "c":
                        s_ev = pb.step("vector", lambda e, dst=dst, sl=sl, bank=bank: e.tensor_copy(dst[:, sl], ps[bank][:, :]),
                                       deps=[s_mm])
                    else:
                        s_ev = pb.step("scalar", lambda e, dst=dst, sl=sl, bank=bank: e.activation(dst[:, sl], ps[bank][:, :], AF.Sigmoid),
                                       deps=[s_mm])
                    grdr[("ps", bank)] = s_ev
            for cc in range(NCHUNK):
                tsl = slice(cc * CH, (cc + 1) * CH)
                bank = 2 + cc % 2

                def vfn(e, tsl=tsl, bank=bank):
                    r = None
                    for c in range(NCD):
                        r = e.matmul(ps[bank][:64, :128], xln[:, c * T + tsl.start:c * T + tsl.stop],
                                     whead[:, 2048 + c * 128:2048 + (c + 1) * 128],
                                     start=(c == 0), stop=(c == NCD - 1))
                    return r
                s_mm = pb.step("tensor", vfn, deps=[lw, grdr.get(("ps", bank))])
                s_ev = pb.step("vector", lambda e, cc=cc, bank=bank: e.tensor_copy(vtm[:, cc * 128:(cc + 1) * 128], ps[bank][:64, :128]),
                               deps=[s_mm])
                grdr[("ps", bank)] = s_ev
            pb.prev = pb.step("scalar", lambda e: e.nop(), deps=[pb.prev, s_ev])
            pb.step("scalar", lambda e: e.activation(gh[:, :], gh[:, :], AF.Ln))
            for cc in range(NCHUNK):
                sl = slice(cc * CH, (cc + 1) * CH)
                pb.step("vector", lambda e, sl=sl: e.tensor_tensor_scan(gh[:, sl], gh[:, sl], gh[:, sl], 0.0, ALU.add, ALU.bypass))
            pb.step("scalar", lambda e: e.activation(ah[:, :], gh[:, :], AF.Exp))
            eA = pb.prev
            for g in range(4):
                sl = slice(g * 512, (g + 1) * 512)
                pb.step("vector", lambda e, sl=sl: e.tensor_tensor(sq2[:, :], qh[:, sl], qh[:, sl], ALU.mult))
                pb.step("tensor", lambda e: e.matmul(ps[0][:1, :], onesc[:, :], sq2[:, :], start=True, stop=True))
                pb.step("vector", lambda e: e.tensor_copy(rowa[:1, :], ps[0][:1, :]))
                pb.step("scalar", lambda e: e.activation(rowa[:1, :], rowa[:1, :], AF.Ln))
                pb.step("scalar", lambda e: e.activation(rowa[:1, :], rowa[:1, :], AF.Exp, scale=-0.5))
                pb.step("tensor", lambda e: e.matmul(ps[2][:, :], ones1[:, :], rowa[:1, :], start=True, stop=True))
                pb.step("vector", lambda e, sl=sl: e.tensor_tensor(tmpf[:, :], ah[:, sl], ps[2][:, :], ALU.mult),
                        deps=[eA, pb.prev])
                pb.step("vector", lambda e, sl=sl: e.tensor_tensor(qs[:, sl], tmpf[:, :], qh[:, sl], ALU.mult))
            for g in range(4):
                sl = slice(g * 512, (g + 1) * 512)
                pb.step("vector", lambda e, sl=sl: e.tensor_tensor(sq2[:, :], kh[:, sl], kh[:, sl], ALU.mult))
                pb.step("tensor", lambda e: e.matmul(ps[0][:1, :], onesc[:, :], sq2[:, :], start=True, stop=True))
                pb.step("vector", lambda e: e.tensor_copy(rowa[:1, :], ps[0][:1, :]))
                pb.step("scalar", lambda e: e.activation(rowa[:1, :], rowa[:1, :], AF.Ln))
                pb.step("scalar", lambda e: e.activation(rowa[:1, :], rowa[:1, :], AF.Exp, scale=-0.5))
                pb.step("tensor", lambda e: e.matmul(ps[2][:, :], ones1[:, :], rowa[:1, :], start=True, stop=True))
                pb.step("scalar", lambda e, sl=sl: e.activation(rahg[:, :], gh[:, sl], AF.Exp, scale=-1.0))
                pb.step("vector", lambda e: e.tensor_tensor(rahg[:, :], rahg[:, :], ps[2][:, :], ALU.mult))
                pb.step("vector", lambda e, sl=sl: e.tensor_tensor(ks[:, sl], rahg[:, :], kh[:, sl], ALU.mult))
            for cc in range(NCHUNK):
                sl = slice(cc * CH, (cc + 1) * CH)
                col = slice(cc * CH + CH - 1, cc * CH + CH)
                pb.step("vector", lambda e, sl=sl, col=col: e.tensor_scalar(ke[:, sl], ks[:, sl], ah[:, col], None, ALU.mult))
            for cc in range(NCHUNK):
                sl = slice(cc * CH, (cc + 1) * CH)
                pb.step("tensor", lambda e, sl=sl: e.transpose(psT[:64, :128], ke[:, sl], idm[:, :]))
                pb.step("vector", lambda e, cc=cc: e.tensor_copy(ketm[:, cc * 128:(cc + 1) * 128], psT[:64, :128]))
            pb.step("vector", lambda e: e.memset(sst[:, :], 0.0))
            pb.step("vector", lambda e: e.memset(sstb[:, :], 0.0))
            crdr = {}
            elem_done = pb.prev
            s_scopy = {}
            s_supd = {}
            s_ofn = {}
            for cc in range(NCHUNK):
                sl = slice(cc * CH, (cc + 1) * CH)
                col = slice(cc * CH + CH - 1, cc * CH + CH)
                pbank = 2 + cc % 2
                tbank = 5 if cc % 2 == 0 else 0
                slot = (cc % 2) * 64
                s_pmm = pb.step("tensor", lambda e, sl=sl, pbank=pbank: e.matmul(
                    ps[pbank][:64, :64], ks[:, sl], qs[:, sl], start=True, stop=True),
                    deps=[elem_done, crdr.get(("ps", pbank)), pb.prev])
                s_msk = pb.step("vector", lambda e, pbank=pbank, slot=slot: e.tensor_tensor(
                    pm[:, slot:slot + 64], ps[pbank][:64, :64], tri[:, :], ALU.mult),
                    deps=[s_pmm, crdr.get(("pm", slot))])
                crdr[("ps", pbank)] = s_msk

                def ofn(e, sl=sl, cc=cc, slot=slot):
                    e.matmul(ps[4][:, :64], vtm[:, cc * 128:(cc + 1) * 128], pm[:, slot:slot + 64], start=True, stop=False)
                    return e.matmul(ps[4][:, :64], sstb[:, :], qs[:, sl], start=False, stop=True)
                s_ofn[cc] = pb.step("tensor", ofn, deps=[s_msk, s_scopy.get(cc - 1), crdr.get(("ps", 4))])
                crdr[("pm", slot)] = s_ofn[cc]
                s_oev = pb.step("vector", lambda e, sl=sl, h=h: e.tensor_copy(
                    ofm[:, h * T + sl.start:h * T + sl.stop], ps[4][:, :64]), deps=[s_ofn[cc]])
                crdr[("ps", 4)] = s_oev

                def tfn(e, cc=cc, tbank=tbank):
                    return e.matmul(ps[tbank][:, :128], ketm[:, cc * 128:(cc + 1) * 128],
                                    vtm[:, cc * 128:(cc + 1) * 128], start=True, stop=True)
                s_tmm = pb.step("tensor", tfn, deps=[elem_done, crdr.get(("ps", tbank))])
                s_supd[cc] = pb.step("vector", lambda e, col=col, tbank=tbank: e.scalar_tensor_tensor(
                    sst[:, :], sst[:, :], ah[:, col], ps[tbank][:, :128], ALU.mult, ALU.add),
                    deps=[s_tmm, s_scopy.get(cc - 1), s_ofn[cc]])
                crdr[("ps", tbank)] = s_supd[cc]
                s_scopy[cc] = pb.step("vector", lambda e: e.tensor_copy(sstb[:, :], sst[:, :]),
                                      deps=[s_supd[cc], s_ofn[cc]])
            pb.prev = pb.step("scalar", lambda e: e.nop(), deps=[s_scopy[NCHUNK - 1], pb.prev])

        rdr = {}
        gla_done = pb.prev
        ldW = {}
        mmW = {}
        accW = {}
        for dc in range(NCD):
            par = dc % 2
            buf = wa if par == 0 else wb
            ldW[dc] = pb.dma("gpsimd", [(buf[:, 0:1024].rearrange("p (c o) -> p c o", o=128),
                                         colslice_ap(wo_d, dc * 128, 128, 8))],
                             deps=[rdr.get(("wbuf", par)), gla_done])
            for g in range(4):
                sl = slice(g * 512, (g + 1) * 512)
                bank = 2 * par + (g % 2)

                def mmfn(e, sl=sl, buf=buf, bank=bank):
                    r = None
                    for c in range(NCD):
                        r = e.matmul(ps[bank][:, :], buf[:, c * 128:(c + 1) * 128],
                                     ofm[:, c * T + sl.start:c * T + sl.stop],
                                     start=(c == 0), stop=(c == NCD - 1))
                    return r
                mmW[(dc, g)] = pb.step("tensor", mmfn, deps=[ldW[dc], rdr.get(("ps", bank)), gla_done])
                rdr[("wbuf", par)] = mmW[(dc, g)]
                cs = slice(dc * T + sl.start, dc * T + sl.stop)
                accW[(dc, g)] = pb.step("vector", lambda e, cs=cs, bank=bank: e.tensor_tensor(
                    x[:, cs], x[:, cs], ps[bank][:, :], ALU.add), deps=[mmW[(dc, g)]])
                rdr[("ps", bank)] = accW[(dc, g)]
        wo_done = pb.step("scalar", lambda e: e.nop(),
                          deps=[accW[(NCD - 1, 3)], accW[(NCD - 2, 3)], accW[(NCD - 1, 2)]])

        pb.prev = wo_done
        emit_ln(x)
        ln2_done = pb.prev

        ofm_free = wo_done
        hb_free = gla_done
        prev_tt_w3 = ln2_done
        prev_tt_h2 = None
        sA = [scr2, sqln]
        for tt in range(NTT):
            tsl = slice(tt * TT, (tt + 1) * TT)
            mt1, mt2, sil, hm, ld12 = {}, {}, {}, {}, {}
            for oc in range(NH):
                par = oc % 2
                buf = wa if par == 0 else wb
                ld12[oc] = pb.dma("gpsimd", [
                    (buf[:, 0:1024].rearrange("p (c o) -> p c o", o=128),
                     colslice_ap(w1_d, oc * 128, 128, 8)),
                    (buf[:, 1024:2048].rearrange("p (c o) -> p c o", o=128),
                     colslice_ap(w2_d, oc * 128, 128, 8))],
                    deps=[rdr.get(("wbuf", par)), ln2_done])
                b1, b2 = 2 * par, 2 * par + 1

                def m1f(e, tsl=tsl, buf=buf, b1=b1):
                    r = None
                    for c in range(NCD):
                        r = e.matmul(ps[b1][:, :], buf[:, c * 128:(c + 1) * 128],
                                     xln[:, c * T + tsl.start:c * T + tsl.stop],
                                     start=(c == 0), stop=(c == NCD - 1))
                    return r
                mt1[oc] = pb.step("tensor", m1f, deps=[ld12[oc], rdr.get(("ps", b1)), ln2_done])
                rdr[("wbuf", par)] = mt1[oc]

                def m2f(e, tsl=tsl, buf=buf, b2=b2):
                    r = None
                    for c in range(NCD):
                        r = e.matmul(ps[b2][:, :], buf[:, 1024 + c * 128:1024 + (c + 1) * 128],
                                     xln[:, c * T + tsl.start:c * T + tsl.stop],
                                     start=(c == 0), stop=(c == NCD - 1))
                    return r
                mt2[oc] = pb.step("tensor", m2f, deps=[ld12[oc], rdr.get(("ps", b2))])
                rdr[("wbuf", par)] = mt2[oc]
                sil[oc] = pb.step("scalar", lambda e, oc=oc, par=par, b1=b1: e.activation(
                    sA[par][:, :], ps[b1][:, :], AF.Silu, scale=scol[:, oc:oc + 1]),
                    deps=[mt1[oc], rdr.get(("sA", par))])
                rdr[("ps", b1)] = sil[oc]
                hm[oc] = pb.step("vector", lambda e, oc=oc, par=par, b2=b2: e.scalar_tensor_tensor(
                    hb[:, oc * TT:(oc + 1) * TT], ps[b2][:, :], scol[:, NH + oc:NH + oc + 1],
                    sA[par][:, :], ALU.mult, ALU.mult),
                    deps=[mt2[oc], sil[oc], prev_tt_w3, hb_free])
                rdr[("ps", b2)] = hm[oc]
                rdr[("sA", par)] = hm[oc]
            hm_last = hm[NH - 1]
            mmh, gel, ldh = {}, {}, {}
            for oc in range(NH):
                par = oc % 2
                buf = wa if par == 0 else wb
                ldh[oc] = pb.dma("gpsimd", [(buf[:, :].rearrange("p (c o) -> p c o", o=128),
                                             colslice_ap(h1_d, oc * 128, 128, 32))],
                                 deps=[rdr.get(("wbuf", par))])
                bank = par

                def mhf(e, buf=buf, bank=bank):
                    r = None
                    for ic in range(NH):
                        r = e.matmul(ps[bank][:, :], buf[:, ic * 128:(ic + 1) * 128],
                                     hb[:, ic * TT:(ic + 1) * TT],
                                     start=(ic == 0), stop=(ic == NH - 1))
                    return r
                mmh[oc] = pb.step("tensor", mhf, deps=[ldh[oc], rdr.get(("ps", bank)), hm_last])
                rdr[("wbuf", par)] = mmh[oc]
                gel[oc] = pb.step("scalar", lambda e, oc=oc, bank=bank: e.activation(
                    g1[:, oc * TT:(oc + 1) * TT], ps[bank][:, :], AF.Gelu, scale=scol2[:, oc:oc + 1]),
                    deps=[mmh[oc], ofm_free, prev_tt_h2])
                rdr[("ps", bank)] = gel[oc]
            gel_last = gel[NH - 1]
            mmh2, gel2, ldh2 = {}, {}, {}
            for oc in range(NH):
                par = oc % 2
                buf = wa if par == 0 else wb
                ldh2[oc] = pb.dma("gpsimd", [(buf[:, :].rearrange("p (c o) -> p c o", o=128),
                                              colslice_ap(h2_d, oc * 128, 128, 32))],
                                  deps=[rdr.get(("wbuf", par))])
                bank = 2 + par

                def mh2f(e, buf=buf, bank=bank):
                    r = None
                    for ic in range(NH):
                        r = e.matmul(ps[bank][:, :], buf[:, ic * 128:(ic + 1) * 128],
                                     g1[:, ic * TT:(ic + 1) * TT],
                                     start=(ic == 0), stop=(ic == NH - 1))
                    return r
                mmh2[oc] = pb.step("tensor", mh2f, deps=[ldh2[oc], rdr.get(("ps", bank)), gel_last])
                rdr[("wbuf", par)] = mmh2[oc]
                gel2[oc] = pb.step("scalar", lambda e, oc=oc, bank=bank: e.activation(
                    hb[:, oc * TT:(oc + 1) * TT], ps[bank][:, :], AF.Gelu, scale=scol2[:, NH + oc:NH + oc + 1]),
                    deps=[mmh2[oc]])
                rdr[("ps", bank)] = gel2[oc]
            gel2_last = gel2[NH - 1]
            prev_tt_h2 = gel2_last
            mmy, accy, ldy = {}, {}, {}
            for dc in range(NCD):
                par = dc % 2
                buf = wa if par == 0 else wb
                ldy[dc] = pb.dma("gpsimd", [(buf[:, :].rearrange("p (c o) -> p c o", o=128),
                                             colslice_ap(w3_d, dc * 128, 128, 32))],
                                 deps=[rdr.get(("wbuf", par))])
                bank = 4 + par

                def myf(e, buf=buf, bank=bank):
                    r = None
                    for ic in range(NH):
                        r = e.matmul(ps[bank][:, :], buf[:, ic * 128:(ic + 1) * 128],
                                     hb[:, ic * TT:(ic + 1) * TT],
                                     start=(ic == 0), stop=(ic == NH - 1))
                    return r
                mmy[dc] = pb.step("tensor", myf, deps=[ldy[dc], rdr.get(("ps", bank)), gel2_last])
                rdr[("wbuf", par)] = mmy[dc]
                cs = slice(dc * T + tsl.start, dc * T + tsl.stop)
                accy[dc] = pb.step("vector", lambda e, cs=cs, dc=dc, bank=bank: e.scalar_tensor_tensor(
                    x[:, cs], ps[bank][:, :], scol[:, 2 * NH + dc:2 * NH + dc + 1], x[:, cs], ALU.mult, ALU.add),
                    deps=[mmy[dc]])
                rdr[("ps", bank)] = accy[dc]
            prev_tt_w3 = pb.step("scalar", lambda e: e.nop(),
                                 deps=[accy[NCD - 1], accy[NCD - 2]])
            pb.prev = prev_tt_w3

        stv = pb.counts["sync"] + 16 * NCD
        pb.dma("sync", [(yT_d[c * 128:(c + 1) * 128, :], x[:, c * T:(c + 1) * T]) for c in range(NCD)])

        def fin(e):
            e.wait_ge(sems["sync"], stv)
            return e.nop()
        pb.step("sync", fin)

        pb.emit(block, sems)
    return nc


def prep_inputs_A(inputs, b):
    asc = np.ascontiguousarray
    return {
        "xT": asc(inputs["L0_attn_x"][b].T),
        "Wq": asc(inputs["L0_attn_Wq"]), "Wk": asc(inputs["L0_attn_Wk"]),
        "Wv": asc(inputs["L0_attn_Wv"]), "Wg": asc(inputs["L0_attn_Wg"]),
        "Wo": asc(inputs["L0_attn_Wo"]),
        "w1T": asc(inputs["L0_ffn_w1_w"].T), "w2T": asc(inputs["L0_ffn_w2_w"].T),
        "h1T": asc(inputs["L0_ffn_h1_w"].T), "h2T": asc(inputs["L0_ffn_h2_w"].T),
        "w3T": asc(inputs["L0_ffn_w3_w"].T),
        "s1": asc(inputs["L0_ffn_w1_s"].reshape(NH, 128).T),
        "s2": asc(inputs["L0_ffn_w2_s"].reshape(NH, 128).T),
        "sh1": asc(inputs["L0_ffn_h1_s"].reshape(NH, 128).T),
        "sh2": asc(inputs["L0_ffn_h2_s"].reshape(NH, 128).T),
        "s3": asc(inputs["L0_ffn_w3_s"].reshape(NCD, 128).T),
    }


TT2 = 1024
NT2 = T // TT2
NE = 8
NB = 8
TB = 256


def build_launchB(dbg=False):
    nc = bass.Bass("TRN2", target_bir_lowering=False)

    xT_d = nc.dram_tensor("xT", [D, T], F32, kind="ExternalInput").ap()
    win_d = nc.dram_tensor("WinT", [D, D], F32, kind="ExternalInput").ap()
    wlam_d = nc.dram_tensor("WlamT", [D, D], F32, kind="ExternalInput").ap()
    wgate_d = nc.dram_tensor("WgateT", [D, D], F32, kind="ExternalInput").ap()
    wout_d = nc.dram_tensor("WoutT", [D, D], F32, kind="ExternalInput").ap()
    bcols_d = nc.dram_tensor("bcols", [128, 48], F32, kind="ExternalInput").ap()
    gwc_d = nc.dram_tensor("gwc", [128, 64], F32, kind="ExternalInput").ap()
    grows_d = nc.dram_tensor("grows", [1, D], F32, kind="ExternalInput").ap()
    wgT_d = nc.dram_tensor("wgT", [NE, D, DH], F32, kind="ExternalInput").ap()
    wuT_d = nc.dram_tensor("wuT", [NE, D, DH], F32, kind="ExternalInput").ap()
    wdT_d = nc.dram_tensor("wdT", [NE, DH, D], F32, kind="ExternalInput").ap()
    outT_d = nc.dram_tensor("outT", [D, T], F32, kind="ExternalOutput").ap()
    if dbg:
        dbgA_d = nc.dram_tensor("dbgA", [D, T], F32, kind="ExternalOutput").ap()
        dbgL_d = nc.dram_tensor("dbgL", [D, T], BF16, kind="ExternalOutput").ap()
        dbgH_d = nc.dram_tensor("dbgH", [D, T], BF16, kind="ExternalOutput").ap()
        cmb_d = nc.dram_tensor("cmbo", [8, T], BF16, kind="ExternalOutput").ap()
        cmbt_d = nc.dram_tensor("cmbt_o", [128, 128], F32, kind="ExternalOutput").ap()
        dbgZ_d = nc.dram_tensor("dbgZ", [D, T], BF16, kind="ExternalOutput").ap()
        dbgX_d = nc.dram_tensor("dbgX", [D, T], F32, kind="ExternalOutput").ap()

    idf_d = nc.inline_tensor(np.eye(128, dtype=np.float32), name="idf")
    sel_np = np.zeros((8, 1024), dtype=ml_dtypes.bfloat16)
    for e in range(8):
        sel_np[e, e * 128:(e + 1) * 128] = 1.0
    sel_d = nc.inline_tensor(sel_np, name="selc")
    cmb_dram = nc.dram_tensor("cmbscr", [128, 128], F32)

    sm = SMap(nc)
    x = sm.seq("x", [128, NCD * T], F32)
    xln = sm.seq("xln", [128, NCD * T], BF16)
    wa = sm.seq("wa", [128, 4096], BF16)
    wb = sm.seq("wb", [128, 4096], BF16)
    scr3 = sm.seq("scr3", [128, 512], BF16)
    scr4 = sm.seq("scr4", [128, 512], BF16)
    tmpf2 = sm.seq("tmpf2", [128, 512], F32)
    grows = sm.seq("grows", [1, D], F32)
    rowa = sm.seq("rowa", [1, 512], F32)
    rowb = sm.seq("rowb", [1, 512], F32)
    rowc = sm.seq("rowc", [1, 512], F32)
    tmpf = sm.seq("tmpf", [128, 512], F32)
    scr2 = sm.seq("scr2", [128, 512], BF16)
    sqln = sm.seq("sqln", [128, 512], BF16)
    ones1 = sm.seq("ones1", [1, 128], F32)
    onesc = sm.seq("onesc", [128, 1], BF16)
    idf = sm.seq("idf", [128, 128], F32)
    sel = sm.seq("sel", [8, 1024], BF16)
    bcols = sm.seq("bcols", [128, 48], F32)
    wgw = sm.seq("wgw", [128, 64], BF16)
    cmbt = sm.seq("cmbt", [128, 128], F32)
    cmbu = sm.seq("cmbu", [128, 128], F32)
    cfm8 = sm.seq("cfm8", [8, T], BF16)
    cns = sm.seq("cns", [1, 8], F32)
    UB = sm.off
    lam = sm.at("lam", [128, NCD * T], BF16, UB)
    hbuf = sm.at("hbuf", [128, NCD * T], BF16, UB + 32768)
    hmoe = sm.at("hmoe", [128, NH * TT2], BF16, UB)
    cb = sm.at("cb", [128, T], BF16, sm.offsets["rowa"])
    assert UB + 65536 <= sm.top, (UB, sm.top)

    from contextlib import ExitStack
    with ExitStack() as es:
        E = es.enter_context
        ps = [E(nc.psum_tensor(f"ps{i}", [128, 512], F32)) for i in range(6)]
        psT = E(nc.psum_tensor("psT", [128, 512], F32))
        psX = E(nc.psum_tensor("psX", [128, 512], F32))
        sems = {e: E(nc.semaphore("sem_" + e)) for e in ENGS}
        block = E(nc.Block())

        pb = PB(nc)

        pb.dma("gpsimd", [(x[:, c * T:(c + 1) * T], xT_d[c * 128:(c + 1) * 128, :])
                          for c in range(NCD)], deps=[])
        ldc = pb.dma("sync", [(idf[:, :], idf_d.ap()), (sel[:, :], sel_d.ap()),
                              (bcols[:, :], bcols_d[:, :]), (grows[:1, :], grows_d[:, :])], deps=[])
        ldw = pb.dma("gpsimd", [(wgw[:, :], gwc_d[:, :])], deps=[])

        def ones_fn(e):
            e.memset(ones1[:, :], 1.0)
            e.memset(cns[:1, :], LN_EPS)
            return e.memset(onesc[:, :], 1.0)
        pb.step("vector", ones_fn, deps=[])
        pb.step("scalar", lambda e: e.nop(), deps=[pb.steps[0], ldc, ldw, pb.prev])

        tmpq = [scr3, scr4]
        tmpa = [tmpf, tmpf2]

        def emit_ln(src_t):
            entry = pb.prev
            rl = {}
            ap_last = None
            for g in range(4):
                sl = slice(g * 512, (g + 1) * 512)
                for c in range(NCD):
                    cs = slice(c * T + sl.start, c * T + sl.stop)
                    par = c % 2
                    cbuf = scr2 if par == 0 else sqln
                    s_cast = pb.step("vector", lambda e, cs=cs, cbuf=cbuf: e.tensor_copy(cbuf[:, :512], src_t[:, cs]),
                                     deps=[rl.get(("cb", par)), entry])
                    s_mm1 = pb.step("tensor", lambda e, c=c, cbuf=cbuf: e.matmul(
                        ps[0][:1, :], onesc[:, :], cbuf[:, :512], start=(c == 0), stop=(c == NCD - 1)),
                        deps=[s_cast, rl.get(("ps", 0))])
                    s_sq = pb.step("vector", lambda e, cbuf=cbuf, par=par: e.tensor_tensor(
                        tmpq[par][:, :], cbuf[:, :512], cbuf[:, :512], ALU.mult),
                        deps=[s_cast, rl.get(("sq", par))])
                    s_mm2 = pb.step("tensor", lambda e, c=c, par=par: e.matmul(
                        ps[1][:1, :], onesc[:, :], tmpq[par][:, :], start=(c == 0), stop=(c == NCD - 1)),
                        deps=[s_sq, rl.get(("ps", 1))])
                    rl[("cb", par)] = s_mm1
                    rl[("sq", par)] = s_mm2
                r1 = pb.step("vector", lambda e: e.tensor_scalar(rowa[:1, :], ps[0][:1, :], 1.0 / D, None, ALU.mult),
                             deps=[s_mm1, rl.get("rows")])
                rl[("ps", 0)] = r1
                r2 = pb.step("vector", lambda e: e.tensor_scalar(rowb[:1, :], ps[1][:1, :], 1.0 / D, None, ALU.mult),
                             deps=[s_mm2, r1, rl.get("bA")])
                rl[("ps", 1)] = r2
                r3 = pb.step("vector", lambda e: e.tensor_tensor(rowc[:1, :], rowa[:1, :], rowa[:1, :], ALU.mult),
                             deps=[r2, rl.get("bB")])
                r4 = pb.step("vector", lambda e: e.tensor_tensor(rowb[:1, :], rowb[:1, :], rowc[:1, :], ALU.subtract), deps=[r3])
                r5 = pb.step("scalar", lambda e: e.activation(rowb[:1, :], rowb[:1, :], AF.Ln, bias=cns[:1, :1]), deps=[r4])
                r6 = pb.step("scalar", lambda e: e.activation(rowb[:1, :], rowb[:1, :], AF.Exp, scale=-0.5), deps=[r5])
                r7 = pb.step("vector", lambda e: e.scalar_tensor_tensor(rowc[:1, :], rowa[:1, :], -1.0, rowb[:1, :], ALU.mult, ALU.mult), deps=[r6])
                rl["rows"] = r7
                bA = pb.step("tensor", lambda e: e.matmul(ps[2][:, :], ones1[:, :], rowb[:1, :], start=True, stop=True),
                             deps=[r6, r7, rl.get(("ps", 2))])
                bB = pb.step("tensor", lambda e: e.matmul(ps[3][:, :], ones1[:, :], rowc[:1, :], start=True, stop=True),
                             deps=[r7, rl.get(("ps", 3))])
                rl["bA"] = bA
                rl["bB"] = bB
                for c in range(NCD):
                    cs = slice(c * T + sl.start, c * T + sl.stop)
                    par = c % 2
                    a1 = pb.step("vector", lambda e, cs=cs, par=par: e.tensor_tensor(
                        tmpa[par][:, :], src_t[:, cs], ps[2][:, :], ALU.mult),
                        deps=[bA, rl.get(("ta", par))])
                    a2 = pb.step("vector", lambda e, cs=cs, par=par: e.tensor_tensor(
                        xln[:, cs], tmpa[par][:, :], ps[3][:, :], ALU.add),
                        deps=[bB, a1])
                    rl[("ta", par)] = a2
                    ap_last = a2
                rl[("ps", 2)] = ap_last
                rl[("ps", 3)] = ap_last
            pb.prev = pb.step("scalar", lambda e: e.nop(), deps=[ap_last])

        emit_ln(x)
        rdr = {}
        ln3_done = pb.prev
        sA = [scr2, scr3]
        sB = [tmpf]
        for oc in range(NCD):
            par = oc % 2
            buf = wa if par == 0 else wb
            ldL = pb.dma("gpsimd", [(buf[:, 0:1024].rearrange("p (c o) -> p c o", o=128),
                                     colslice_ap(wlam_d, oc * 128, 128, 8)),
                                    (buf[:, 1024:2048].rearrange("p (c o) -> p c o", o=128),
                                     colslice_ap(win_d, oc * 128, 128, 8))],
                         deps=[rdr.get(("wbuf", par)), ln3_done])
            for g in range(4):
                sl = slice(g * 512, (g + 1) * 512)
                b1, b2 = 2 * par, 2 * par + 1

                def mml(e, sl=sl, buf=buf, b1=b1):
                    r = None
                    for c in range(NCD):
                        r = e.matmul(ps[b1][:, :], buf[:, c * 128:(c + 1) * 128],
                                     xln[:, c * T + sl.start:c * T + sl.stop],
                                     start=(c == 0), stop=(c == NCD - 1))
                    return r
                s_mml = pb.step("tensor", mml, deps=[ldL, rdr.get(("ps", b1)), ln3_done])
                rdr[("wbuf", par)] = s_mml
                s_sig = pb.step("scalar", lambda e, oc=oc, sl=sl, b1=b1: e.activation(
                    lam[:, oc * T + sl.start:oc * T + sl.stop], ps[b1][:, :], AF.Sigmoid,
                    bias=bcols[:, 8 + oc:9 + oc]), deps=[s_mml])
                rdr[("ps", b1)] = s_sig

                def mmi(e, sl=sl, buf=buf, b2=b2):
                    r = None
                    for c in range(NCD):
                        r = e.matmul(ps[b2][:, :], buf[:, 1024 + c * 128:1024 + (c + 1) * 128],
                                     xln[:, c * T + sl.start:c * T + sl.stop],
                                     start=(c == 0), stop=(c == NCD - 1))
                    return r
                s_mmi = pb.step("tensor", mmi, deps=[ldL, rdr.get(("ps", b2))])
                rdr[("wbuf", par)] = s_mmi
                s_sil = pb.step("scalar", lambda e, oc=oc, par=par, b2=b2: e.activation(
                    sA[par][:, :], ps[b2][:, :], AF.Silu, bias=bcols[:, oc:1 + oc]),
                    deps=[s_mmi, rdr.get(("sA", par))])
                rdr[("ps", b2)] = s_sil
                s_om = pb.step("vector", lambda e, oc=oc, sl=sl: e.tensor_scalar(
                    tmpf[:, :], lam[:, oc * T + sl.start:oc * T + sl.stop], -1.0, 1.0, ALU.mult, ALU.add),
                    deps=[s_sig, rdr.get("tmpf")])
                s_hm = pb.step("vector", lambda e, oc=oc, sl=sl, par=par: e.tensor_tensor(
                    hbuf[:, oc * T + sl.start:oc * T + sl.stop], tmpf[:, :], sA[par][:, :], ALU.mult),
                    deps=[s_om, s_sil])
                rdr[("sA", par)] = s_hm
                rdr["tmpf"] = s_hm
        pb.prev = pb.step("scalar", lambda e: e.nop(), deps=[pb.prev])
        for c in range(NCD):
            for b in range(1, NB):
                s_prev = slice(c * T + (b - 1) * TB, c * T + b * TB)
                s_cur = slice(c * T + b * TB, c * T + (b + 1) * TB)
                pb.step("vector", lambda e, s_prev=s_prev, s_cur=s_cur: e.tensor_tensor(
                    tmpf[:, :TB], lam[:, s_cur], hbuf[:, s_prev], ALU.mult))
                pb.step("vector", lambda e, s_cur=s_cur: e.tensor_tensor(
                    hbuf[:, s_cur], hbuf[:, s_cur], tmpf[:, :TB], ALU.add))
        if dbg:
            pb.dma("sync", [(dbgL_d[c * 128:(c + 1) * 128, :], lam[:, c * T:(c + 1) * T])
                            for c in range(NCD)])
            pb.dma("sync", [(dbgH_d[c * 128:(c + 1) * 128, :], hbuf[:, c * T:(c + 1) * T])
                            for c in range(NCD)])
        entry_h = pb.prev
        rl = {}
        for g in range(4):
            sl = slice(g * 512, (g + 1) * 512)
            for c in range(NCD):
                cs = slice(c * T + sl.start, c * T + sl.stop)
                par = c % 2
                s_mm1 = pb.step("tensor", lambda e, c=c, cs=cs: e.matmul(
                    ps[0][:1, :], onesc[:, :], hbuf[:, cs], start=(c == 0), stop=(c == NCD - 1)),
                    deps=[entry_h, rl.get(("ps", 0))])
                s_sq = pb.step("vector", lambda e, cs=cs, par=par: e.tensor_tensor(
                    tmpq[par][:, :], hbuf[:, cs], hbuf[:, cs], ALU.mult),
                    deps=[entry_h, rl.get(("sq", par))])
                s_mm2 = pb.step("tensor", lambda e, c=c, par=par: e.matmul(
                    ps[1][:1, :], onesc[:, :], tmpq[par][:, :], start=(c == 0), stop=(c == NCD - 1)),
                    deps=[s_sq, rl.get(("ps", 1))])
                rl[("sq", par)] = s_mm2
            r1 = pb.step("vector", lambda e: e.tensor_scalar(rowa[:1, :], ps[0][:1, :], 1.0 / D, None, ALU.mult),
                         deps=[s_mm1, rl.get("rows")])
            rl[("ps", 0)] = r1
            r2 = pb.step("vector", lambda e: e.tensor_scalar(rowb[:1, :], ps[1][:1, :], 1.0 / D, None, ALU.mult),
                         deps=[s_mm2, r1, rl.get("bA")])
            rl[("ps", 1)] = r2
            r3 = pb.step("vector", lambda e: e.tensor_tensor(rowc[:1, :], rowa[:1, :], rowa[:1, :], ALU.mult),
                         deps=[r2, rl.get("bB")])
            r4 = pb.step("vector", lambda e: e.tensor_tensor(rowb[:1, :], rowb[:1, :], rowc[:1, :], ALU.subtract), deps=[r3])
            r5 = pb.step("scalar", lambda e: e.activation(rowb[:1, :], rowb[:1, :], AF.Ln, bias=cns[:1, :1]), deps=[r4])
            r6 = pb.step("scalar", lambda e: e.activation(rowb[:1, :], rowb[:1, :], AF.Exp, scale=-0.5), deps=[r5])
            r7 = pb.step("vector", lambda e: e.scalar_tensor_tensor(rowc[:1, :], rowa[:1, :], -1.0, rowb[:1, :], ALU.mult, ALU.mult), deps=[r6])
            rl["rows"] = r7
            last_bA = None
            for c in range(NCD):
                cs = slice(c * T + sl.start, c * T + sl.stop)
                par = c % 2
                buf = wa if par == 0 else wb
                ldg = pb.dma("gpsimd", [(buf[:, 0:1024].rearrange("p (cc o) -> p cc o", o=128),
                                         colslice_ap(wgate_d, c * 128, 128, 8))],
                             deps=[rdr.get(("wbuf", par))])
                bA = pb.step("tensor", lambda e, c=c: e.matmul(
                    ps[2][:, :], grows[:1, c * 128:(c + 1) * 128], rowb[:1, :], start=True, stop=True),
                    deps=[r7, rl.get(("ps", 2))])
                bB = pb.step("tensor", lambda e, c=c: e.matmul(
                    ps[3][:, :], grows[:1, c * 128:(c + 1) * 128], rowc[:1, :], start=True, stop=True),
                    deps=[r7, rl.get(("ps", 3))])
                rl["bA"] = bA
                rl["bB"] = bB
                last_bA = bA

                def mmg(e, sl=sl, buf=buf, par=par):
                    r = None
                    for cc in range(NCD):
                        r = e.matmul(ps[4 + par][:, :], buf[:, cc * 128:(cc + 1) * 128],
                                     xln[:, cc * T + sl.start:cc * T + sl.stop],
                                     start=(cc == 0), stop=(cc == NCD - 1))
                    return r
                s_g = pb.step("tensor", mmg, deps=[ldg, rl.get(("psg", par))])
                rdr[("wbuf", par)] = s_g
                sg_buf = scr2 if par == 0 else sqln
                s_sil = pb.step("scalar", lambda e, c=c, par=par, sg_buf=sg_buf: e.activation(
                    sg_buf[:, :], ps[4 + par][:, :], AF.Silu, bias=bcols[:, 16 + c:17 + c]),
                    deps=[s_g, rl.get(("sg", par))])
                rl[("psg", par)] = s_sil
                a1 = pb.step("vector", lambda e, cs=cs, par=par: e.tensor_tensor(
                    tmpa[par][:, :], hbuf[:, cs], ps[2][:, :], ALU.mult),
                    deps=[bA, rl.get(("ta", par))])
                rl[("ps", 2)] = a1
                a2 = pb.step("vector", lambda e, c=c, par=par: e.scalar_tensor_tensor(
                    tmpa[par][:, :], tmpa[par][:, :], bcols[:, 40 + c:41 + c], ps[3][:, :], ALU.add, ALU.add),
                    deps=[bB, a1])
                rl[("ps", 3)] = a2
                s_z = pb.step("vector", lambda e, cs=cs, par=par, sg_buf=sg_buf: e.tensor_tensor(
                    hbuf[:, cs], tmpa[par][:, :], sg_buf[:, :], ALU.mult),
                    deps=[a2, s_sil])
                rl[("ta", par)] = s_z
                rl[("sg", par)] = s_z
        pb.prev = pb.step("scalar", lambda e: e.nop(), deps=[s_z])

        zdone = pb.prev
        if dbg:
            pb.dma("sync", [(dbgZ_d[c * 128:(c + 1) * 128, :], hbuf[:, c * T:(c + 1) * T])
                            for c in range(NCD)])
            pb.dma("sync", [(dbgX_d[c * 128:(c + 1) * 128, :], x[:, c * T:(c + 1) * T])
                            for c in range(NCD)])
            zdone = pb.prev
        for dc in range(NCD):
            par = dc % 2
            buf = wa if par == 0 else wb
            ldo = pb.dma("gpsimd", [(buf[:, 0:1024].rearrange("p (c o) -> p c o", o=128),
                                     colslice_ap(wout_d, dc * 128, 128, 8))],
                         deps=[rdr.get(("wbuf", par)), zdone])
            for g in range(4):
                sl = slice(g * 512, (g + 1) * 512)
                bank = 2 * par + (g % 2)

                def mmo(e, sl=sl, buf=buf, bank=bank):
                    r = None
                    for c in range(NCD):
                        r = e.matmul(ps[bank][:, :], buf[:, c * 128:(c + 1) * 128],
                                     hbuf[:, c * T + sl.start:c * T + sl.stop],
                                     start=(c == 0), stop=(c == NCD - 1))
                    return r
                s_mm = pb.step("tensor", mmo, deps=[ldo, rdr.get(("ps", bank)), zdone])
                rdr[("wbuf", par)] = s_mm
                cs = slice(dc * T + sl.start, dc * T + sl.stop)
                s_acc = pb.step("vector", lambda e, cs=cs, dc=dc, bank=bank: e.scalar_tensor_tensor(
                    x[:, cs], ps[bank][:, :], bcols[:, 24 + dc:25 + dc], x[:, cs], ALU.add, ALU.add),
                    deps=[s_mm])
                rdr[("ps", bank)] = s_acc
        pb.prev = pb.step("scalar", lambda e: e.nop(), deps=[pb.prev])

        if dbg:
            _d = pb.dma("sync", [(dbgA_d[c * 128:(c + 1) * 128, :], x[:, c * T:(c + 1) * T])
                                 for c in range(NCD)])
            pb.prev = _d
        emit_ln(x)
        for t in range(16):
            tsl = slice(t * 128, (t + 1) * 128)

            def mmlg(e, tsl=tsl):
                r = None
                for c in range(NCD):
                    r = e.matmul(ps[0][:, :8], xln[:, c * T + tsl.start:c * T + tsl.stop],
                                 wgw[:, c * 8:(c + 1) * 8], start=(c == 0), stop=(c == NCD - 1))
                return r
            pb.step("tensor", mmlg)
            s_lgev = pb.step("vector", lambda e: e.tensor_copy(tmpf[:, 0:8], ps[0][:, :8]))
            rdr[("ps", 0)] = s_lgev
            pb.step("vector", lambda e: e.reduce_max(tmpf[:, 8:9], tmpf[:, 0:8], axis=AX.X))
            pb.step("vector", lambda e: e.tensor_scalar(tmpf[:, 16:24], tmpf[:, 0:8], tmpf[:, 8:9], None, ALU.is_ge))
            pb.step("vector", lambda e: e.scalar_tensor_tensor(tmpf[:, 24:32], tmpf[:, 16:24], -1e30, tmpf[:, 0:8], ALU.mult, ALU.add))
            pb.step("vector", lambda e: e.reduce_max(tmpf[:, 10:11], tmpf[:, 24:32], axis=AX.X))
            pb.step("vector", lambda e: e.tensor_scalar(tmpf[:, 9:10], tmpf[:, 8:9], -1.0, None, ALU.mult))
            pb.step("scalar", lambda e: e.activation(tmpf[:, 11:12], tmpf[:, 10:11], AF.Exp, bias=tmpf[:, 9:10]))
            pb.step("vector", lambda e: e.tensor_scalar(tmpf[:, 11:12], tmpf[:, 11:12], 1.0, None, ALU.add))
            pb.step("vector", lambda e: e.reciprocal(tmpf[:, 12:13], tmpf[:, 11:12]))
            pb.step("scalar", lambda e: e.activation(tmpf[:, 24:32], tmpf[:, 0:8], AF.Exp, bias=tmpf[:, 9:10]))
            pb.step("vector", lambda e: e.tensor_scalar(tmpf[:, 16:24], tmpf[:, 0:8], tmpf[:, 10:11], None, ALU.is_ge))
            pb.step("vector", lambda e: e.tensor_tensor(tmpf[:, 24:32], tmpf[:, 16:24], tmpf[:, 24:32], ALU.mult))
            pb.step("vector", lambda e, t=t: e.tensor_scalar(cmbt[:, t * 8:(t + 1) * 8], tmpf[:, 24:32], tmpf[:, 12:13], None, ALU.mult))
        pb.step("tensor", lambda e: e.transpose(psT[:, :128], cmbt[:, :], idf[:, :]))
        pb.step("vector", lambda e: e.tensor_copy(cmbu[:, :], psT[:, :128]))
        pb.dma("sync", [(cmb_dram.ap(), cmbu[:, :])])
        pb.dma("gpsimd", [(cfm8[:, :].rearrange("e (t j) -> e t j", j=128),
                          cmb_dram.ap().rearrange("(t e) j -> e t j", e=8))])
        if dbg:
            pb.dma("sync", [(cmb_d[:, :], cfm8[:, :])])
            pb.dma("sync", [(cmbt_d[:, :], cmbt[:, :])])
        cmb_ready = pb.prev
        sAm = [scr2, scr3]
        sBm = [sqln, scr4]
        prev_dn = cmb_ready
        prev_ue_last = None
        for ex in range(NE):
            cb_evs = []
            for g in range(4):
                sl = slice(g * 512, (g + 1) * 512)
                s_cb = pb.step("tensor", lambda e, ex=ex, sl=sl: e.matmul(
                    ps[4][:, :], sel[:, ex * 128:(ex + 1) * 128], cfm8[:, sl], start=True, stop=True),
                    deps=[cmb_ready, rdr.get(("ps", 4)), prev_ue_last])
                s_ev = pb.step("vector", lambda e, sl=sl: e.tensor_copy(cb[:, sl], ps[4][:, :]),
                               deps=[s_cb, prev_ue_last])
                rdr[("ps", 4)] = s_ev
                cb_evs.append(s_ev)
            cb_done = cb_evs[-1]
            for t2 in range(NT2):
                base = t2 * TT2
                hms = []
                for oc in range(NH):
                    par = oc % 2
                    buf = wa if par == 0 else wb
                    ldgu = pb.dma("gpsimd", [
                        (buf[:, 0:1024].rearrange("p (c o) -> p c o", o=128),
                         colslice_ap(wgT_d[ex], oc * 128, 128, 8)),
                        (buf[:, 1024:2048].rearrange("p (c o) -> p c o", o=128),
                         colslice_ap(wuT_d[ex], oc * 128, 128, 8))],
                        deps=[rdr.get(("wbuf", par))])
                    for hf in range(2):
                        sl = slice(base + hf * 512, base + hf * 512 + 512)
                        b1, b2 = 2 * par, 2 * par + 1

                        def mmg2(e, sl=sl, buf=buf, b1=b1):
                            r = None
                            for c in range(NCD):
                                r = e.matmul(ps[b1][:, :], buf[:, c * 128:(c + 1) * 128],
                                             xln[:, c * T + sl.start:c * T + sl.stop],
                                             start=(c == 0), stop=(c == NCD - 1))
                            return r
                        s_g = pb.step("tensor", mmg2, deps=[ldgu, rdr.get(("ps", b1)), cmb_ready])
                        rdr[("wbuf", par)] = s_g

                        def mmu2(e, sl=sl, buf=buf, b2=b2):
                            r = None
                            for c in range(NCD):
                                r = e.matmul(ps[b2][:, :], buf[:, 1024 + c * 128:1024 + (c + 1) * 128],
                                             xln[:, c * T + sl.start:c * T + sl.stop],
                                             start=(c == 0), stop=(c == NCD - 1))
                            return r
                        s_u = pb.step("tensor", mmu2, deps=[ldgu, rdr.get(("ps", b2)), cmb_ready])
                        rdr[("wbuf", par)] = s_u
                        s_si = pb.step("scalar", lambda e, par=par, b1=b1: e.activation(
                            sAm[par][:, :], ps[b1][:, :], AF.Silu),
                            deps=[s_g, rdr.get(("sA", par))])
                        rdr[("ps", b1)] = s_si
                        s_ue = pb.step("vector", lambda e, sl=sl, par=par, b2=b2: e.tensor_tensor(
                            sBm[par][:, :], ps[b2][:, :], cb[:, sl], ALU.mult),
                            deps=[s_u, cb_done, rdr.get(("sB", par))])
                        rdr[("ps", b2)] = s_ue
                        prev_ue_last = s_ue
                        s_hm = pb.step("vector", lambda e, oc=oc, hf=hf, par=par: e.tensor_tensor(
                            hmoe[:, oc * TT2 + hf * 512:oc * TT2 + hf * 512 + 512],
                            sAm[par][:, :], sBm[par][:, :], ALU.mult),
                            deps=[s_si, s_ue, prev_dn])
                        rdr[("sA", par)] = s_hm
                        rdr[("sB", par)] = s_hm
                        hms.append(s_hm)
                hm_all = hms[-1]
                for dc in range(NCD):
                    par = dc % 2
                    buf = wa if par == 0 else wb
                    ldd = pb.dma("gpsimd", [(buf[:, :].rearrange("p (c o) -> p c o", o=128),
                                             colslice_ap(wdT_d[ex], dc * 128, 128, 32))],
                                 deps=[rdr.get(("wbuf", par))])
                    for hf in range(2):
                        bank = 5 if hf == 0 else 2 * par

                        def mmd(e, hf=hf, buf=buf, bank=bank):
                            r = None
                            for ic in range(NH):
                                r = e.matmul(ps[bank][:, :], buf[:, ic * 128:(ic + 1) * 128],
                                             hmoe[:, ic * TT2 + hf * 512:ic * TT2 + hf * 512 + 512],
                                             start=(ic == 0), stop=(ic == NH - 1))
                            return r
                        s_d = pb.step("tensor", mmd, deps=[ldd, rdr.get(("ps", bank)), hm_all])
                        rdr[("wbuf", par)] = s_d
                        cs = slice(dc * T + base + hf * 512, dc * T + base + hf * 512 + 512)
                        s_a = pb.step("vector", lambda e, cs=cs, bank=bank: e.tensor_tensor(
                            x[:, cs], x[:, cs], ps[bank][:, :], ALU.add), deps=[s_d])
                        rdr[("ps", bank)] = s_a
                        prev_dn = s_a
        pb.prev = pb.step("scalar", lambda e: e.nop(), deps=[pb.prev, prev_dn])

        stv = pb.counts["sync"] + 16 * NCD
        pb.dma("sync", [(outT_d[c * 128:(c + 1) * 128, :], x[:, c * T:(c + 1) * T]) for c in range(NCD)])

        def fin(e):
            e.wait_ge(sems["sync"], stv)
            return e.nop()
        pb.step("sync", fin)

        pb.emit(block, sems)
    return nc


def prep_inputs_B(inputs, yT_list, core):
    asc = np.ascontiguousarray

    def colpack(vecs):
        out = np.zeros((128, 48), np.float32)
        for j, v in enumerate(vecs):
            out[:, j * 8:(j + 1) * 8] = v.reshape(8, 128).T
        return out
    xT = np.concatenate([yT_list[b][:, core * TB:(core + 1) * TB] for b in range(NB)], axis=1)
    gwc = np.zeros((128, 64), np.float32)
    gw = inputs["L1_ffn_gate_w"]
    for c in range(8):
        gwc[:, c * 8:(c + 1) * 8] = gw[:, c * 128:(c + 1) * 128].T
    return {
        "xT": asc(xT),
        "WinT": asc(inputs["L1_attn_W_in"].T), "WlamT": asc(inputs["L1_attn_W_lam"].T),
        "WgateT": asc(inputs["L1_attn_W_gate"].T), "WoutT": asc(inputs["L1_attn_W_out"].T),
        "bcols": colpack([inputs["L1_attn_b_in"], inputs["L1_attn_b_lam"],
                          inputs["L1_attn_b_gate"], inputs["L1_attn_b_out"],
                          inputs["L1_attn_gamma"], inputs["L1_attn_beta"]]),
        "gwc": asc(gwc),
        "grows": inputs["L1_attn_gamma"].reshape(1, D).astype(np.float32),
        "wgT": asc(np.transpose(inputs["L1_ffn_wg"], (0, 2, 1))),
        "wuT": asc(np.transpose(inputs["L1_ffn_wu"], (0, 2, 1))),
        "wdT": asc(np.transpose(inputs["L1_ffn_wd"], (0, 2, 1))),
    }


from concourse.bass_utils import run_bass_kernel_spmd

_CACHE = {}


def kernel(**inputs):
    if "A" not in _CACHE:
        _CACHE["A"] = build_launchA()
        _CACHE["B"] = build_launchB()
    ncA, ncB = _CACHE["A"], _CACHE["B"]

    in_maps_A = [prep_inputs_A(inputs, b) for b in range(8)]
    resA = run_bass_kernel_spmd(ncA, in_maps_A, core_ids=list(range(8)))
    yT_list = [resA.results[b]["yT"] for b in range(8)]

    in_maps_B = [prep_inputs_B(inputs, yT_list, c) for c in range(8)]
    resB = run_bass_kernel_spmd(ncB, in_maps_B, core_ids=list(range(8)))

    out = np.empty((8, 2048, 1024), np.float32)
    for c in range(8):
        out[:, c * TB:(c + 1) * TB, :] = resB.results[c]["outT"].T.reshape(NB, TB, D)
    return out


# version 07
# speedup vs baseline: 1.2233x; 1.1282x over previous
import sys
sys.path.insert(0, "/opt/trn_rl_repo")
import numpy as np
import ml_dtypes
import concourse.bass as bass
import concourse.mybir as mybir

F32 = mybir.dt.float32
BF16 = mybir.dt.bfloat16
AF = mybir.ActivationFunctionType
ALU = mybir.AluOpType
AX = mybir.AxisListType

ENGS = ("tensor", "vector", "scalar", "gpsimd", "sync")


class Step:
    __slots__ = ("eng", "fn", "deps", "idx", "val", "ndma", "_sem_holder")

    def __init__(self, eng, fn, deps, idx, ndma):
        self.eng, self.fn, self.deps, self.idx, self.ndma = eng, fn, deps, idx, ndma
        self.val = None


class PB:
    def __init__(self, nc):
        self.nc = nc
        self.steps = []
        self.counts = {e: 0 for e in ENGS}
        self.prev = None

    def step(self, eng, fn, deps=None, ndma=0):
        if deps is None:
            deps = [self.prev] if self.prev is not None else []
        deps = [d for d in deps if d is not None]
        s = Step(eng, fn, deps, len(self.steps), ndma)
        self.counts[eng] += (16 * ndma) if ndma else 1
        s.val = self.counts[eng]
        self.steps.append(s)
        self.prev = s
        return s

    def dma(self, queue, pairs, deps=None):
        holder = {}

        def fn(e):
            sem = holder["sem"]
            for o, i in pairs:
                e.dma_start(out=o, in_=i).then_inc(sem, 16)
            return None
        s = self.step(queue, fn, deps=deps, ndma=len(pairs))
        s._sem_holder = holder
        return s

    def emit(self, block, sems):
        per_eng = {e: [s for s in self.steps if s.eng == e] for e in ENGS}

        def make_body(eng):
            def body(e):
                watermark = {}
                for s in per_eng[eng]:
                    best = {}
                    for d in s.deps:
                        if d.eng not in best or d.val > best[d.eng].val:
                            best[d.eng] = d
                    for d in best.values():
                        if d.eng == eng == "tensor":
                            continue
                        if d.val <= watermark.get(d.eng, -1):
                            continue
                        e.wait_ge(sems[d.eng], d.val)
                        watermark[d.eng] = d.val
                    if s.ndma:
                        s._sem_holder["sem"] = sems[eng]
                        s.fn(e)
                    else:
                        last = s.fn(e)
                        assert last is not None, f"step {s.idx} on {eng} returned None"
                        last.then_inc(sems[eng], 1)
            return body

        for eng in ENGS:
            if per_eng[eng]:
                getattr(block, eng)(make_body(eng))


D, T = 1024, 2048
NCD = 8
H, DK = 8, 128
CH = 64
NCHUNK = T // CH
TT = 512
NTT = T // TT
DH = 4096
NH = DH // 128
LN_EPS = 1e-5


def colslice_ap(w_d, o0, width, nchunks):
    return w_d.rearrange("(c p) o -> p c o", p=128)[:, :nchunks, o0:o0 + width]


class SMap:

    def __init__(self, nc):
        self.nc = nc
        self.off = (nc.sbuf_base + 31) & ~31
        self.top = nc.sbuf_top
        self.offsets = {}

    def at(self, name, shape, dtype, offset):
        return self.nc.alloc_sbuf_tensor_at(name, shape, dtype, offset=offset)

    def seq(self, name, shape, dtype):
        per = int(np.prod(shape[1:])) * mybir.dt.size(dtype)
        h = self.at(name, shape, dtype, self.off)
        self.offsets[name] = self.off
        self.off = (self.off + per + 31) & ~31
        assert self.off <= self.top, f"SBUF overflow at {name}: {self.off} > {self.top}"
        return h


def build_launchA():
    nc = bass.Bass("TRN2", target_bir_lowering=False)

    xT_d = nc.dram_tensor("xT", [D, T], F32, kind="ExternalInput").ap()
    wq_d = nc.dram_tensor("Wq", [D, D], F32, kind="ExternalInput").ap()
    wk_d = nc.dram_tensor("Wk", [D, D], F32, kind="ExternalInput").ap()
    wv_d = nc.dram_tensor("Wv", [D, D], F32, kind="ExternalInput").ap()
    wg_d = nc.dram_tensor("Wg", [D, D], F32, kind="ExternalInput").ap()
    wo_d = nc.dram_tensor("Wo", [D, D], F32, kind="ExternalInput").ap()
    w1_d = nc.dram_tensor("w1T", [D, DH], F32, kind="ExternalInput").ap()
    w2_d = nc.dram_tensor("w2T", [D, DH], F32, kind="ExternalInput").ap()
    h1_d = nc.dram_tensor("h1T", [DH, DH], F32, kind="ExternalInput").ap()
    h2_d = nc.dram_tensor("h2T", [DH, DH], F32, kind="ExternalInput").ap()
    w3_d = nc.dram_tensor("w3T", [DH, D], F32, kind="ExternalInput").ap()
    s1_d = nc.dram_tensor("s1", [128, NH], F32, kind="ExternalInput").ap()
    s2_d = nc.dram_tensor("s2", [128, NH], F32, kind="ExternalInput").ap()
    sh1_d = nc.dram_tensor("sh1", [128, NH], F32, kind="ExternalInput").ap()
    sh2_d = nc.dram_tensor("sh2", [128, NH], F32, kind="ExternalInput").ap()
    s3_d = nc.dram_tensor("s3", [128, NCD], F32, kind="ExternalInput").ap()
    yT_d = nc.dram_tensor("yT", [D, T], F32, kind="ExternalOutput").ap()

    w1b_d = nc.dram_tensor("w1b", [NH, 128, 1024], BF16)
    w2b_d = nc.dram_tensor("w2b", [NH, 128, 1024], BF16)
    h1b_d = nc.dram_tensor("h1b", [NH, 128, 4096], BF16)
    h2b_d = nc.dram_tensor("h2b", [NH, 128, 4096], BF16)
    w3b_d = nc.dram_tensor("w3b", [NCD, 128, 4096], BF16)
    tri_d = nc.inline_tensor(np.triu(np.ones((CH, CH), dtype=np.float32)), name="tric")
    id_d = nc.inline_tensor(np.eye(128, dtype=ml_dtypes.bfloat16), name="idc")

    sm = SMap(nc)
    x = sm.seq("x", [128, NCD * T], F32)
    xln = sm.seq("xln", [128, NCD * T], BF16)
    ofm = sm.seq("ofm", [128, H * T], BF16)
    wa = sm.seq("wa", [128, 4096], BF16)
    rowa = sm.seq("rowa", [1, 512], F32)
    rowb = sm.seq("rowb", [1, 512], F32)
    rowc = sm.seq("rowc", [1, 512], F32)
    tmpf = sm.seq("tmpf", [128, 512], F32)
    tmpf2 = sm.seq("tmpf2", [128, 512], F32)
    scr2 = sm.seq("scr2", [128, 512], BF16)
    sqln = sm.seq("sqln", [128, 512], BF16)
    sq2 = sm.seq("sq2", [128, 512], BF16)
    sq3 = sm.seq("sq3", [128, 512], BF16)
    ones1 = sm.seq("ones1", [1, 128], F32)
    onesc = sm.seq("onesc", [128, 1], BF16)
    tri = sm.seq("tri", [CH, CH], F32)
    idm = sm.seq("idm", [128, 128], BF16)
    scol = sm.seq("scol", [128, 2 * NH + NCD], F32)
    scol2 = sm.seq("scol2", [128, 2 * NH], F32)
    cns = sm.seq("cns", [1, 8], F32)
    U = sm.off
    vtm = sm.at("vtm", [64, NCHUNK * 128], BF16, U + 0)
    qh = sm.at("qh", [128, T], BF16, U + 8192)
    kh = sm.at("kh", [128, T], BF16, U + 12288)
    gh = sm.at("gh", [128, T], F32, U + 16384)
    ke = sm.at("ke", [128, T], BF16, U + 16384)
    ah = sm.at("ah", [128, T], F32, U + 24576)
    qs = sm.at("qs", [128, T], BF16, U + 32768)
    ks = sm.at("ks", [128, T], BF16, U + 36864)
    sst = sm.at("sst", [128, 128], BF16, U + 40960)
    pm = sm.at("pm", [64, 2 * CH], BF16, U + 41472)
    rahg = sm.at("rahg", [128, 512], F32, U + 41984)
    whead = sm.at("whead", [128, 4096], BF16, U + 48032)
    ketm = sm.at("ketm", [64, NCHUNK * 128], BF16, U + 48032)
    wb = sm.at("wb", [128, 4096], BF16, U + 48032)
    GLA_END = U + 48032 + 8192
    hb = sm.at("hb", [128, NH * TT], BF16, U)
    g1 = sm.at("g1", [128, NH * TT], BF16, sm.offsets["ofm"])
    assert GLA_END <= sm.top and U + NH * TT * 2 <= sm.top, (U, GLA_END, sm.top)

    from contextlib import ExitStack
    with ExitStack() as es:
        E = es.enter_context
        ps = [E(nc.psum_tensor(f"ps{i}", [128, 512], F32)) for i in range(6)]
        psT = E(nc.psum_tensor("psT", [128, 1024], BF16))
        psP = E(nc.psum_tensor("psP", [64, 512], F32))
        sems = {e: E(nc.semaphore("sem_" + e)) for e in ENGS}
        block = E(nc.Block())

        pb = PB(nc)

        ld0 = pb.dma("gpsimd", [(x[:, c * T:(c + 1) * T], xT_d[c * 128:(c + 1) * 128, :])
                                for c in range(NCD)], deps=[])
        ldc = pb.dma("sync", [(tri[:, :], tri_d.ap()), (idm[:, :], id_d.ap()),
                              (scol[:, 0:NH], s1_d[:, :]), (scol[:, NH:2 * NH], s2_d[:, :]),
                              (scol[:, 2 * NH:2 * NH + NCD], s3_d[:, :]),
                              (scol2[:, 0:NH], sh1_d[:, :]), (scol2[:, NH:2 * NH], sh2_d[:, :])],
                     deps=[])

        def ones_fn(e):
            e.memset(ones1[:, :], 1.0)
            e.memset(cns[:1, :], LN_EPS)
            return e.memset(onesc[:, :], 1.0)
        pb.step("vector", ones_fn, deps=[])
        pb.step("scalar", lambda e: e.nop(), deps=[ld0, ldc, pb.prev])

        tmpq = [sq2, sq3]
        tmpa = [tmpf, tmpf2]
        def emit_ln(src_t):
            entry = pb.prev
            rl = {}
            bcast = {}
            for g in range(4):
                sl = slice(g * 512, (g + 1) * 512)
                for c in range(NCD):
                    cs = slice(c * T + sl.start, c * T + sl.stop)
                    par = c % 2
                    cbuf = scr2 if par == 0 else sqln
                    s_cast = pb.step("vector", lambda e, cs=cs, cbuf=cbuf: e.tensor_copy(cbuf[:, :512], src_t[:, cs]),
                                     deps=[rl.get(("cb", par)), entry])
                    s_mm1 = pb.step("tensor", lambda e, c=c, cbuf=cbuf: e.matmul(
                        ps[0][:1, :], onesc[:, :], cbuf[:, :512], start=(c == 0), stop=(c == NCD - 1)),
                        deps=[s_cast, rl.get(("ps", 0))])
                    s_sq = pb.step("vector", lambda e, cbuf=cbuf, par=par: e.tensor_tensor(
                        tmpq[par][:, :], cbuf[:, :512], cbuf[:, :512], ALU.mult),
                        deps=[s_cast, rl.get(("sq", par))])
                    s_mm2 = pb.step("tensor", lambda e, c=c, par=par: e.matmul(
                        ps[1][:1, :], onesc[:, :], tmpq[par][:, :], start=(c == 0), stop=(c == NCD - 1)),
                        deps=[s_sq, rl.get(("ps", 1))])
                    rl[("cb", par)] = s_mm1
                    rl[("sq", par)] = s_mm2
                r1 = pb.step("vector", lambda e: e.tensor_scalar(rowa[:1, :], ps[0][:1, :], 1.0 / D, None, ALU.mult),
                             deps=[s_mm1, rl.get("rows")])
                rl[("ps", 0)] = r1
                r2 = pb.step("vector", lambda e: e.tensor_scalar(rowb[:1, :], ps[1][:1, :], 1.0 / D, None, ALU.mult),
                             deps=[s_mm2, r1, rl.get("bA")])
                rl[("ps", 1)] = r2
                r3 = pb.step("vector", lambda e: e.tensor_tensor(rowc[:1, :], rowa[:1, :], rowa[:1, :], ALU.mult),
                             deps=[r2, rl.get("bB")])
                r4 = pb.step("vector", lambda e: e.tensor_tensor(rowb[:1, :], rowb[:1, :], rowc[:1, :], ALU.subtract), deps=[r3])
                r5 = pb.step("scalar", lambda e: e.activation(rowb[:1, :], rowb[:1, :], AF.Ln, bias=cns[:1, :1]), deps=[r4])
                r6 = pb.step("scalar", lambda e: e.activation(rowb[:1, :], rowb[:1, :], AF.Exp, scale=-0.5), deps=[r5])
                r7 = pb.step("vector", lambda e: e.scalar_tensor_tensor(rowc[:1, :], rowa[:1, :], -1.0, rowb[:1, :], ALU.mult, ALU.mult), deps=[r6])
                rl["rows"] = r7
                bA = pb.step("tensor", lambda e: e.matmul(ps[2][:, :], ones1[:, :], rowb[:1, :], start=True, stop=True),
                             deps=[r6, r7, rl.get(("ps", 2))])
                bB = pb.step("tensor", lambda e: e.matmul(ps[3][:, :], ones1[:, :], rowc[:1, :], start=True, stop=True),
                             deps=[r7, rl.get(("ps", 3))])
                rl["bA"] = bA
                rl["bB"] = bB
                ap_last = None
                for c in range(NCD):
                    cs = slice(c * T + sl.start, c * T + sl.stop)
                    par = c % 2
                    a1 = pb.step("vector", lambda e, cs=cs, par=par: e.tensor_tensor(
                        tmpa[par][:, :], src_t[:, cs], ps[2][:, :], ALU.mult),
                        deps=[bA, rl.get(("ta", par))])
                    a2 = pb.step("vector", lambda e, cs=cs, par=par: e.tensor_tensor(
                        xln[:, cs], tmpa[par][:, :], ps[3][:, :], ALU.add),
                        deps=[bB, a1])
                    rl[("ta", par)] = a2
                    ap_last = a2
                rl[("ps", 2)] = ap_last
                rl[("ps", 3)] = ap_last
            pb.prev = pb.step("scalar", lambda e: e.nop(), deps=[ap_last])

        emit_ln(x)

        for h in range(H):
            pb.dma("gpsimd", [
                (whead[:, 0:1024].rearrange("p (c o) -> p c o", o=128), colslice_ap(wq_d, h * 128, 128, 8)),
                (whead[:, 1024:2048].rearrange("p (c o) -> p c o", o=128), colslice_ap(wk_d, h * 128, 128, 8)),
                (whead[:, 2048:3072].rearrange("p (c o) -> p c o", o=128), colslice_ap(wv_d, h * 128, 128, 8)),
                (whead[:, 3072:4096].rearrange("p (c o) -> p c o", o=128), colslice_ap(wg_d, h * 128, 128, 8))])
            lw = pb.prev
            grdr = {}
            for off, dst, mode in ((0, qh, "c"), (1024, kh, "c"), (3072, gh, "s")):
                for g in range(4):
                    sl = slice(g * 512, (g + 1) * 512)
                    bank = g % 2

                    def mmfn(e, off=off, sl=sl, bank=bank):
                        r = None
                        for c in range(NCD):
                            r = e.matmul(ps[bank][:, :], whead[:, off + c * 128:off + (c + 1) * 128],
                                         xln[:, c * T + sl.start:c * T + sl.stop],
                                         start=(c == 0), stop=(c == NCD - 1))
                        return r
                    s_mm = pb.step("tensor", mmfn, deps=[lw, grdr.get(("ps", bank))])
                    if mode == "c":
                        s_ev = pb.step("vector", lambda e, dst=dst, sl=sl, bank=bank: e.tensor_copy(dst[:, sl], ps[bank][:, :]),
                                       deps=[s_mm])
                    else:
                        s_ev = pb.step("scalar", lambda e, dst=dst, sl=sl, bank=bank: e.activation(dst[:, sl], ps[bank][:, :], AF.Sigmoid),
                                       deps=[s_mm])
                    grdr[("ps", bank)] = s_ev
            for cc in range(NCHUNK):
                tsl = slice(cc * CH, (cc + 1) * CH)
                bank = 2 + cc % 2

                def vfn(e, tsl=tsl, bank=bank):
                    r = None
                    for c in range(NCD):
                        r = e.matmul(ps[bank][:64, :128], xln[:, c * T + tsl.start:c * T + tsl.stop],
                                     whead[:, 2048 + c * 128:2048 + (c + 1) * 128],
                                     start=(c == 0), stop=(c == NCD - 1))
                    return r
                s_mm = pb.step("tensor", vfn, deps=[lw, grdr.get(("ps", bank))])
                s_ev = pb.step("vector", lambda e, cc=cc, bank=bank: e.tensor_copy(vtm[:, cc * 128:(cc + 1) * 128], ps[bank][:64, :128]),
                               deps=[s_mm])
                grdr[("ps", bank)] = s_ev
            pb.prev = pb.step("scalar", lambda e: e.nop(), deps=[pb.prev, s_ev])
            pb.step("scalar", lambda e: e.activation(gh[:, :], gh[:, :], AF.Ln))
            for cc in range(NCHUNK):
                sl = slice(cc * CH, (cc + 1) * CH)
                pb.step("vector", lambda e, sl=sl: e.tensor_tensor_scan(gh[:, sl], gh[:, sl], gh[:, sl], 0.0, ALU.add, ALU.bypass))
            pb.step("scalar", lambda e: e.activation(ah[:, :], gh[:, :], AF.Exp))
            eA = pb.prev
            for g in range(4):
                sl = slice(g * 512, (g + 1) * 512)
                pb.step("vector", lambda e, sl=sl: e.tensor_tensor(sq2[:, :], qh[:, sl], qh[:, sl], ALU.mult))
                pb.step("tensor", lambda e: e.matmul(ps[0][:1, :], onesc[:, :], sq2[:, :], start=True, stop=True))
                pb.step("vector", lambda e: e.tensor_copy(rowa[:1, :], ps[0][:1, :]))
                pb.step("scalar", lambda e: e.activation(rowa[:1, :], rowa[:1, :], AF.Ln))
                pb.step("scalar", lambda e: e.activation(rowa[:1, :], rowa[:1, :], AF.Exp, scale=-0.5))
                pb.step("tensor", lambda e: e.matmul(ps[2][:, :], ones1[:, :], rowa[:1, :], start=True, stop=True))
                pb.step("vector", lambda e, sl=sl: e.tensor_tensor(tmpf[:, :], ah[:, sl], ps[2][:, :], ALU.mult),
                        deps=[eA, pb.prev])
                pb.step("vector", lambda e, sl=sl: e.tensor_tensor(qs[:, sl], tmpf[:, :], qh[:, sl], ALU.mult))
            for g in range(4):
                sl = slice(g * 512, (g + 1) * 512)
                pb.step("vector", lambda e, sl=sl: e.tensor_tensor(sq2[:, :], kh[:, sl], kh[:, sl], ALU.mult))
                pb.step("tensor", lambda e: e.matmul(ps[0][:1, :], onesc[:, :], sq2[:, :], start=True, stop=True))
                pb.step("vector", lambda e: e.tensor_copy(rowa[:1, :], ps[0][:1, :]))
                pb.step("scalar", lambda e: e.activation(rowa[:1, :], rowa[:1, :], AF.Ln))
                pb.step("scalar", lambda e: e.activation(rowa[:1, :], rowa[:1, :], AF.Exp, scale=-0.5))
                pb.step("tensor", lambda e: e.matmul(ps[2][:, :], ones1[:, :], rowa[:1, :], start=True, stop=True))
                pb.step("scalar", lambda e, sl=sl: e.activation(rahg[:, :], gh[:, sl], AF.Exp, scale=-1.0))
                pb.step("vector", lambda e: e.tensor_tensor(rahg[:, :], rahg[:, :], ps[2][:, :], ALU.mult))
                pb.step("vector", lambda e, sl=sl: e.tensor_tensor(ks[:, sl], rahg[:, :], kh[:, sl], ALU.mult))
            for cc in range(NCHUNK):
                sl = slice(cc * CH, (cc + 1) * CH)
                col = slice(cc * CH + CH - 1, cc * CH + CH)
                pb.step("vector", lambda e, sl=sl, col=col: e.tensor_scalar(ke[:, sl], ks[:, sl], ah[:, col], None, ALU.mult))
            for cc in range(NCHUNK):
                sl = slice(cc * CH, (cc + 1) * CH)
                pb.step("tensor", lambda e, sl=sl: e.transpose(psT[:64, :128], ke[:, sl], idm[:, :]))
                pb.step("vector", lambda e, cc=cc: e.tensor_copy(ketm[:, cc * 128:(cc + 1) * 128], psT[:64, :128]))
            pb.step("vector", lambda e: e.memset(sst[:, :], 0.0))
            crdr = {}
            elem_done = pb.prev
            s_supd = {}
            s_ofn = {}
            for cc in range(NCHUNK):
                sl = slice(cc * CH, (cc + 1) * CH)
                col = slice(cc * CH + CH - 1, cc * CH + CH)
                pbank = 2 + cc % 2
                tbank = 5 if cc % 2 == 0 else 0
                slot = (cc % 2) * 64
                s_pmm = pb.step("tensor", lambda e, sl=sl, pbank=pbank: e.matmul(
                    ps[pbank][:64, :64], ks[:, sl], qs[:, sl], start=True, stop=True),
                    deps=[elem_done, crdr.get(("ps", pbank))])
                s_msk = pb.step("vector", lambda e, pbank=pbank, slot=slot: e.tensor_tensor(
                    pm[:, slot:slot + 64], ps[pbank][:64, :64], tri[:, :], ALU.mult),
                    deps=[s_pmm, crdr.get(("pm", slot))])
                crdr[("ps", pbank)] = s_msk

                def ofn(e, sl=sl, cc=cc, slot=slot):
                    e.matmul(ps[4][:, :64], vtm[:, cc * 128:(cc + 1) * 128], pm[:, slot:slot + 64], start=True, stop=False)
                    return e.matmul(ps[4][:, :64], sst[:, :], qs[:, sl], start=False, stop=True)
                s_ofn[cc] = pb.step("tensor", ofn, deps=[s_msk, s_supd.get(cc - 1), crdr.get(("ps", 4))])
                crdr[("pm", slot)] = s_ofn[cc]
                s_oev = pb.step("vector", lambda e, sl=sl, h=h: e.tensor_copy(
                    ofm[:, h * T + sl.start:h * T + sl.stop], ps[4][:, :64]), deps=[s_ofn[cc]])
                crdr[("ps", 4)] = s_oev

                def tfn(e, cc=cc, tbank=tbank):
                    return e.matmul(ps[tbank][:, :128], ketm[:, cc * 128:(cc + 1) * 128],
                                    vtm[:, cc * 128:(cc + 1) * 128], start=True, stop=True)
                s_tmm = pb.step("tensor", tfn, deps=[elem_done, crdr.get(("ps", tbank))])
                s_supd[cc] = pb.step("vector", lambda e, col=col, tbank=tbank: e.scalar_tensor_tensor(
                    sst[:, :], sst[:, :], ah[:, col], ps[tbank][:, :128], ALU.mult, ALU.add),
                    deps=[s_tmm, s_ofn[cc]])
                crdr[("ps", tbank)] = s_supd[cc]
            pb.prev = pb.step("scalar", lambda e: e.nop(), deps=[s_supd[NCHUNK - 1], pb.prev])

        rdr = {}
        gla_done = pb.prev
        ldW = {}
        mmW = {}
        accW = {}
        for dc in range(NCD):
            par = dc % 2
            buf = wa if par == 0 else wb
            ldW[dc] = pb.dma("gpsimd", [(buf[:, 0:1024].rearrange("p (c o) -> p c o", o=128),
                                         colslice_ap(wo_d, dc * 128, 128, 8))],
                             deps=[rdr.get(("wbuf", par)), gla_done])
            for g in range(4):
                sl = slice(g * 512, (g + 1) * 512)
                bank = 2 * par + (g % 2)

                def mmfn(e, sl=sl, buf=buf, bank=bank):
                    r = None
                    for c in range(NCD):
                        r = e.matmul(ps[bank][:, :], buf[:, c * 128:(c + 1) * 128],
                                     ofm[:, c * T + sl.start:c * T + sl.stop],
                                     start=(c == 0), stop=(c == NCD - 1))
                    return r
                mmW[(dc, g)] = pb.step("tensor", mmfn, deps=[ldW[dc], rdr.get(("ps", bank)), gla_done])
                rdr[("wbuf", par)] = mmW[(dc, g)]
                cs = slice(dc * T + sl.start, dc * T + sl.stop)
                accW[(dc, g)] = pb.step("vector", lambda e, cs=cs, bank=bank: e.tensor_tensor(
                    x[:, cs], x[:, cs], ps[bank][:, :], ALU.add), deps=[mmW[(dc, g)]])
                rdr[("ps", bank)] = accW[(dc, g)]
        wo_done = pb.step("scalar", lambda e: e.nop(),
                          deps=[accW[(NCD - 1, 3)], accW[(NCD - 2, 3)], accW[(NCD - 1, 2)]])

        pb.prev = wo_done
        emit_ln(x)
        ln2_done = pb.prev

        stores = {}
        last_store = {}
        ofm_free = wo_done
        hb_free = gla_done
        prev_tt_w3 = ln2_done
        prev_tt_h2 = None
        sA = [scr2, sqln]
        for tt in range(NTT):
            tsl = slice(tt * TT, (tt + 1) * TT)
            mt1, mt2, sil, hm, ld12 = {}, {}, {}, {}, {}
            for oc in range(NH):
                par = oc % 2
                buf = wa if par == 0 else wb
                if tt == 0:
                    ld12[oc] = pb.dma("gpsimd", [
                        (buf[:, 0:1024].rearrange("p (c o) -> p c o", o=128),
                         colslice_ap(w1_d, oc * 128, 128, 8)),
                        (buf[:, 1024:2048].rearrange("p (c o) -> p c o", o=128),
                         colslice_ap(w2_d, oc * 128, 128, 8))],
                        deps=[rdr.get(("wbuf", par)), last_store.get(par), ln2_done])
                    stores[("w12", oc)] = pb.dma("sync", [
                        (w1b_d[oc], buf[:, 0:1024]), (w2b_d[oc], buf[:, 1024:2048])],
                        deps=[ld12[oc]])
                    last_store[par] = stores[("w12", oc)]
                else:
                    ld12[oc] = pb.dma("sync", [
                        (buf[:, 0:1024], w1b_d[oc]), (buf[:, 1024:2048], w2b_d[oc])],
                        deps=[rdr.get(("wbuf", par)), last_store.get(par), stores[("w12", oc)], ln2_done])
                b1, b2 = 2 * par, 2 * par + 1

                def m1f(e, tsl=tsl, buf=buf, b1=b1):
                    r = None
                    for c in range(NCD):
                        r = e.matmul(ps[b1][:, :], buf[:, c * 128:(c + 1) * 128],
                                     xln[:, c * T + tsl.start:c * T + tsl.stop],
                                     start=(c == 0), stop=(c == NCD - 1))
                    return r
                mt1[oc] = pb.step("tensor", m1f, deps=[ld12[oc], rdr.get(("ps", b1)), ln2_done])
                rdr[("wbuf", par)] = mt1[oc]

                def m2f(e, tsl=tsl, buf=buf, b2=b2):
                    r = None
                    for c in range(NCD):
                        r = e.matmul(ps[b2][:, :], buf[:, 1024 + c * 128:1024 + (c + 1) * 128],
                                     xln[:, c * T + tsl.start:c * T + tsl.stop],
                                     start=(c == 0), stop=(c == NCD - 1))
                    return r
                mt2[oc] = pb.step("tensor", m2f, deps=[ld12[oc], rdr.get(("ps", b2))])
                rdr[("wbuf", par)] = mt2[oc]
                sil[oc] = pb.step("scalar", lambda e, oc=oc, par=par, b1=b1: e.activation(
                    sA[par][:, :], ps[b1][:, :], AF.Silu, scale=scol[:, oc:oc + 1]),
                    deps=[mt1[oc], rdr.get(("sA", par))])
                rdr[("ps", b1)] = sil[oc]
                hm[oc] = pb.step("vector", lambda e, oc=oc, par=par, b2=b2: e.scalar_tensor_tensor(
                    hb[:, oc * TT:(oc + 1) * TT], ps[b2][:, :], scol[:, NH + oc:NH + oc + 1],
                    sA[par][:, :], ALU.mult, ALU.mult),
                    deps=[mt2[oc], sil[oc], prev_tt_w3, hb_free])
                rdr[("ps", b2)] = hm[oc]
                rdr[("sA", par)] = hm[oc]
            hm_last = hm[NH - 1]
            mmh, gel, ldh = {}, {}, {}
            for oc in range(NH):
                par = oc % 2
                buf = wa if par == 0 else wb
                if tt == 0:
                    ldh[oc] = pb.dma("gpsimd", [(buf[:, :].rearrange("p (c o) -> p c o", o=128),
                                                 colslice_ap(h1_d, oc * 128, 128, 32))],
                                     deps=[rdr.get(("wbuf", par)), last_store.get(par)])
                    stores[("h1", oc)] = pb.dma("sync", [(h1b_d[oc], buf[:, :])], deps=[ldh[oc]])
                    last_store[par] = stores[("h1", oc)]
                else:
                    ldh[oc] = pb.dma("sync", [(buf[:, :], h1b_d[oc])],
                                     deps=[rdr.get(("wbuf", par)), last_store.get(par), stores[("h1", oc)]])
                bank = par

                def mhf(e, buf=buf, bank=bank):
                    r = None
                    for ic in range(NH):
                        r = e.matmul(ps[bank][:, :], buf[:, ic * 128:(ic + 1) * 128],
                                     hb[:, ic * TT:(ic + 1) * TT],
                                     start=(ic == 0), stop=(ic == NH - 1))
                    return r
                mmh[oc] = pb.step("tensor", mhf, deps=[ldh[oc], rdr.get(("ps", bank)), hm_last])
                rdr[("wbuf", par)] = mmh[oc]
                gel[oc] = pb.step("scalar", lambda e, oc=oc, bank=bank: e.activation(
                    g1[:, oc * TT:(oc + 1) * TT], ps[bank][:, :], AF.Gelu, scale=scol2[:, oc:oc + 1]),
                    deps=[mmh[oc], ofm_free, prev_tt_h2])
                rdr[("ps", bank)] = gel[oc]
            gel_last = gel[NH - 1]
            mmh2, gel2, ldh2 = {}, {}, {}
            for oc in range(NH):
                par = oc % 2
                buf = wa if par == 0 else wb
                if tt == 0:
                    ldh2[oc] = pb.dma("gpsimd", [(buf[:, :].rearrange("p (c o) -> p c o", o=128),
                                                  colslice_ap(h2_d, oc * 128, 128, 32))],
                                      deps=[rdr.get(("wbuf", par)), last_store.get(par)])
                    stores[("h2", oc)] = pb.dma("sync", [(h2b_d[oc], buf[:, :])], deps=[ldh2[oc]])
                    last_store[par] = stores[("h2", oc)]
                else:
                    ldh2[oc] = pb.dma("sync", [(buf[:, :], h2b_d[oc])],
                                      deps=[rdr.get(("wbuf", par)), last_store.get(par), stores[("h2", oc)]])
                bank = 2 + par

                def mh2f(e, buf=buf, bank=bank):
                    r = None
                    for ic in range(NH):
                        r = e.matmul(ps[bank][:, :], buf[:, ic * 128:(ic + 1) * 128],
                                     g1[:, ic * TT:(ic + 1) * TT],
                                     start=(ic == 0), stop=(ic == NH - 1))
                    return r
                mmh2[oc] = pb.step("tensor", mh2f, deps=[ldh2[oc], rdr.get(("ps", bank)), gel_last])
                rdr[("wbuf", par)] = mmh2[oc]
                gel2[oc] = pb.step("scalar", lambda e, oc=oc, bank=bank: e.activation(
                    hb[:, oc * TT:(oc + 1) * TT], ps[bank][:, :], AF.Gelu, scale=scol2[:, NH + oc:NH + oc + 1]),
                    deps=[mmh2[oc]])
                rdr[("ps", bank)] = gel2[oc]
            gel2_last = gel2[NH - 1]
            prev_tt_h2 = gel2_last
            mmy, accy, ldy = {}, {}, {}
            for dc in range(NCD):
                par = dc % 2
                buf = wa if par == 0 else wb
                if tt == 0:
                    ldy[dc] = pb.dma("gpsimd", [(buf[:, :].rearrange("p (c o) -> p c o", o=128),
                                                 colslice_ap(w3_d, dc * 128, 128, 32))],
                                     deps=[rdr.get(("wbuf", par)), last_store.get(par)])
                    stores[("w3", dc)] = pb.dma("sync", [(w3b_d[dc], buf[:, :])], deps=[ldy[dc]])
                    last_store[par] = stores[("w3", dc)]
                else:
                    ldy[dc] = pb.dma("sync", [(buf[:, :], w3b_d[dc])],
                                     deps=[rdr.get(("wbuf", par)), last_store.get(par), stores[("w3", dc)]])
                bank = 4 + par

                def myf(e, buf=buf, bank=bank):
                    r = None
                    for ic in range(NH):
                        r = e.matmul(ps[bank][:, :], buf[:, ic * 128:(ic + 1) * 128],
                                     hb[:, ic * TT:(ic + 1) * TT],
                                     start=(ic == 0), stop=(ic == NH - 1))
                    return r
                mmy[dc] = pb.step("tensor", myf, deps=[ldy[dc], rdr.get(("ps", bank)), gel2_last])
                rdr[("wbuf", par)] = mmy[dc]
                cs = slice(dc * T + tsl.start, dc * T + tsl.stop)
                accy[dc] = pb.step("vector", lambda e, cs=cs, dc=dc, bank=bank: e.scalar_tensor_tensor(
                    x[:, cs], ps[bank][:, :], scol[:, 2 * NH + dc:2 * NH + dc + 1], x[:, cs], ALU.mult, ALU.add),
                    deps=[mmy[dc]])
                rdr[("ps", bank)] = accy[dc]
            prev_tt_w3 = pb.step("scalar", lambda e: e.nop(),
                                 deps=[accy[NCD - 1], accy[NCD - 2]])
            pb.prev = prev_tt_w3

        stv = pb.counts["sync"] + 16 * NCD
        pb.dma("sync", [(yT_d[c * 128:(c + 1) * 128, :], x[:, c * T:(c + 1) * T]) for c in range(NCD)])

        def fin(e):
            e.wait_ge(sems["sync"], stv)
            return e.nop()
        pb.step("sync", fin)

        pb.emit(block, sems)
    return nc


def prep_inputs_A(inputs, b):
    asc = np.ascontiguousarray
    return {
        "xT": asc(inputs["L0_attn_x"][b].T),
        "Wq": asc(inputs["L0_attn_Wq"]), "Wk": asc(inputs["L0_attn_Wk"]),
        "Wv": asc(inputs["L0_attn_Wv"]), "Wg": asc(inputs["L0_attn_Wg"]),
        "Wo": asc(inputs["L0_attn_Wo"]),
        "w1T": asc(inputs["L0_ffn_w1_w"].T), "w2T": asc(inputs["L0_ffn_w2_w"].T),
        "h1T": asc(inputs["L0_ffn_h1_w"].T), "h2T": asc(inputs["L0_ffn_h2_w"].T),
        "w3T": asc(inputs["L0_ffn_w3_w"].T),
        "s1": asc(inputs["L0_ffn_w1_s"].reshape(NH, 128).T),
        "s2": asc(inputs["L0_ffn_w2_s"].reshape(NH, 128).T),
        "sh1": asc(inputs["L0_ffn_h1_s"].reshape(NH, 128).T),
        "sh2": asc(inputs["L0_ffn_h2_s"].reshape(NH, 128).T),
        "s3": asc(inputs["L0_ffn_w3_s"].reshape(NCD, 128).T),
    }


TT2 = 1024
NT2 = T // TT2
NE = 8
NB = 8
TB = 256


def build_launchB(dbg=False):
    nc = bass.Bass("TRN2", target_bir_lowering=False)

    xT_d = nc.dram_tensor("xT", [D, T], F32, kind="ExternalInput").ap()
    win_d = nc.dram_tensor("WinT", [D, D], F32, kind="ExternalInput").ap()
    wlam_d = nc.dram_tensor("WlamT", [D, D], F32, kind="ExternalInput").ap()
    wgate_d = nc.dram_tensor("WgateT", [D, D], F32, kind="ExternalInput").ap()
    wout_d = nc.dram_tensor("WoutT", [D, D], F32, kind="ExternalInput").ap()
    bcols_d = nc.dram_tensor("bcols", [128, 48], F32, kind="ExternalInput").ap()
    gwc_d = nc.dram_tensor("gwc", [128, 64], F32, kind="ExternalInput").ap()
    grows_d = nc.dram_tensor("grows", [1, D], F32, kind="ExternalInput").ap()
    wgT_d = nc.dram_tensor("wgT", [NE, D, DH], F32, kind="ExternalInput").ap()
    wuT_d = nc.dram_tensor("wuT", [NE, D, DH], F32, kind="ExternalInput").ap()
    wdT_d = nc.dram_tensor("wdT", [NE, DH, D], F32, kind="ExternalInput").ap()
    outT_d = nc.dram_tensor("outT", [D, T], F32, kind="ExternalOutput").ap()
    if dbg:
        dbgA_d = nc.dram_tensor("dbgA", [D, T], F32, kind="ExternalOutput").ap()
        dbgL_d = nc.dram_tensor("dbgL", [D, T], BF16, kind="ExternalOutput").ap()
        dbgH_d = nc.dram_tensor("dbgH", [D, T], BF16, kind="ExternalOutput").ap()
        cmb_d = nc.dram_tensor("cmbo", [8, T], BF16, kind="ExternalOutput").ap()
        cmbt_d = nc.dram_tensor("cmbt_o", [128, 128], F32, kind="ExternalOutput").ap()
        dbgZ_d = nc.dram_tensor("dbgZ", [D, T], BF16, kind="ExternalOutput").ap()
        dbgX_d = nc.dram_tensor("dbgX", [D, T], F32, kind="ExternalOutput").ap()

    idf_d = nc.inline_tensor(np.eye(128, dtype=np.float32), name="idf")
    sel_np = np.zeros((8, 1024), dtype=ml_dtypes.bfloat16)
    for e in range(8):
        sel_np[e, e * 128:(e + 1) * 128] = 1.0
    sel_d = nc.inline_tensor(sel_np, name="selc")
    cmb_dram = nc.dram_tensor("cmbscr", [128, 128], F32)

    sm = SMap(nc)
    x = sm.seq("x", [128, NCD * T], F32)
    xln = sm.seq("xln", [128, NCD * T], BF16)
    wa = sm.seq("wa", [128, 4096], BF16)
    wb = sm.seq("wb", [128, 4096], BF16)
    scr3 = sm.seq("scr3", [128, 512], BF16)
    scr4 = sm.seq("scr4", [128, 512], BF16)
    tmpf2 = sm.seq("tmpf2", [128, 512], F32)
    grows = sm.seq("grows", [1, D], F32)
    rowa = sm.seq("rowa", [1, 512], F32)
    rowb = sm.seq("rowb", [1, 512], F32)
    rowc = sm.seq("rowc", [1, 512], F32)
    tmpf = sm.seq("tmpf", [128, 512], F32)
    scr2 = sm.seq("scr2", [128, 512], BF16)
    sqln = sm.seq("sqln", [128, 512], BF16)
    ones1 = sm.seq("ones1", [1, 128], F32)
    onesc = sm.seq("onesc", [128, 1], BF16)
    idf = sm.seq("idf", [128, 128], F32)
    sel = sm.seq("sel", [8, 1024], BF16)
    bcols = sm.seq("bcols", [128, 48], F32)
    wgw = sm.seq("wgw", [128, 64], BF16)
    cmbt = sm.seq("cmbt", [128, 128], F32)
    cmbu = sm.seq("cmbu", [128, 128], F32)
    cfm8 = sm.seq("cfm8", [8, T], BF16)
    cns = sm.seq("cns", [1, 8], F32)
    UB = sm.off
    lam = sm.at("lam", [128, NCD * T], BF16, UB)
    hbuf = sm.at("hbuf", [128, NCD * T], BF16, UB + 32768)
    hmoe = sm.at("hmoe", [128, NH * TT2], BF16, UB)
    cb = sm.at("cb", [128, T], BF16, sm.offsets["rowa"])
    assert UB + 65536 <= sm.top, (UB, sm.top)

    from contextlib import ExitStack
    with ExitStack() as es:
        E = es.enter_context
        ps = [E(nc.psum_tensor(f"ps{i}", [128, 512], F32)) for i in range(6)]
        psT = E(nc.psum_tensor("psT", [128, 512], F32))
        psX = E(nc.psum_tensor("psX", [128, 512], F32))
        sems = {e: E(nc.semaphore("sem_" + e)) for e in ENGS}
        block = E(nc.Block())

        pb = PB(nc)

        pb.dma("gpsimd", [(x[:, c * T:(c + 1) * T], xT_d[c * 128:(c + 1) * 128, :])
                          for c in range(NCD)], deps=[])
        ldc = pb.dma("sync", [(idf[:, :], idf_d.ap()), (sel[:, :], sel_d.ap()),
                              (bcols[:, :], bcols_d[:, :]), (grows[:1, :], grows_d[:, :])], deps=[])
        ldw = pb.dma("gpsimd", [(wgw[:, :], gwc_d[:, :])], deps=[])

        def ones_fn(e):
            e.memset(ones1[:, :], 1.0)
            e.memset(cns[:1, :], LN_EPS)
            return e.memset(onesc[:, :], 1.0)
        pb.step("vector", ones_fn, deps=[])
        pb.step("scalar", lambda e: e.nop(), deps=[pb.steps[0], ldc, ldw, pb.prev])

        tmpq = [scr3, scr4]
        tmpa = [tmpf, tmpf2]

        def emit_ln(src_t):
            entry = pb.prev
            rl = {}
            ap_last = None
            for g in range(4):
                sl = slice(g * 512, (g + 1) * 512)
                for c in range(NCD):
                    cs = slice(c * T + sl.start, c * T + sl.stop)
                    par = c % 2
                    cbuf = scr2 if par == 0 else sqln
                    s_cast = pb.step("vector", lambda e, cs=cs, cbuf=cbuf: e.tensor_copy(cbuf[:, :512], src_t[:, cs]),
                                     deps=[rl.get(("cb", par)), entry])
                    s_mm1 = pb.step("tensor", lambda e, c=c, cbuf=cbuf: e.matmul(
                        ps[0][:1, :], onesc[:, :], cbuf[:, :512], start=(c == 0), stop=(c == NCD - 1)),
                        deps=[s_cast, rl.get(("ps", 0))])
                    s_sq = pb.step("vector", lambda e, cbuf=cbuf, par=par: e.tensor_tensor(
                        tmpq[par][:, :], cbuf[:, :512], cbuf[:, :512], ALU.mult),
                        deps=[s_cast, rl.get(("sq", par))])
                    s_mm2 = pb.step("tensor", lambda e, c=c, par=par: e.matmul(
                        ps[1][:1, :], onesc[:, :], tmpq[par][:, :], start=(c == 0), stop=(c == NCD - 1)),
                        deps=[s_sq, rl.get(("ps", 1))])
                    rl[("cb", par)] = s_mm1
                    rl[("sq", par)] = s_mm2
                r1 = pb.step("vector", lambda e: e.tensor_scalar(rowa[:1, :], ps[0][:1, :], 1.0 / D, None, ALU.mult),
                             deps=[s_mm1, rl.get("rows")])
                rl[("ps", 0)] = r1
                r2 = pb.step("vector", lambda e: e.tensor_scalar(rowb[:1, :], ps[1][:1, :], 1.0 / D, None, ALU.mult),
                             deps=[s_mm2, r1, rl.get("bA")])
                rl[("ps", 1)] = r2
                r3 = pb.step("vector", lambda e: e.tensor_tensor(rowc[:1, :], rowa[:1, :], rowa[:1, :], ALU.mult),
                             deps=[r2, rl.get("bB")])
                r4 = pb.step("vector", lambda e: e.tensor_tensor(rowb[:1, :], rowb[:1, :], rowc[:1, :], ALU.subtract), deps=[r3])
                r5 = pb.step("scalar", lambda e: e.activation(rowb[:1, :], rowb[:1, :], AF.Ln, bias=cns[:1, :1]), deps=[r4])
                r6 = pb.step("scalar", lambda e: e.activation(rowb[:1, :], rowb[:1, :], AF.Exp, scale=-0.5), deps=[r5])
                r7 = pb.step("vector", lambda e: e.scalar_tensor_tensor(rowc[:1, :], rowa[:1, :], -1.0, rowb[:1, :], ALU.mult, ALU.mult), deps=[r6])
                rl["rows"] = r7
                bA = pb.step("tensor", lambda e: e.matmul(ps[2][:, :], ones1[:, :], rowb[:1, :], start=True, stop=True),
                             deps=[r6, r7, rl.get(("ps", 2))])
                bB = pb.step("tensor", lambda e: e.matmul(ps[3][:, :], ones1[:, :], rowc[:1, :], start=True, stop=True),
                             deps=[r7, rl.get(("ps", 3))])
                rl["bA"] = bA
                rl["bB"] = bB
                for c in range(NCD):
                    cs = slice(c * T + sl.start, c * T + sl.stop)
                    par = c % 2
                    a1 = pb.step("vector", lambda e, cs=cs, par=par: e.tensor_tensor(
                        tmpa[par][:, :], src_t[:, cs], ps[2][:, :], ALU.mult),
                        deps=[bA, rl.get(("ta", par))])
                    a2 = pb.step("vector", lambda e, cs=cs, par=par: e.tensor_tensor(
                        xln[:, cs], tmpa[par][:, :], ps[3][:, :], ALU.add),
                        deps=[bB, a1])
                    rl[("ta", par)] = a2
                    ap_last = a2
                rl[("ps", 2)] = ap_last
                rl[("ps", 3)] = ap_last
            pb.prev = pb.step("scalar", lambda e: e.nop(), deps=[ap_last])

        emit_ln(x)
        rdr = {}
        ln3_done = pb.prev
        sA = [scr2, scr3]
        sB = [tmpf]
        for oc in range(NCD):
            par = oc % 2
            buf = wa if par == 0 else wb
            ldL = pb.dma("gpsimd", [(buf[:, 0:1024].rearrange("p (c o) -> p c o", o=128),
                                     colslice_ap(wlam_d, oc * 128, 128, 8)),
                                    (buf[:, 1024:2048].rearrange("p (c o) -> p c o", o=128),
                                     colslice_ap(win_d, oc * 128, 128, 8))],
                         deps=[rdr.get(("wbuf", par)), ln3_done])
            for g in range(4):
                sl = slice(g * 512, (g + 1) * 512)
                b1, b2 = 2 * par, 2 * par + 1

                def mml(e, sl=sl, buf=buf, b1=b1):
                    r = None
                    for c in range(NCD):
                        r = e.matmul(ps[b1][:, :], buf[:, c * 128:(c + 1) * 128],
                                     xln[:, c * T + sl.start:c * T + sl.stop],
                                     start=(c == 0), stop=(c == NCD - 1))
                    return r
                s_mml = pb.step("tensor", mml, deps=[ldL, rdr.get(("ps", b1)), ln3_done])
                rdr[("wbuf", par)] = s_mml
                s_sig = pb.step("scalar", lambda e, oc=oc, sl=sl, b1=b1: e.activation(
                    lam[:, oc * T + sl.start:oc * T + sl.stop], ps[b1][:, :], AF.Sigmoid,
                    bias=bcols[:, 8 + oc:9 + oc]), deps=[s_mml])
                rdr[("ps", b1)] = s_sig

                def mmi(e, sl=sl, buf=buf, b2=b2):
                    r = None
                    for c in range(NCD):
                        r = e.matmul(ps[b2][:, :], buf[:, 1024 + c * 128:1024 + (c + 1) * 128],
                                     xln[:, c * T + sl.start:c * T + sl.stop],
                                     start=(c == 0), stop=(c == NCD - 1))
                    return r
                s_mmi = pb.step("tensor", mmi, deps=[ldL, rdr.get(("ps", b2))])
                rdr[("wbuf", par)] = s_mmi
                s_sil = pb.step("scalar", lambda e, oc=oc, par=par, b2=b2: e.activation(
                    sA[par][:, :], ps[b2][:, :], AF.Silu, bias=bcols[:, oc:1 + oc]),
                    deps=[s_mmi, rdr.get(("sA", par))])
                rdr[("ps", b2)] = s_sil
                s_om = pb.step("vector", lambda e, oc=oc, sl=sl: e.tensor_scalar(
                    tmpf[:, :], lam[:, oc * T + sl.start:oc * T + sl.stop], -1.0, 1.0, ALU.mult, ALU.add),
                    deps=[s_sig, rdr.get("tmpf")])
                s_hm = pb.step("vector", lambda e, oc=oc, sl=sl, par=par: e.tensor_tensor(
                    hbuf[:, oc * T + sl.start:oc * T + sl.stop], tmpf[:, :], sA[par][:, :], ALU.mult),
                    deps=[s_om, s_sil])
                rdr[("sA", par)] = s_hm
                rdr["tmpf"] = s_hm
        pb.prev = pb.step("scalar", lambda e: e.nop(), deps=[pb.prev])
        for c in range(NCD):
            for b in range(1, NB):
                s_prev = slice(c * T + (b - 1) * TB, c * T + b * TB)
                s_cur = slice(c * T + b * TB, c * T + (b + 1) * TB)
                pb.step("vector", lambda e, s_prev=s_prev, s_cur=s_cur: e.tensor_tensor(
                    tmpf[:, :TB], lam[:, s_cur], hbuf[:, s_prev], ALU.mult))
                pb.step("vector", lambda e, s_cur=s_cur: e.tensor_tensor(
                    hbuf[:, s_cur], hbuf[:, s_cur], tmpf[:, :TB], ALU.add))
        if dbg:
            pb.dma("sync", [(dbgL_d[c * 128:(c + 1) * 128, :], lam[:, c * T:(c + 1) * T])
                            for c in range(NCD)])
            pb.dma("sync", [(dbgH_d[c * 128:(c + 1) * 128, :], hbuf[:, c * T:(c + 1) * T])
                            for c in range(NCD)])
        entry_h = pb.prev
        rl = {}
        for g in range(4):
            sl = slice(g * 512, (g + 1) * 512)
            for c in range(NCD):
                cs = slice(c * T + sl.start, c * T + sl.stop)
                par = c % 2
                s_mm1 = pb.step("tensor", lambda e, c=c, cs=cs: e.matmul(
                    ps[0][:1, :], onesc[:, :], hbuf[:, cs], start=(c == 0), stop=(c == NCD - 1)),
                    deps=[entry_h, rl.get(("ps", 0))])
                s_sq = pb.step("vector", lambda e, cs=cs, par=par: e.tensor_tensor(
                    tmpq[par][:, :], hbuf[:, cs], hbuf[:, cs], ALU.mult),
                    deps=[entry_h, rl.get(("sq", par))])
                s_mm2 = pb.step("tensor", lambda e, c=c, par=par: e.matmul(
                    ps[1][:1, :], onesc[:, :], tmpq[par][:, :], start=(c == 0), stop=(c == NCD - 1)),
                    deps=[s_sq, rl.get(("ps", 1))])
                rl[("sq", par)] = s_mm2
            r1 = pb.step("vector", lambda e: e.tensor_scalar(rowa[:1, :], ps[0][:1, :], 1.0 / D, None, ALU.mult),
                         deps=[s_mm1, rl.get("rows")])
            rl[("ps", 0)] = r1
            r2 = pb.step("vector", lambda e: e.tensor_scalar(rowb[:1, :], ps[1][:1, :], 1.0 / D, None, ALU.mult),
                         deps=[s_mm2, r1, rl.get("bA")])
            rl[("ps", 1)] = r2
            r3 = pb.step("vector", lambda e: e.tensor_tensor(rowc[:1, :], rowa[:1, :], rowa[:1, :], ALU.mult),
                         deps=[r2, rl.get("bB")])
            r4 = pb.step("vector", lambda e: e.tensor_tensor(rowb[:1, :], rowb[:1, :], rowc[:1, :], ALU.subtract), deps=[r3])
            r5 = pb.step("scalar", lambda e: e.activation(rowb[:1, :], rowb[:1, :], AF.Ln, bias=cns[:1, :1]), deps=[r4])
            r6 = pb.step("scalar", lambda e: e.activation(rowb[:1, :], rowb[:1, :], AF.Exp, scale=-0.5), deps=[r5])
            r7 = pb.step("vector", lambda e: e.scalar_tensor_tensor(rowc[:1, :], rowa[:1, :], -1.0, rowb[:1, :], ALU.mult, ALU.mult), deps=[r6])
            rl["rows"] = r7
            last_bA = None
            for c in range(NCD):
                cs = slice(c * T + sl.start, c * T + sl.stop)
                par = c % 2
                buf = wa if par == 0 else wb
                ldg = pb.dma("gpsimd", [(buf[:, 0:1024].rearrange("p (cc o) -> p cc o", o=128),
                                         colslice_ap(wgate_d, c * 128, 128, 8))],
                             deps=[rdr.get(("wbuf", par))])
                bA = pb.step("tensor", lambda e, c=c: e.matmul(
                    ps[2][:, :], grows[:1, c * 128:(c + 1) * 128], rowb[:1, :], start=True, stop=True),
                    deps=[r7, rl.get(("ps", 2))])
                bB = pb.step("tensor", lambda e, c=c: e.matmul(
                    ps[3][:, :], grows[:1, c * 128:(c + 1) * 128], rowc[:1, :], start=True, stop=True),
                    deps=[r7, rl.get(("ps", 3))])
                rl["bA"] = bA
                rl["bB"] = bB
                last_bA = bA

                def mmg(e, sl=sl, buf=buf, par=par):
                    r = None
                    for cc in range(NCD):
                        r = e.matmul(ps[4 + par][:, :], buf[:, cc * 128:(cc + 1) * 128],
                                     xln[:, cc * T + sl.start:cc * T + sl.stop],
                                     start=(cc == 0), stop=(cc == NCD - 1))
                    return r
                s_g = pb.step("tensor", mmg, deps=[ldg, rl.get(("psg", par))])
                rdr[("wbuf", par)] = s_g
                sg_buf = scr2 if par == 0 else sqln
                s_sil = pb.step("scalar", lambda e, c=c, par=par, sg_buf=sg_buf: e.activation(
                    sg_buf[:, :], ps[4 + par][:, :], AF.Silu, bias=bcols[:, 16 + c:17 + c]),
                    deps=[s_g, rl.get(("sg", par))])
                rl[("psg", par)] = s_sil
                a1 = pb.step("vector", lambda e, cs=cs, par=par: e.tensor_tensor(
                    tmpa[par][:, :], hbuf[:, cs], ps[2][:, :], ALU.mult),
                    deps=[bA, rl.get(("ta", par))])
                rl[("ps", 2)] = a1
                a2 = pb.step("vector", lambda e, c=c, par=par: e.scalar_tensor_tensor(
                    tmpa[par][:, :], tmpa[par][:, :], bcols[:, 40 + c:41 + c], ps[3][:, :], ALU.add, ALU.add),
                    deps=[bB, a1])
                rl[("ps", 3)] = a2
                s_z = pb.step("vector", lambda e, cs=cs, par=par, sg_buf=sg_buf: e.tensor_tensor(
                    hbuf[:, cs], tmpa[par][:, :], sg_buf[:, :], ALU.mult),
                    deps=[a2, s_sil])
                rl[("ta", par)] = s_z
                rl[("sg", par)] = s_z
        pb.prev = pb.step("scalar", lambda e: e.nop(), deps=[s_z])

        zdone = pb.prev
        if dbg:
            pb.dma("sync", [(dbgZ_d[c * 128:(c + 1) * 128, :], hbuf[:, c * T:(c + 1) * T])
                            for c in range(NCD)])
            pb.dma("sync", [(dbgX_d[c * 128:(c + 1) * 128, :], x[:, c * T:(c + 1) * T])
                            for c in range(NCD)])
            zdone = pb.prev
        for dc in range(NCD):
            par = dc % 2
            buf = wa if par == 0 else wb
            ldo = pb.dma("gpsimd", [(buf[:, 0:1024].rearrange("p (c o) -> p c o", o=128),
                                     colslice_ap(wout_d, dc * 128, 128, 8))],
                         deps=[rdr.get(("wbuf", par)), zdone])
            for g in range(4):
                sl = slice(g * 512, (g + 1) * 512)
                bank = 2 * par + (g % 2)

                def mmo(e, sl=sl, buf=buf, bank=bank):
                    r = None
                    for c in range(NCD):
                        r = e.matmul(ps[bank][:, :], buf[:, c * 128:(c + 1) * 128],
                                     hbuf[:, c * T + sl.start:c * T + sl.stop],
                                     start=(c == 0), stop=(c == NCD - 1))
                    return r
                s_mm = pb.step("tensor", mmo, deps=[ldo, rdr.get(("ps", bank)), zdone])
                rdr[("wbuf", par)] = s_mm
                cs = slice(dc * T + sl.start, dc * T + sl.stop)
                s_acc = pb.step("vector", lambda e, cs=cs, dc=dc, bank=bank: e.scalar_tensor_tensor(
                    x[:, cs], ps[bank][:, :], bcols[:, 24 + dc:25 + dc], x[:, cs], ALU.add, ALU.add),
                    deps=[s_mm])
                rdr[("ps", bank)] = s_acc
        pb.prev = pb.step("scalar", lambda e: e.nop(), deps=[pb.prev])

        if dbg:
            _d = pb.dma("sync", [(dbgA_d[c * 128:(c + 1) * 128, :], x[:, c * T:(c + 1) * T])
                                 for c in range(NCD)])
            pb.prev = _d
        emit_ln(x)
        ln4_done = pb.prev
        lrl = {}
        for t in range(16):
            tsl = slice(t * 128, (t + 1) * 128)
            par = t % 2
            lbank = par
            o0 = 32 * par

            def mmlg(e, tsl=tsl, lbank=lbank):
                r = None
                for c in range(NCD):
                    r = e.matmul(ps[lbank][:, :8], xln[:, c * T + tsl.start:c * T + tsl.stop],
                                 wgw[:, c * 8:(c + 1) * 8], start=(c == 0), stop=(c == NCD - 1))
                return r
            s_mm = pb.step("tensor", mmlg, deps=[ln4_done, lrl.get(("ps", lbank))])
            F = o0
            s1 = pb.step("vector", lambda e, lbank=lbank, F=F: e.tensor_copy(tmpf[:, F:F + 8], ps[lbank][:, :8]),
                         deps=[s_mm, lrl.get(("tf", par))])
            lrl[("ps", lbank)] = s1
            s2 = pb.step("vector", lambda e, F=F: e.reduce_max(tmpf[:, F + 8:F + 9], tmpf[:, F:F + 8], axis=AX.X), deps=[s1])
            s3 = pb.step("vector", lambda e, F=F: e.tensor_scalar(tmpf[:, F + 16:F + 24], tmpf[:, F:F + 8], tmpf[:, F + 8:F + 9], None, ALU.is_ge), deps=[s2])
            s4 = pb.step("vector", lambda e, F=F: e.scalar_tensor_tensor(tmpf[:, F + 24:F + 32], tmpf[:, F + 16:F + 24], -1e30, tmpf[:, F:F + 8], ALU.mult, ALU.add), deps=[s3])
            s5 = pb.step("vector", lambda e, F=F: e.reduce_max(tmpf[:, F + 10:F + 11], tmpf[:, F + 24:F + 32], axis=AX.X), deps=[s4])
            s6 = pb.step("vector", lambda e, F=F: e.tensor_scalar(tmpf[:, F + 9:F + 10], tmpf[:, F + 8:F + 9], -1.0, None, ALU.mult), deps=[s5])
            s7 = pb.step("scalar", lambda e, F=F: e.activation(tmpf[:, F + 11:F + 12], tmpf[:, F + 10:F + 11], AF.Exp, bias=tmpf[:, F + 9:F + 10]), deps=[s6])
            s8 = pb.step("vector", lambda e, F=F: e.tensor_scalar(tmpf[:, F + 11:F + 12], tmpf[:, F + 11:F + 12], 1.0, None, ALU.add), deps=[s7])
            s9 = pb.step("vector", lambda e, F=F: e.reciprocal(tmpf[:, F + 12:F + 13], tmpf[:, F + 11:F + 12]), deps=[s8])
            s10 = pb.step("scalar", lambda e, F=F: e.activation(tmpf[:, F + 24:F + 32], tmpf[:, F:F + 8], AF.Exp, bias=tmpf[:, F + 9:F + 10]), deps=[s9])
            s11 = pb.step("vector", lambda e, F=F: e.tensor_scalar(tmpf[:, F + 16:F + 24], tmpf[:, F:F + 8], tmpf[:, F + 10:F + 11], None, ALU.is_ge), deps=[s10])
            s12 = pb.step("vector", lambda e, F=F: e.tensor_tensor(tmpf[:, F + 24:F + 32], tmpf[:, F + 16:F + 24], tmpf[:, F + 24:F + 32], ALU.mult), deps=[s11])
            s13 = pb.step("vector", lambda e, t=t, F=F: e.tensor_scalar(cmbt[:, t * 8:(t + 1) * 8], tmpf[:, F + 24:F + 32], tmpf[:, F + 12:F + 13], None, ALU.mult), deps=[s12])
            lrl[("tf", par)] = s13
        pb.prev = pb.step("scalar", lambda e: e.nop(), deps=[s13])
        pb.step("tensor", lambda e: e.transpose(psT[:, :128], cmbt[:, :], idf[:, :]))
        pb.step("vector", lambda e: e.tensor_copy(cmbu[:, :], psT[:, :128]))
        pb.dma("sync", [(cmb_dram.ap(), cmbu[:, :])])
        pb.dma("gpsimd", [(cfm8[:, :].rearrange("e (t j) -> e t j", j=128),
                          cmb_dram.ap().rearrange("(t e) j -> e t j", e=8))])
        if dbg:
            pb.dma("sync", [(cmb_d[:, :], cfm8[:, :])])
            pb.dma("sync", [(cmbt_d[:, :], cmbt[:, :])])
        cmb_ready = pb.prev
        sAm = [scr2, scr3]
        sBm = [sqln, scr4]
        prev_dn = cmb_ready
        prev_ue_last = None
        for ex in range(NE):
            cb_evs = []
            for g in range(4):
                sl = slice(g * 512, (g + 1) * 512)
                s_cb = pb.step("tensor", lambda e, ex=ex, sl=sl: e.matmul(
                    ps[4][:, :], sel[:, ex * 128:(ex + 1) * 128], cfm8[:, sl], start=True, stop=True),
                    deps=[cmb_ready, rdr.get(("ps", 4)), prev_ue_last])
                s_ev = pb.step("vector", lambda e, sl=sl: e.tensor_copy(cb[:, sl], ps[4][:, :]),
                               deps=[s_cb, prev_ue_last])
                rdr[("ps", 4)] = s_ev
                cb_evs.append(s_ev)
            cb_done = cb_evs[-1]
            for t2 in range(NT2):
                base = t2 * TT2
                hms = []
                for oc in range(NH):
                    par = oc % 2
                    buf = wa if par == 0 else wb
                    ldgu = pb.dma("gpsimd", [
                        (buf[:, 0:1024].rearrange("p (c o) -> p c o", o=128),
                         colslice_ap(wgT_d[ex], oc * 128, 128, 8)),
                        (buf[:, 1024:2048].rearrange("p (c o) -> p c o", o=128),
                         colslice_ap(wuT_d[ex], oc * 128, 128, 8))],
                        deps=[rdr.get(("wbuf", par))])
                    for hf in range(2):
                        sl = slice(base + hf * 512, base + hf * 512 + 512)
                        b1, b2 = 2 * par, 2 * par + 1

                        def mmg2(e, sl=sl, buf=buf, b1=b1):
                            r = None
                            for c in range(NCD):
                                r = e.matmul(ps[b1][:, :], buf[:, c * 128:(c + 1) * 128],
                                             xln[:, c * T + sl.start:c * T + sl.stop],
                                             start=(c == 0), stop=(c == NCD - 1))
                            return r
                        s_g = pb.step("tensor", mmg2, deps=[ldgu, rdr.get(("ps", b1)), cmb_ready])
                        rdr[("wbuf", par)] = s_g

                        def mmu2(e, sl=sl, buf=buf, b2=b2):
                            r = None
                            for c in range(NCD):
                                r = e.matmul(ps[b2][:, :], buf[:, 1024 + c * 128:1024 + (c + 1) * 128],
                                             xln[:, c * T + sl.start:c * T + sl.stop],
                                             start=(c == 0), stop=(c == NCD - 1))
                            return r
                        s_u = pb.step("tensor", mmu2, deps=[ldgu, rdr.get(("ps", b2)), cmb_ready])
                        rdr[("wbuf", par)] = s_u
                        s_si = pb.step("scalar", lambda e, par=par, b1=b1: e.activation(
                            sAm[par][:, :], ps[b1][:, :], AF.Silu),
                            deps=[s_g, rdr.get(("sA", par))])
                        rdr[("ps", b1)] = s_si
                        s_ue = pb.step("vector", lambda e, sl=sl, par=par, b2=b2: e.tensor_tensor(
                            sBm[par][:, :], ps[b2][:, :], cb[:, sl], ALU.mult),
                            deps=[s_u, cb_done, rdr.get(("sB", par))])
                        rdr[("ps", b2)] = s_ue
                        prev_ue_last = s_ue
                        s_hm = pb.step("vector", lambda e, oc=oc, hf=hf, par=par: e.tensor_tensor(
                            hmoe[:, oc * TT2 + hf * 512:oc * TT2 + hf * 512 + 512],
                            sAm[par][:, :], sBm[par][:, :], ALU.mult),
                            deps=[s_si, s_ue, prev_dn])
                        rdr[("sA", par)] = s_hm
                        rdr[("sB", par)] = s_hm
                        hms.append(s_hm)
                hm_all = hms[-1]
                for dc in range(NCD):
                    par = dc % 2
                    buf = wa if par == 0 else wb
                    ldd = pb.dma("gpsimd", [(buf[:, :].rearrange("p (c o) -> p c o", o=128),
                                             colslice_ap(wdT_d[ex], dc * 128, 128, 32))],
                                 deps=[rdr.get(("wbuf", par))])
                    for hf in range(2):
                        bank = 5 if hf == 0 else 2 * par

                        def mmd(e, hf=hf, buf=buf, bank=bank):
                            r = None
                            for ic in range(NH):
                                r = e.matmul(ps[bank][:, :], buf[:, ic * 128:(ic + 1) * 128],
                                             hmoe[:, ic * TT2 + hf * 512:ic * TT2 + hf * 512 + 512],
                                             start=(ic == 0), stop=(ic == NH - 1))
                            return r
                        s_d = pb.step("tensor", mmd, deps=[ldd, rdr.get(("ps", bank)), hm_all])
                        rdr[("wbuf", par)] = s_d
                        cs = slice(dc * T + base + hf * 512, dc * T + base + hf * 512 + 512)
                        s_a = pb.step("vector", lambda e, cs=cs, bank=bank: e.tensor_tensor(
                            x[:, cs], x[:, cs], ps[bank][:, :], ALU.add), deps=[s_d])
                        rdr[("ps", bank)] = s_a
                        prev_dn = s_a
        pb.prev = pb.step("scalar", lambda e: e.nop(), deps=[pb.prev, prev_dn])

        stv = pb.counts["sync"] + 16 * NCD
        pb.dma("sync", [(outT_d[c * 128:(c + 1) * 128, :], x[:, c * T:(c + 1) * T]) for c in range(NCD)])

        def fin(e):
            e.wait_ge(sems["sync"], stv)
            return e.nop()
        pb.step("sync", fin)

        pb.emit(block, sems)
    return nc


def prep_inputs_B(inputs, yT_list, core):
    asc = np.ascontiguousarray

    def colpack(vecs):
        out = np.zeros((128, 48), np.float32)
        for j, v in enumerate(vecs):
            out[:, j * 8:(j + 1) * 8] = v.reshape(8, 128).T
        return out
    xT = np.concatenate([yT_list[b][:, core * TB:(core + 1) * TB] for b in range(NB)], axis=1)
    gwc = np.zeros((128, 64), np.float32)
    gw = inputs["L1_ffn_gate_w"]
    for c in range(8):
        gwc[:, c * 8:(c + 1) * 8] = gw[:, c * 128:(c + 1) * 128].T
    return {
        "xT": asc(xT),
        "WinT": asc(inputs["L1_attn_W_in"].T), "WlamT": asc(inputs["L1_attn_W_lam"].T),
        "WgateT": asc(inputs["L1_attn_W_gate"].T), "WoutT": asc(inputs["L1_attn_W_out"].T),
        "bcols": colpack([inputs["L1_attn_b_in"], inputs["L1_attn_b_lam"],
                          inputs["L1_attn_b_gate"], inputs["L1_attn_b_out"],
                          inputs["L1_attn_gamma"], inputs["L1_attn_beta"]]),
        "gwc": asc(gwc),
        "grows": inputs["L1_attn_gamma"].reshape(1, D).astype(np.float32),
        "wgT": asc(np.transpose(inputs["L1_ffn_wg"], (0, 2, 1))),
        "wuT": asc(np.transpose(inputs["L1_ffn_wu"], (0, 2, 1))),
        "wdT": asc(np.transpose(inputs["L1_ffn_wd"], (0, 2, 1))),
    }


from concourse.bass_utils import run_bass_kernel_spmd

_CACHE = {}


def kernel(**inputs):
    if "A" not in _CACHE:
        _CACHE["A"] = build_launchA()
        _CACHE["B"] = build_launchB()
    ncA, ncB = _CACHE["A"], _CACHE["B"]

    in_maps_A = [prep_inputs_A(inputs, b) for b in range(8)]
    resA = run_bass_kernel_spmd(ncA, in_maps_A, core_ids=list(range(8)))
    yT_list = [resA.results[b]["yT"] for b in range(8)]

    in_maps_B = [prep_inputs_B(inputs, yT_list, c) for c in range(8)]
    resB = run_bass_kernel_spmd(ncB, in_maps_B, core_ids=list(range(8)))

    out = np.empty((8, 2048, 1024), np.float32)
    for c in range(8):
        out[:, c * TB:(c + 1) * TB, :] = resB.results[c]["outT"].T.reshape(NB, TB, D)
    return out


# version 08
# speedup vs baseline: 1.2343x; 1.0090x over previous
import sys
sys.path.insert(0, "/opt/trn_rl_repo")
import numpy as np
import ml_dtypes
import concourse.bass as bass
import concourse.mybir as mybir

F32 = mybir.dt.float32
BF16 = mybir.dt.bfloat16
AF = mybir.ActivationFunctionType
ALU = mybir.AluOpType
AX = mybir.AxisListType

ENGS = ("tensor", "vector", "scalar", "gpsimd", "sync")


class Step:
    __slots__ = ("eng", "fn", "deps", "idx", "val", "ndma", "_sem_holder")

    def __init__(self, eng, fn, deps, idx, ndma):
        self.eng, self.fn, self.deps, self.idx, self.ndma = eng, fn, deps, idx, ndma
        self.val = None


class PB:
    def __init__(self, nc):
        self.nc = nc
        self.steps = []
        self.counts = {e: 0 for e in ENGS}
        self.prev = None

    def step(self, eng, fn, deps=None, ndma=0):
        if deps is None:
            deps = [self.prev] if self.prev is not None else []
        deps = [d for d in deps if d is not None]
        s = Step(eng, fn, deps, len(self.steps), ndma)
        self.counts[eng] += (16 * ndma) if ndma else 1
        s.val = self.counts[eng]
        self.steps.append(s)
        self.prev = s
        return s

    def dma(self, queue, pairs, deps=None):
        holder = {}

        def fn(e):
            sem = holder["sem"]
            for o, i in pairs:
                e.dma_start(out=o, in_=i).then_inc(sem, 16)
            return None
        s = self.step(queue, fn, deps=deps, ndma=len(pairs))
        s._sem_holder = holder
        return s

    def emit(self, block, sems):
        per_eng = {e: [s for s in self.steps if s.eng == e] for e in ENGS}

        def make_body(eng):
            def body(e):
                watermark = {}
                for s in per_eng[eng]:
                    best = {}
                    for d in s.deps:
                        if d.eng not in best or d.val > best[d.eng].val:
                            best[d.eng] = d
                    for d in best.values():
                        if d.eng == eng == "tensor":
                            continue
                        if d.val <= watermark.get(d.eng, -1):
                            continue
                        e.wait_ge(sems[d.eng], d.val)
                        watermark[d.eng] = d.val
                    if s.ndma:
                        s._sem_holder["sem"] = sems[eng]
                        s.fn(e)
                    else:
                        last = s.fn(e)
                        assert last is not None, f"step {s.idx} on {eng} returned None"
                        last.then_inc(sems[eng], 1)
            return body

        for eng in ENGS:
            if per_eng[eng]:
                getattr(block, eng)(make_body(eng))


D, T = 1024, 2048
NCD = 8
H, DK = 8, 128
CH = 64
NCHUNK = T // CH
TT = 512
NTT = T // TT
DH = 4096
NH = DH // 128
LN_EPS = 1e-5


def colslice_ap(w_d, o0, width, nchunks):
    return w_d.rearrange("(c p) o -> p c o", p=128)[:, :nchunks, o0:o0 + width]


class SMap:

    def __init__(self, nc):
        self.nc = nc
        self.off = (nc.sbuf_base + 31) & ~31
        self.top = nc.sbuf_top
        self.offsets = {}

    def at(self, name, shape, dtype, offset):
        return self.nc.alloc_sbuf_tensor_at(name, shape, dtype, offset=offset)

    def seq(self, name, shape, dtype):
        per = int(np.prod(shape[1:])) * mybir.dt.size(dtype)
        h = self.at(name, shape, dtype, self.off)
        self.offsets[name] = self.off
        self.off = (self.off + per + 31) & ~31
        assert self.off <= self.top, f"SBUF overflow at {name}: {self.off} > {self.top}"
        return h


def build_launchA():
    nc = bass.Bass("TRN2", target_bir_lowering=False)

    xT_d = nc.dram_tensor("xT", [D, T], F32, kind="ExternalInput").ap()
    wq_d = nc.dram_tensor("Wq", [D, D], F32, kind="ExternalInput").ap()
    wk_d = nc.dram_tensor("Wk", [D, D], F32, kind="ExternalInput").ap()
    wv_d = nc.dram_tensor("Wv", [D, D], F32, kind="ExternalInput").ap()
    wg_d = nc.dram_tensor("Wg", [D, D], F32, kind="ExternalInput").ap()
    wo_d = nc.dram_tensor("Wo", [D, D], F32, kind="ExternalInput").ap()
    w1_d = nc.dram_tensor("w1T", [D, DH], F32, kind="ExternalInput").ap()
    w2_d = nc.dram_tensor("w2T", [D, DH], F32, kind="ExternalInput").ap()
    h1_d = nc.dram_tensor("h1T", [DH, DH], F32, kind="ExternalInput").ap()
    h2_d = nc.dram_tensor("h2T", [DH, DH], F32, kind="ExternalInput").ap()
    w3_d = nc.dram_tensor("w3T", [DH, D], F32, kind="ExternalInput").ap()
    s1_d = nc.dram_tensor("s1", [128, NH], F32, kind="ExternalInput").ap()
    s2_d = nc.dram_tensor("s2", [128, NH], F32, kind="ExternalInput").ap()
    sh1_d = nc.dram_tensor("sh1", [128, NH], F32, kind="ExternalInput").ap()
    sh2_d = nc.dram_tensor("sh2", [128, NH], F32, kind="ExternalInput").ap()
    s3_d = nc.dram_tensor("s3", [128, NCD], F32, kind="ExternalInput").ap()
    yT_d = nc.dram_tensor("yT", [D, T], F32, kind="ExternalOutput").ap()

    w1b_d = nc.dram_tensor("w1b", [NH, 128, 1024], BF16)
    w2b_d = nc.dram_tensor("w2b", [NH, 128, 1024], BF16)
    h1b_d = nc.dram_tensor("h1b", [NH, 128, 4096], BF16)
    h2b_d = nc.dram_tensor("h2b", [NH, 128, 4096], BF16)
    w3b_d = nc.dram_tensor("w3b", [NCD, 128, 4096], BF16)
    tri_d = nc.inline_tensor(np.triu(np.ones((CH, CH), dtype=np.float32)), name="tric")
    id_d = nc.inline_tensor(np.eye(128, dtype=ml_dtypes.bfloat16), name="idc")

    sm = SMap(nc)
    x = sm.seq("x", [128, NCD * T], F32)
    xln = sm.seq("xln", [128, NCD * T], BF16)
    ofm = sm.seq("ofm", [128, H * T], BF16)
    wa = sm.seq("wa", [128, 4096], BF16)
    rowa = sm.seq("rowa", [1, 512], F32)
    rowb = sm.seq("rowb", [1, 512], F32)
    rowc = sm.seq("rowc", [1, 512], F32)
    tmpf = sm.seq("tmpf", [128, 512], F32)
    tmpf2 = sm.seq("tmpf2", [128, 512], F32)
    scr2 = sm.seq("scr2", [128, 512], BF16)
    sqln = sm.seq("sqln", [128, 512], BF16)
    sq2 = sm.seq("sq2", [128, 512], BF16)
    sq3 = sm.seq("sq3", [128, 512], BF16)
    ones1 = sm.seq("ones1", [1, 128], F32)
    onesc = sm.seq("onesc", [128, 1], BF16)
    tri = sm.seq("tri", [CH, CH], F32)
    idm = sm.seq("idm", [128, 128], BF16)
    scol = sm.seq("scol", [128, 2 * NH + NCD], F32)
    scol2 = sm.seq("scol2", [128, 2 * NH], F32)
    cns = sm.seq("cns", [1, 8], F32)
    U = sm.off
    vtm = sm.at("vtm", [64, NCHUNK * 128], BF16, U + 0)
    qh = sm.at("qh", [128, T], BF16, U + 8192)
    kh = sm.at("kh", [128, T], BF16, U + 12288)
    gh = sm.at("gh", [128, T], F32, U + 16384)
    ke = sm.at("ke", [128, T], BF16, U + 16384)
    ah = sm.at("ah", [128, T], F32, U + 24576)
    qs = sm.at("qs", [128, T], BF16, U + 32768)
    ks = sm.at("ks", [128, T], BF16, U + 36864)
    sst = sm.at("sst", [128, 128], BF16, U + 40960)
    pm = sm.at("pm", [64, 2 * CH], BF16, U + 41472)
    rahg = sm.at("rahg", [128, 512], F32, U + 41984)
    whead = sm.at("whead", [128, 4096], BF16, U + 48032)
    ketm = sm.at("ketm", [64, NCHUNK * 128], BF16, U + 48032)
    wb = sm.at("wb", [128, 4096], BF16, U + 48032)
    GLA_END = U + 48032 + 8192
    hb = sm.at("hb", [128, NH * TT], BF16, U)
    g1 = sm.at("g1", [128, NH * TT], BF16, sm.offsets["ofm"])
    assert GLA_END <= sm.top and U + NH * TT * 2 <= sm.top, (U, GLA_END, sm.top)

    from contextlib import ExitStack
    with ExitStack() as es:
        E = es.enter_context
        ps = [E(nc.psum_tensor(f"ps{i}", [128, 512], F32)) for i in range(6)]
        psT = E(nc.psum_tensor("psT", [128, 1024], BF16))
        psP = E(nc.psum_tensor("psP", [64, 512], F32))
        sems = {e: E(nc.semaphore("sem_" + e)) for e in ENGS}
        block = E(nc.Block())

        pb = PB(nc)

        ld0 = pb.dma("gpsimd", [(x[:, c * T:(c + 1) * T], xT_d[c * 128:(c + 1) * 128, :])
                                for c in range(NCD)], deps=[])
        ldc = pb.dma("sync", [(tri[:, :], tri_d.ap()), (idm[:, :], id_d.ap()),
                              (scol[:, 0:NH], s1_d[:, :]), (scol[:, NH:2 * NH], s2_d[:, :]),
                              (scol[:, 2 * NH:2 * NH + NCD], s3_d[:, :]),
                              (scol2[:, 0:NH], sh1_d[:, :]), (scol2[:, NH:2 * NH], sh2_d[:, :])],
                     deps=[])

        def ones_fn(e):
            e.memset(ones1[:, :], 1.0)
            e.memset(cns[:1, :], LN_EPS)
            return e.memset(onesc[:, :], 1.0)
        pb.step("vector", ones_fn, deps=[])
        pb.step("scalar", lambda e: e.nop(), deps=[ld0, ldc, pb.prev])

        tmpq = [sq2, sq3]
        tmpa = [tmpf, tmpf2]
        def emit_ln(src_t):
            entry = pb.prev
            rl = {}
            bcast = {}
            for g in range(4):
                sl = slice(g * 512, (g + 1) * 512)
                for c in range(NCD):
                    cs = slice(c * T + sl.start, c * T + sl.stop)
                    par = c % 2
                    cbuf = scr2 if par == 0 else sqln
                    s_cast = pb.step("vector", lambda e, cs=cs, cbuf=cbuf: e.tensor_copy(cbuf[:, :512], src_t[:, cs]),
                                     deps=[rl.get(("cb", par)), entry])
                    s_mm1 = pb.step("tensor", lambda e, c=c, cbuf=cbuf: e.matmul(
                        ps[0][:1, :], onesc[:, :], cbuf[:, :512], start=(c == 0), stop=(c == NCD - 1)),
                        deps=[s_cast, rl.get(("ps", 0))])
                    s_sq = pb.step("vector", lambda e, cbuf=cbuf, par=par: e.tensor_tensor(
                        tmpq[par][:, :], cbuf[:, :512], cbuf[:, :512], ALU.mult),
                        deps=[s_cast, rl.get(("sq", par))])
                    s_mm2 = pb.step("tensor", lambda e, c=c, par=par: e.matmul(
                        ps[1][:1, :], onesc[:, :], tmpq[par][:, :], start=(c == 0), stop=(c == NCD - 1)),
                        deps=[s_sq, rl.get(("ps", 1))])
                    rl[("cb", par)] = s_mm1
                    rl[("sq", par)] = s_mm2
                r1 = pb.step("vector", lambda e: e.tensor_scalar(rowa[:1, :], ps[0][:1, :], 1.0 / D, None, ALU.mult),
                             deps=[s_mm1, rl.get("rows")])
                rl[("ps", 0)] = r1
                r2 = pb.step("vector", lambda e: e.tensor_scalar(rowb[:1, :], ps[1][:1, :], 1.0 / D, None, ALU.mult),
                             deps=[s_mm2, r1, rl.get("bA")])
                rl[("ps", 1)] = r2
                r3 = pb.step("vector", lambda e: e.tensor_tensor(rowc[:1, :], rowa[:1, :], rowa[:1, :], ALU.mult),
                             deps=[r2, rl.get("bB")])
                r4 = pb.step("vector", lambda e: e.tensor_tensor(rowb[:1, :], rowb[:1, :], rowc[:1, :], ALU.subtract), deps=[r3])
                r5 = pb.step("scalar", lambda e: e.activation(rowb[:1, :], rowb[:1, :], AF.Ln, bias=cns[:1, :1]), deps=[r4])
                r6 = pb.step("scalar", lambda e: e.activation(rowb[:1, :], rowb[:1, :], AF.Exp, scale=-0.5), deps=[r5])
                r7 = pb.step("vector", lambda e: e.scalar_tensor_tensor(rowc[:1, :], rowa[:1, :], -1.0, rowb[:1, :], ALU.mult, ALU.mult), deps=[r6])
                rl["rows"] = r7
                bA = pb.step("tensor", lambda e: e.matmul(ps[2][:, :], ones1[:, :], rowb[:1, :], start=True, stop=True),
                             deps=[r6, r7, rl.get(("ps", 2))])
                bB = pb.step("tensor", lambda e: e.matmul(ps[3][:, :], ones1[:, :], rowc[:1, :], start=True, stop=True),
                             deps=[r7, rl.get(("ps", 3))])
                rl["bA"] = bA
                rl["bB"] = bB
                ap_last = None
                for c in range(NCD):
                    cs = slice(c * T + sl.start, c * T + sl.stop)
                    par = c % 2
                    a1 = pb.step("vector", lambda e, cs=cs, par=par: e.tensor_tensor(
                        tmpa[par][:, :], src_t[:, cs], ps[2][:, :], ALU.mult),
                        deps=[bA, rl.get(("ta", par))])
                    a2 = pb.step("vector", lambda e, cs=cs, par=par: e.tensor_tensor(
                        xln[:, cs], tmpa[par][:, :], ps[3][:, :], ALU.add),
                        deps=[bB, a1])
                    rl[("ta", par)] = a2
                    ap_last = a2
                rl[("ps", 2)] = ap_last
                rl[("ps", 3)] = ap_last
            pb.prev = pb.step("scalar", lambda e: e.nop(), deps=[ap_last])

        emit_ln(x)

        for h in range(H):
            pb.dma("gpsimd", [
                (whead[:, 0:1024].rearrange("p (c o) -> p c o", o=128), colslice_ap(wq_d, h * 128, 128, 8)),
                (whead[:, 1024:2048].rearrange("p (c o) -> p c o", o=128), colslice_ap(wk_d, h * 128, 128, 8)),
                (whead[:, 2048:3072].rearrange("p (c o) -> p c o", o=128), colslice_ap(wv_d, h * 128, 128, 8)),
                (whead[:, 3072:4096].rearrange("p (c o) -> p c o", o=128), colslice_ap(wg_d, h * 128, 128, 8))])
            lw = pb.prev
            grdr = {}
            for off, dst, mode in ((0, qh, "c"), (1024, kh, "c"), (3072, gh, "s")):
                for g in range(4):
                    sl = slice(g * 512, (g + 1) * 512)
                    bank = g % 2

                    def mmfn(e, off=off, sl=sl, bank=bank):
                        r = None
                        for c in range(NCD):
                            r = e.matmul(ps[bank][:, :], whead[:, off + c * 128:off + (c + 1) * 128],
                                         xln[:, c * T + sl.start:c * T + sl.stop],
                                         start=(c == 0), stop=(c == NCD - 1))
                        return r
                    s_mm = pb.step("tensor", mmfn, deps=[lw, grdr.get(("ps", bank))])
                    if mode == "c":
                        s_ev = pb.step("vector", lambda e, dst=dst, sl=sl, bank=bank: e.tensor_copy(dst[:, sl], ps[bank][:, :]),
                                       deps=[s_mm])
                    else:
                        s_ev = pb.step("scalar", lambda e, dst=dst, sl=sl, bank=bank: e.activation(dst[:, sl], ps[bank][:, :], AF.Sigmoid),
                                       deps=[s_mm])
                    grdr[("ps", bank)] = s_ev
            for cc in range(NCHUNK):
                tsl = slice(cc * CH, (cc + 1) * CH)
                bank = 2 + cc % 2

                def vfn(e, tsl=tsl, bank=bank):
                    r = None
                    for c in range(NCD):
                        r = e.matmul(ps[bank][:64, :128], xln[:, c * T + tsl.start:c * T + tsl.stop],
                                     whead[:, 2048 + c * 128:2048 + (c + 1) * 128],
                                     start=(c == 0), stop=(c == NCD - 1))
                    return r
                s_mm = pb.step("tensor", vfn, deps=[lw, grdr.get(("ps", bank))])
                s_ev = pb.step("vector", lambda e, cc=cc, bank=bank: e.tensor_copy(vtm[:, cc * 128:(cc + 1) * 128], ps[bank][:64, :128]),
                               deps=[s_mm])
                grdr[("ps", bank)] = s_ev
            pb.prev = pb.step("scalar", lambda e: e.nop(), deps=[pb.prev, s_ev])
            pb.step("scalar", lambda e: e.activation(gh[:, :], gh[:, :], AF.Ln))
            for cc in range(NCHUNK):
                sl = slice(cc * CH, (cc + 1) * CH)
                pb.step("vector", lambda e, sl=sl: e.tensor_tensor_scan(gh[:, sl], gh[:, sl], gh[:, sl], 0.0, ALU.add, ALU.bypass))
            pb.step("scalar", lambda e: e.activation(ah[:, :], gh[:, :], AF.Exp))
            eA = pb.prev
            for g in range(4):
                sl = slice(g * 512, (g + 1) * 512)
                pb.step("vector", lambda e, sl=sl: e.tensor_tensor(sq2[:, :], qh[:, sl], qh[:, sl], ALU.mult))
                pb.step("tensor", lambda e: e.matmul(ps[0][:1, :], onesc[:, :], sq2[:, :], start=True, stop=True))
                pb.step("vector", lambda e: e.tensor_copy(rowa[:1, :], ps[0][:1, :]))
                pb.step("scalar", lambda e: e.activation(rowa[:1, :], rowa[:1, :], AF.Ln))
                pb.step("scalar", lambda e: e.activation(rowa[:1, :], rowa[:1, :], AF.Exp, scale=-0.5))
                pb.step("tensor", lambda e: e.matmul(ps[2][:, :], ones1[:, :], rowa[:1, :], start=True, stop=True))
                pb.step("vector", lambda e, sl=sl: e.tensor_tensor(tmpf[:, :], ah[:, sl], ps[2][:, :], ALU.mult),
                        deps=[eA, pb.prev])
                pb.step("vector", lambda e, sl=sl: e.tensor_tensor(qs[:, sl], tmpf[:, :], qh[:, sl], ALU.mult))
            for g in range(4):
                sl = slice(g * 512, (g + 1) * 512)
                pb.step("vector", lambda e, sl=sl: e.tensor_tensor(sq2[:, :], kh[:, sl], kh[:, sl], ALU.mult))
                pb.step("tensor", lambda e: e.matmul(ps[0][:1, :], onesc[:, :], sq2[:, :], start=True, stop=True))
                pb.step("vector", lambda e: e.tensor_copy(rowa[:1, :], ps[0][:1, :]))
                pb.step("scalar", lambda e: e.activation(rowa[:1, :], rowa[:1, :], AF.Ln))
                pb.step("scalar", lambda e: e.activation(rowa[:1, :], rowa[:1, :], AF.Exp, scale=-0.5))
                pb.step("tensor", lambda e: e.matmul(ps[2][:, :], ones1[:, :], rowa[:1, :], start=True, stop=True))
                pb.step("scalar", lambda e, sl=sl: e.activation(rahg[:, :], gh[:, sl], AF.Exp, scale=-1.0))
                pb.step("vector", lambda e: e.tensor_tensor(rahg[:, :], rahg[:, :], ps[2][:, :], ALU.mult))
                pb.step("vector", lambda e, sl=sl: e.tensor_tensor(ks[:, sl], rahg[:, :], kh[:, sl], ALU.mult))
            for cc in range(NCHUNK):
                sl = slice(cc * CH, (cc + 1) * CH)
                col = slice(cc * CH + CH - 1, cc * CH + CH)
                pb.step("vector", lambda e, sl=sl, col=col: e.tensor_scalar(ke[:, sl], ks[:, sl], ah[:, col], None, ALU.mult))
            for cc in range(NCHUNK):
                sl = slice(cc * CH, (cc + 1) * CH)
                pb.step("tensor", lambda e, sl=sl: e.transpose(psT[:64, :128], ke[:, sl], idm[:, :]))
                pb.step("vector", lambda e, cc=cc: e.tensor_copy(ketm[:, cc * 128:(cc + 1) * 128], psT[:64, :128]))
            pb.step("vector", lambda e: e.memset(sst[:, :], 0.0))
            crdr = {}
            elem_done = pb.prev
            s_supd = {}
            s_ofn = {}
            for cc in range(NCHUNK):
                sl = slice(cc * CH, (cc + 1) * CH)
                col = slice(cc * CH + CH - 1, cc * CH + CH)
                pbank = 2 + cc % 2
                tbank = 5 if cc % 2 == 0 else 0
                slot = (cc % 2) * 64
                s_pmm = pb.step("tensor", lambda e, sl=sl, pbank=pbank: e.matmul(
                    ps[pbank][:64, :64], ks[:, sl], qs[:, sl], start=True, stop=True),
                    deps=[elem_done, crdr.get(("ps", pbank))])
                s_msk = pb.step("vector", lambda e, pbank=pbank, slot=slot: e.tensor_tensor(
                    pm[:, slot:slot + 64], ps[pbank][:64, :64], tri[:, :], ALU.mult),
                    deps=[s_pmm, crdr.get(("pm", slot))])
                crdr[("ps", pbank)] = s_msk

                def ofn(e, sl=sl, cc=cc, slot=slot):
                    e.matmul(ps[4][:, :64], vtm[:, cc * 128:(cc + 1) * 128], pm[:, slot:slot + 64], start=True, stop=False)
                    return e.matmul(ps[4][:, :64], sst[:, :], qs[:, sl], start=False, stop=True)
                s_ofn[cc] = pb.step("tensor", ofn, deps=[s_msk, s_supd.get(cc - 1), crdr.get(("ps", 4))])
                crdr[("pm", slot)] = s_ofn[cc]
                s_oev = pb.step("vector", lambda e, sl=sl, h=h: e.tensor_copy(
                    ofm[:, h * T + sl.start:h * T + sl.stop], ps[4][:, :64]), deps=[s_ofn[cc]])
                crdr[("ps", 4)] = s_oev

                def tfn(e, cc=cc, tbank=tbank):
                    return e.matmul(ps[tbank][:, :128], ketm[:, cc * 128:(cc + 1) * 128],
                                    vtm[:, cc * 128:(cc + 1) * 128], start=True, stop=True)
                s_tmm = pb.step("tensor", tfn, deps=[elem_done, crdr.get(("ps", tbank))])
                s_supd[cc] = pb.step("vector", lambda e, col=col, tbank=tbank: e.scalar_tensor_tensor(
                    sst[:, :], sst[:, :], ah[:, col], ps[tbank][:, :128], ALU.mult, ALU.add),
                    deps=[s_tmm, s_ofn[cc]])
                crdr[("ps", tbank)] = s_supd[cc]
            pb.prev = pb.step("scalar", lambda e: e.nop(), deps=[s_supd[NCHUNK - 1], pb.prev])

        rdr = {}
        gla_done = pb.prev
        ldW = {}
        mmW = {}
        accW = {}
        for dc in range(NCD):
            par = dc % 2
            buf = wa if par == 0 else wb
            ldW[dc] = pb.dma("gpsimd", [(buf[:, 0:1024].rearrange("p (c o) -> p c o", o=128),
                                         colslice_ap(wo_d, dc * 128, 128, 8))],
                             deps=[rdr.get(("wbuf", par)), gla_done])
            for g in range(4):
                sl = slice(g * 512, (g + 1) * 512)
                bank = 2 * par + (g % 2)

                def mmfn(e, sl=sl, buf=buf, bank=bank):
                    r = None
                    for c in range(NCD):
                        r = e.matmul(ps[bank][:, :], buf[:, c * 128:(c + 1) * 128],
                                     ofm[:, c * T + sl.start:c * T + sl.stop],
                                     start=(c == 0), stop=(c == NCD - 1))
                    return r
                mmW[(dc, g)] = pb.step("tensor", mmfn, deps=[ldW[dc], rdr.get(("ps", bank)), gla_done])
                rdr[("wbuf", par)] = mmW[(dc, g)]
                cs = slice(dc * T + sl.start, dc * T + sl.stop)
                accW[(dc, g)] = pb.step("vector", lambda e, cs=cs, bank=bank: e.tensor_tensor(
                    x[:, cs], x[:, cs], ps[bank][:, :], ALU.add), deps=[mmW[(dc, g)]])
                rdr[("ps", bank)] = accW[(dc, g)]
        wo_done = pb.step("scalar", lambda e: e.nop(),
                          deps=[accW[(NCD - 1, 3)], accW[(NCD - 2, 3)], accW[(NCD - 1, 2)]])

        pb.prev = wo_done
        emit_ln(x)
        ln2_done = pb.prev

        stores = {}
        last_store = {}
        ofm_free = wo_done
        hb_free = gla_done
        prev_tt_w3 = ln2_done
        prev_tt_h2 = None
        sA = [scr2, sqln]
        for tt in range(NTT):
            tsl = slice(tt * TT, (tt + 1) * TT)
            mt1, mt2, sil, hm, ld12 = {}, {}, {}, {}, {}
            for oc in range(NH):
                par = oc % 2
                buf = wa if par == 0 else wb
                if tt == 0:
                    ld12[oc] = pb.dma("gpsimd", [
                        (buf[:, 0:1024].rearrange("p (c o) -> p c o", o=128),
                         colslice_ap(w1_d, oc * 128, 128, 8)),
                        (buf[:, 1024:2048].rearrange("p (c o) -> p c o", o=128),
                         colslice_ap(w2_d, oc * 128, 128, 8))],
                        deps=[rdr.get(("wbuf", par)), last_store.get(par), ln2_done])
                    stores[("w12", oc)] = pb.dma("sync", [
                        (w1b_d[oc], buf[:, 0:1024]), (w2b_d[oc], buf[:, 1024:2048])],
                        deps=[ld12[oc]])
                    last_store[par] = stores[("w12", oc)]
                else:
                    ld12[oc] = pb.dma("sync", [
                        (buf[:, 0:1024], w1b_d[oc]), (buf[:, 1024:2048], w2b_d[oc])],
                        deps=[rdr.get(("wbuf", par)), last_store.get(par), stores[("w12", oc)], ln2_done])
                b1, b2 = 2 * par, 2 * par + 1

                def m1f(e, tsl=tsl, buf=buf, b1=b1):
                    r = None
                    for c in range(NCD):
                        r = e.matmul(ps[b1][:, :], buf[:, c * 128:(c + 1) * 128],
                                     xln[:, c * T + tsl.start:c * T + tsl.stop],
                                     start=(c == 0), stop=(c == NCD - 1))
                    return r
                mt1[oc] = pb.step("tensor", m1f, deps=[ld12[oc], rdr.get(("ps", b1)), ln2_done])
                rdr[("wbuf", par)] = mt1[oc]

                def m2f(e, tsl=tsl, buf=buf, b2=b2):
                    r = None
                    for c in range(NCD):
                        r = e.matmul(ps[b2][:, :], buf[:, 1024 + c * 128:1024 + (c + 1) * 128],
                                     xln[:, c * T + tsl.start:c * T + tsl.stop],
                                     start=(c == 0), stop=(c == NCD - 1))
                    return r
                mt2[oc] = pb.step("tensor", m2f, deps=[ld12[oc], rdr.get(("ps", b2))])
                rdr[("wbuf", par)] = mt2[oc]
                sil[oc] = pb.step("scalar", lambda e, oc=oc, par=par, b1=b1: e.activation(
                    sA[par][:, :], ps[b1][:, :], AF.Silu, scale=scol[:, oc:oc + 1]),
                    deps=[mt1[oc], rdr.get(("sA", par))])
                rdr[("ps", b1)] = sil[oc]
                hm[oc] = pb.step("vector", lambda e, oc=oc, par=par, b2=b2: e.scalar_tensor_tensor(
                    hb[:, oc * TT:(oc + 1) * TT], ps[b2][:, :], scol[:, NH + oc:NH + oc + 1],
                    sA[par][:, :], ALU.mult, ALU.mult),
                    deps=[mt2[oc], sil[oc], prev_tt_w3, hb_free])
                rdr[("ps", b2)] = hm[oc]
                rdr[("sA", par)] = hm[oc]
            hm_last = hm[NH - 1]
            mmh, gel, ldh = {}, {}, {}
            for oc in range(NH):
                par = oc % 2
                buf = wa if par == 0 else wb
                if tt == 0:
                    ldh[oc] = pb.dma("gpsimd", [(buf[:, :].rearrange("p (c o) -> p c o", o=128),
                                                 colslice_ap(h1_d, oc * 128, 128, 32))],
                                     deps=[rdr.get(("wbuf", par)), last_store.get(par)])
                    stores[("h1", oc)] = pb.dma("sync", [(h1b_d[oc], buf[:, :])], deps=[ldh[oc]])
                    last_store[par] = stores[("h1", oc)]
                else:
                    ldh[oc] = pb.dma("sync", [(buf[:, :], h1b_d[oc])],
                                     deps=[rdr.get(("wbuf", par)), last_store.get(par), stores[("h1", oc)]])
                bank = par

                def mhf(e, buf=buf, bank=bank):
                    r = None
                    for ic in range(NH):
                        r = e.matmul(ps[bank][:, :], buf[:, ic * 128:(ic + 1) * 128],
                                     hb[:, ic * TT:(ic + 1) * TT],
                                     start=(ic == 0), stop=(ic == NH - 1))
                    return r
                mmh[oc] = pb.step("tensor", mhf, deps=[ldh[oc], rdr.get(("ps", bank)), hm_last])
                rdr[("wbuf", par)] = mmh[oc]
                gel[oc] = pb.step("scalar", lambda e, oc=oc, bank=bank: e.activation(
                    g1[:, oc * TT:(oc + 1) * TT], ps[bank][:, :], AF.Gelu, scale=scol2[:, oc:oc + 1]),
                    deps=[mmh[oc], ofm_free, prev_tt_h2])
                rdr[("ps", bank)] = gel[oc]
            gel_last = gel[NH - 1]
            mmh2, gel2, ldh2 = {}, {}, {}
            for oc in range(NH):
                par = oc % 2
                buf = wa if par == 0 else wb
                if tt == 0:
                    ldh2[oc] = pb.dma("gpsimd", [(buf[:, :].rearrange("p (c o) -> p c o", o=128),
                                                  colslice_ap(h2_d, oc * 128, 128, 32))],
                                      deps=[rdr.get(("wbuf", par)), last_store.get(par)])
                    stores[("h2", oc)] = pb.dma("sync", [(h2b_d[oc], buf[:, :])], deps=[ldh2[oc]])
                    last_store[par] = stores[("h2", oc)]
                else:
                    ldh2[oc] = pb.dma("sync", [(buf[:, :], h2b_d[oc])],
                                      deps=[rdr.get(("wbuf", par)), last_store.get(par), stores[("h2", oc)]])
                bank = 2 + par

                def mh2f(e, buf=buf, bank=bank):
                    r = None
                    for ic in range(NH):
                        r = e.matmul(ps[bank][:, :], buf[:, ic * 128:(ic + 1) * 128],
                                     g1[:, ic * TT:(ic + 1) * TT],
                                     start=(ic == 0), stop=(ic == NH - 1))
                    return r
                mmh2[oc] = pb.step("tensor", mh2f, deps=[ldh2[oc], rdr.get(("ps", bank)), gel_last])
                rdr[("wbuf", par)] = mmh2[oc]
                gel2[oc] = pb.step("scalar", lambda e, oc=oc, bank=bank: e.activation(
                    hb[:, oc * TT:(oc + 1) * TT], ps[bank][:, :], AF.Gelu, scale=scol2[:, NH + oc:NH + oc + 1]),
                    deps=[mmh2[oc]])
                rdr[("ps", bank)] = gel2[oc]
            gel2_last = gel2[NH - 1]
            prev_tt_h2 = gel2_last
            mmy, accy, ldy = {}, {}, {}
            for dc in range(NCD):
                par = dc % 2
                buf = wa if par == 0 else wb
                if tt == 0:
                    ldy[dc] = pb.dma("gpsimd", [(buf[:, :].rearrange("p (c o) -> p c o", o=128),
                                                 colslice_ap(w3_d, dc * 128, 128, 32))],
                                     deps=[rdr.get(("wbuf", par)), last_store.get(par)])
                    stores[("w3", dc)] = pb.dma("sync", [(w3b_d[dc], buf[:, :])], deps=[ldy[dc]])
                    last_store[par] = stores[("w3", dc)]
                else:
                    ldy[dc] = pb.dma("sync", [(buf[:, :], w3b_d[dc])],
                                     deps=[rdr.get(("wbuf", par)), last_store.get(par), stores[("w3", dc)]])
                bank = 4 + par

                def myf(e, buf=buf, bank=bank):
                    r = None
                    for ic in range(NH):
                        r = e.matmul(ps[bank][:, :], buf[:, ic * 128:(ic + 1) * 128],
                                     hb[:, ic * TT:(ic + 1) * TT],
                                     start=(ic == 0), stop=(ic == NH - 1))
                    return r
                mmy[dc] = pb.step("tensor", myf, deps=[ldy[dc], rdr.get(("ps", bank)), gel2_last])
                rdr[("wbuf", par)] = mmy[dc]
                cs = slice(dc * T + tsl.start, dc * T + tsl.stop)
                accy[dc] = pb.step("vector", lambda e, cs=cs, dc=dc, bank=bank: e.scalar_tensor_tensor(
                    x[:, cs], ps[bank][:, :], scol[:, 2 * NH + dc:2 * NH + dc + 1], x[:, cs], ALU.mult, ALU.add),
                    deps=[mmy[dc]])
                rdr[("ps", bank)] = accy[dc]
            prev_tt_w3 = pb.step("scalar", lambda e: e.nop(),
                                 deps=[accy[NCD - 1], accy[NCD - 2]])
            pb.prev = prev_tt_w3

        stv = pb.counts["sync"] + 16 * NCD
        pb.dma("sync", [(yT_d[c * 128:(c + 1) * 128, :], x[:, c * T:(c + 1) * T]) for c in range(NCD)])

        def fin(e):
            e.wait_ge(sems["sync"], stv)
            return e.nop()
        pb.step("sync", fin)

        pb.emit(block, sems)
    return nc


def prep_inputs_A(inputs, b):
    asc = np.ascontiguousarray
    return {
        "xT": asc(inputs["L0_attn_x"][b].T),
        "Wq": asc(inputs["L0_attn_Wq"]), "Wk": asc(inputs["L0_attn_Wk"]),
        "Wv": asc(inputs["L0_attn_Wv"]), "Wg": asc(inputs["L0_attn_Wg"]),
        "Wo": asc(inputs["L0_attn_Wo"]),
        "w1T": asc(inputs["L0_ffn_w1_w"].T), "w2T": asc(inputs["L0_ffn_w2_w"].T),
        "h1T": asc(inputs["L0_ffn_h1_w"].T), "h2T": asc(inputs["L0_ffn_h2_w"].T),
        "w3T": asc(inputs["L0_ffn_w3_w"].T),
        "s1": asc(inputs["L0_ffn_w1_s"].reshape(NH, 128).T),
        "s2": asc(inputs["L0_ffn_w2_s"].reshape(NH, 128).T),
        "sh1": asc(inputs["L0_ffn_h1_s"].reshape(NH, 128).T),
        "sh2": asc(inputs["L0_ffn_h2_s"].reshape(NH, 128).T),
        "s3": asc(inputs["L0_ffn_w3_s"].reshape(NCD, 128).T),
    }


TT2 = 1024
NT2 = T // TT2
NE = 8
NB = 8
TB = 256


def build_launchB(dbg=False):
    nc = bass.Bass("TRN2", target_bir_lowering=False)

    xT_d = nc.dram_tensor("xT", [D, T], F32, kind="ExternalInput").ap()
    win_d = nc.dram_tensor("WinT", [D, D], F32, kind="ExternalInput").ap()
    wlam_d = nc.dram_tensor("WlamT", [D, D], F32, kind="ExternalInput").ap()
    wgate_d = nc.dram_tensor("WgateT", [D, D], F32, kind="ExternalInput").ap()
    wout_d = nc.dram_tensor("WoutT", [D, D], F32, kind="ExternalInput").ap()
    bcols_d = nc.dram_tensor("bcols", [128, 48], F32, kind="ExternalInput").ap()
    gwc_d = nc.dram_tensor("gwc", [128, 64], F32, kind="ExternalInput").ap()
    grows_d = nc.dram_tensor("grows", [1, D], F32, kind="ExternalInput").ap()
    wgT_d = nc.dram_tensor("wgT", [NE, D, DH], F32, kind="ExternalInput").ap()
    wuT_d = nc.dram_tensor("wuT", [NE, D, DH], F32, kind="ExternalInput").ap()
    wdT_d = nc.dram_tensor("wdT", [NE, DH, D], F32, kind="ExternalInput").ap()
    outT_d = nc.dram_tensor("outT", [D, T], F32, kind="ExternalOutput").ap()
    if dbg:
        dbgA_d = nc.dram_tensor("dbgA", [D, T], F32, kind="ExternalOutput").ap()
        dbgL_d = nc.dram_tensor("dbgL", [D, T], BF16, kind="ExternalOutput").ap()
        dbgH_d = nc.dram_tensor("dbgH", [D, T], BF16, kind="ExternalOutput").ap()
        cmb_d = nc.dram_tensor("cmbo", [8, T], BF16, kind="ExternalOutput").ap()
        cmbt_d = nc.dram_tensor("cmbt_o", [128, 128], F32, kind="ExternalOutput").ap()
        dbgZ_d = nc.dram_tensor("dbgZ", [D, T], BF16, kind="ExternalOutput").ap()
        dbgX_d = nc.dram_tensor("dbgX", [D, T], F32, kind="ExternalOutput").ap()

    idf_d = nc.inline_tensor(np.eye(128, dtype=np.float32), name="idf")
    sel_np = np.zeros((8, 1024), dtype=ml_dtypes.bfloat16)
    for e in range(8):
        sel_np[e, e * 128:(e + 1) * 128] = 1.0
    sel_d = nc.inline_tensor(sel_np, name="selc")
    cmb_dram = nc.dram_tensor("cmbscr", [128, 128], F32)

    sm = SMap(nc)
    x = sm.seq("x", [128, NCD * T], F32)
    xln = sm.seq("xln", [128, NCD * T], BF16)
    wa = sm.seq("wa", [128, 4096], BF16)
    wb = sm.seq("wb", [128, 4096], BF16)
    scr3 = sm.seq("scr3", [128, 512], BF16)
    scr4 = sm.seq("scr4", [128, 512], BF16)
    tmpf2 = sm.seq("tmpf2", [128, 512], F32)
    grows = sm.seq("grows", [1, D], F32)
    rowa = sm.seq("rowa", [1, 512], F32)
    rowb = sm.seq("rowb", [1, 512], F32)
    rowc = sm.seq("rowc", [1, 512], F32)
    tmpf = sm.seq("tmpf", [128, 512], F32)
    scr2 = sm.seq("scr2", [128, 512], BF16)
    sqln = sm.seq("sqln", [128, 512], BF16)
    ones1 = sm.seq("ones1", [1, 128], F32)
    onesc = sm.seq("onesc", [128, 1], BF16)
    idf = sm.seq("idf", [128, 128], F32)
    sel = sm.seq("sel", [8, 1024], BF16)
    bcols = sm.seq("bcols", [128, 48], F32)
    wgw = sm.seq("wgw", [128, 64], BF16)
    cmbt = sm.seq("cmbt", [128, 128], F32)
    cmbu = sm.seq("cmbu", [128, 128], F32)
    cfm8 = sm.seq("cfm8", [8, T], BF16)
    cns = sm.seq("cns", [1, 8], F32)
    UB = sm.off
    lam = sm.at("lam", [128, NCD * T], BF16, UB)
    hbuf = sm.at("hbuf", [128, NCD * T], BF16, UB + 32768)
    hmoe = sm.at("hmoe", [128, NH * TT2], BF16, UB)
    cb = sm.at("cb", [128, T], BF16, sm.offsets["rowa"])
    assert UB + 65536 <= sm.top, (UB, sm.top)

    from contextlib import ExitStack
    with ExitStack() as es:
        E = es.enter_context
        ps = [E(nc.psum_tensor(f"ps{i}", [128, 512], F32)) for i in range(6)]
        psT = E(nc.psum_tensor("psT", [128, 512], F32))
        psX = E(nc.psum_tensor("psX", [128, 512], F32))
        sems = {e: E(nc.semaphore("sem_" + e)) for e in ENGS}
        block = E(nc.Block())

        pb = PB(nc)

        pb.dma("gpsimd", [(x[:, c * T:(c + 1) * T], xT_d[c * 128:(c + 1) * 128, :])
                          for c in range(NCD)], deps=[])
        ldc = pb.dma("sync", [(idf[:, :], idf_d.ap()), (sel[:, :], sel_d.ap()),
                              (bcols[:, :], bcols_d[:, :]), (grows[:1, :], grows_d[:, :])], deps=[])
        ldw = pb.dma("gpsimd", [(wgw[:, :], gwc_d[:, :])], deps=[])

        def ones_fn(e):
            e.memset(ones1[:, :], 1.0)
            e.memset(cns[:1, :], LN_EPS)
            return e.memset(onesc[:, :], 1.0)
        pb.step("vector", ones_fn, deps=[])
        pb.step("scalar", lambda e: e.nop(), deps=[pb.steps[0], ldc, ldw, pb.prev])

        tmpq = [scr3, scr4]
        tmpa = [tmpf, tmpf2]

        def emit_ln(src_t):
            entry = pb.prev
            rl = {}
            ap_last = None
            for g in range(4):
                sl = slice(g * 512, (g + 1) * 512)
                for c in range(NCD):
                    cs = slice(c * T + sl.start, c * T + sl.stop)
                    par = c % 2
                    cbuf = scr2 if par == 0 else sqln
                    s_cast = pb.step("vector", lambda e, cs=cs, cbuf=cbuf: e.tensor_copy(cbuf[:, :512], src_t[:, cs]),
                                     deps=[rl.get(("cb", par)), entry])
                    s_mm1 = pb.step("tensor", lambda e, c=c, cbuf=cbuf: e.matmul(
                        ps[0][:1, :], onesc[:, :], cbuf[:, :512], start=(c == 0), stop=(c == NCD - 1)),
                        deps=[s_cast, rl.get(("ps", 0))])
                    s_sq = pb.step("vector", lambda e, cbuf=cbuf, par=par: e.tensor_tensor(
                        tmpq[par][:, :], cbuf[:, :512], cbuf[:, :512], ALU.mult),
                        deps=[s_cast, rl.get(("sq", par))])
                    s_mm2 = pb.step("tensor", lambda e, c=c, par=par: e.matmul(
                        ps[1][:1, :], onesc[:, :], tmpq[par][:, :], start=(c == 0), stop=(c == NCD - 1)),
                        deps=[s_sq, rl.get(("ps", 1))])
                    rl[("cb", par)] = s_mm1
                    rl[("sq", par)] = s_mm2
                r1 = pb.step("vector", lambda e: e.tensor_scalar(rowa[:1, :], ps[0][:1, :], 1.0 / D, None, ALU.mult),
                             deps=[s_mm1, rl.get("rows")])
                rl[("ps", 0)] = r1
                r2 = pb.step("vector", lambda e: e.tensor_scalar(rowb[:1, :], ps[1][:1, :], 1.0 / D, None, ALU.mult),
                             deps=[s_mm2, r1, rl.get("bA")])
                rl[("ps", 1)] = r2
                r3 = pb.step("vector", lambda e: e.tensor_tensor(rowc[:1, :], rowa[:1, :], rowa[:1, :], ALU.mult),
                             deps=[r2, rl.get("bB")])
                r4 = pb.step("vector", lambda e: e.tensor_tensor(rowb[:1, :], rowb[:1, :], rowc[:1, :], ALU.subtract), deps=[r3])
                r5 = pb.step("scalar", lambda e: e.activation(rowb[:1, :], rowb[:1, :], AF.Ln, bias=cns[:1, :1]), deps=[r4])
                r6 = pb.step("scalar", lambda e: e.activation(rowb[:1, :], rowb[:1, :], AF.Exp, scale=-0.5), deps=[r5])
                r7 = pb.step("vector", lambda e: e.scalar_tensor_tensor(rowc[:1, :], rowa[:1, :], -1.0, rowb[:1, :], ALU.mult, ALU.mult), deps=[r6])
                rl["rows"] = r7
                bA = pb.step("tensor", lambda e: e.matmul(ps[2][:, :], ones1[:, :], rowb[:1, :], start=True, stop=True),
                             deps=[r6, r7, rl.get(("ps", 2))])
                bB = pb.step("tensor", lambda e: e.matmul(ps[3][:, :], ones1[:, :], rowc[:1, :], start=True, stop=True),
                             deps=[r7, rl.get(("ps", 3))])
                rl["bA"] = bA
                rl["bB"] = bB
                for c in range(NCD):
                    cs = slice(c * T + sl.start, c * T + sl.stop)
                    par = c % 2
                    a1 = pb.step("vector", lambda e, cs=cs, par=par: e.tensor_tensor(
                        tmpa[par][:, :], src_t[:, cs], ps[2][:, :], ALU.mult),
                        deps=[bA, rl.get(("ta", par))])
                    a2 = pb.step("vector", lambda e, cs=cs, par=par: e.tensor_tensor(
                        xln[:, cs], tmpa[par][:, :], ps[3][:, :], ALU.add),
                        deps=[bB, a1])
                    rl[("ta", par)] = a2
                    ap_last = a2
                rl[("ps", 2)] = ap_last
                rl[("ps", 3)] = ap_last
            pb.prev = pb.step("scalar", lambda e: e.nop(), deps=[ap_last])

        emit_ln(x)
        rdr = {}
        ln3_done = pb.prev
        sA = [scr2, scr3]
        sB = [tmpf]
        for oc in range(NCD):
            par = oc % 2
            buf = wa if par == 0 else wb
            ldL = pb.dma("gpsimd", [(buf[:, 0:1024].rearrange("p (c o) -> p c o", o=128),
                                     colslice_ap(wlam_d, oc * 128, 128, 8)),
                                    (buf[:, 1024:2048].rearrange("p (c o) -> p c o", o=128),
                                     colslice_ap(win_d, oc * 128, 128, 8))],
                         deps=[rdr.get(("wbuf", par)), ln3_done])
            for g in range(4):
                sl = slice(g * 512, (g + 1) * 512)
                b1, b2 = 2 * par, 2 * par + 1

                def mml(e, sl=sl, buf=buf, b1=b1):
                    r = None
                    for c in range(NCD):
                        r = e.matmul(ps[b1][:, :], buf[:, c * 128:(c + 1) * 128],
                                     xln[:, c * T + sl.start:c * T + sl.stop],
                                     start=(c == 0), stop=(c == NCD - 1))
                    return r
                s_mml = pb.step("tensor", mml, deps=[ldL, rdr.get(("ps", b1)), ln3_done])
                rdr[("wbuf", par)] = s_mml
                s_sig = pb.step("scalar", lambda e, oc=oc, sl=sl, b1=b1: e.activation(
                    lam[:, oc * T + sl.start:oc * T + sl.stop], ps[b1][:, :], AF.Sigmoid,
                    bias=bcols[:, 8 + oc:9 + oc]), deps=[s_mml])
                rdr[("ps", b1)] = s_sig

                def mmi(e, sl=sl, buf=buf, b2=b2):
                    r = None
                    for c in range(NCD):
                        r = e.matmul(ps[b2][:, :], buf[:, 1024 + c * 128:1024 + (c + 1) * 128],
                                     xln[:, c * T + sl.start:c * T + sl.stop],
                                     start=(c == 0), stop=(c == NCD - 1))
                    return r
                s_mmi = pb.step("tensor", mmi, deps=[ldL, rdr.get(("ps", b2))])
                rdr[("wbuf", par)] = s_mmi
                s_sil = pb.step("scalar", lambda e, oc=oc, par=par, b2=b2: e.activation(
                    sA[par][:, :], ps[b2][:, :], AF.Silu, bias=bcols[:, oc:1 + oc]),
                    deps=[s_mmi, rdr.get(("sA", par))])
                rdr[("ps", b2)] = s_sil
                s_om = pb.step("vector", lambda e, oc=oc, sl=sl: e.tensor_scalar(
                    tmpf[:, :], lam[:, oc * T + sl.start:oc * T + sl.stop], -1.0, 1.0, ALU.mult, ALU.add),
                    deps=[s_sig, rdr.get("tmpf")])
                s_hm = pb.step("vector", lambda e, oc=oc, sl=sl, par=par: e.tensor_tensor(
                    hbuf[:, oc * T + sl.start:oc * T + sl.stop], tmpf[:, :], sA[par][:, :], ALU.mult),
                    deps=[s_om, s_sil])
                rdr[("sA", par)] = s_hm
                rdr["tmpf"] = s_hm
        pb.prev = pb.step("scalar", lambda e: e.nop(), deps=[pb.prev])
        for c in range(NCD):
            for b in range(1, NB):
                s_prev = slice(c * T + (b - 1) * TB, c * T + b * TB)
                s_cur = slice(c * T + b * TB, c * T + (b + 1) * TB)
                pb.step("vector", lambda e, s_prev=s_prev, s_cur=s_cur: e.tensor_tensor(
                    tmpf[:, :TB], lam[:, s_cur], hbuf[:, s_prev], ALU.mult))
                pb.step("vector", lambda e, s_cur=s_cur: e.tensor_tensor(
                    hbuf[:, s_cur], hbuf[:, s_cur], tmpf[:, :TB], ALU.add))
        if dbg:
            pb.dma("sync", [(dbgL_d[c * 128:(c + 1) * 128, :], lam[:, c * T:(c + 1) * T])
                            for c in range(NCD)])
            pb.dma("sync", [(dbgH_d[c * 128:(c + 1) * 128, :], hbuf[:, c * T:(c + 1) * T])
                            for c in range(NCD)])
        entry_h = pb.prev
        rl = {}
        for g in range(4):
            sl = slice(g * 512, (g + 1) * 512)
            for c in range(NCD):
                cs = slice(c * T + sl.start, c * T + sl.stop)
                par = c % 2
                s_mm1 = pb.step("tensor", lambda e, c=c, cs=cs: e.matmul(
                    ps[0][:1, :], onesc[:, :], hbuf[:, cs], start=(c == 0), stop=(c == NCD - 1)),
                    deps=[entry_h, rl.get(("ps", 0))])
                s_sq = pb.step("vector", lambda e, cs=cs, par=par: e.tensor_tensor(
                    tmpq[par][:, :], hbuf[:, cs], hbuf[:, cs], ALU.mult),
                    deps=[entry_h, rl.get(("sq", par))])
                s_mm2 = pb.step("tensor", lambda e, c=c, par=par: e.matmul(
                    ps[1][:1, :], onesc[:, :], tmpq[par][:, :], start=(c == 0), stop=(c == NCD - 1)),
                    deps=[s_sq, rl.get(("ps", 1))])
                rl[("sq", par)] = s_mm2
            r1 = pb.step("vector", lambda e: e.tensor_scalar(rowa[:1, :], ps[0][:1, :], 1.0 / D, None, ALU.mult),
                         deps=[s_mm1, rl.get("rows")])
            rl[("ps", 0)] = r1
            r2 = pb.step("vector", lambda e: e.tensor_scalar(rowb[:1, :], ps[1][:1, :], 1.0 / D, None, ALU.mult),
                         deps=[s_mm2, r1, rl.get("bA")])
            rl[("ps", 1)] = r2
            r3 = pb.step("vector", lambda e: e.tensor_tensor(rowc[:1, :], rowa[:1, :], rowa[:1, :], ALU.mult),
                         deps=[r2, rl.get("bB")])
            r4 = pb.step("vector", lambda e: e.tensor_tensor(rowb[:1, :], rowb[:1, :], rowc[:1, :], ALU.subtract), deps=[r3])
            r5 = pb.step("scalar", lambda e: e.activation(rowb[:1, :], rowb[:1, :], AF.Ln, bias=cns[:1, :1]), deps=[r4])
            r6 = pb.step("scalar", lambda e: e.activation(rowb[:1, :], rowb[:1, :], AF.Exp, scale=-0.5), deps=[r5])
            r7 = pb.step("vector", lambda e: e.scalar_tensor_tensor(rowc[:1, :], rowa[:1, :], -1.0, rowb[:1, :], ALU.mult, ALU.mult), deps=[r6])
            rl["rows"] = r7
            last_bA = None
            for c in range(NCD):
                cs = slice(c * T + sl.start, c * T + sl.stop)
                par = c % 2
                buf = wa if par == 0 else wb
                ldg = pb.dma("gpsimd", [(buf[:, 0:1024].rearrange("p (cc o) -> p cc o", o=128),
                                         colslice_ap(wgate_d, c * 128, 128, 8))],
                             deps=[rdr.get(("wbuf", par))])
                bA = pb.step("tensor", lambda e, c=c: e.matmul(
                    ps[2][:, :], grows[:1, c * 128:(c + 1) * 128], rowb[:1, :], start=True, stop=True),
                    deps=[r7, rl.get(("ps", 2))])
                bB = pb.step("tensor", lambda e, c=c: e.matmul(
                    ps[3][:, :], grows[:1, c * 128:(c + 1) * 128], rowc[:1, :], start=True, stop=True),
                    deps=[r7, rl.get(("ps", 3))])
                rl["bA"] = bA
                rl["bB"] = bB
                last_bA = bA

                def mmg(e, sl=sl, buf=buf, par=par):
                    r = None
                    for cc in range(NCD):
                        r = e.matmul(ps[4 + par][:, :], buf[:, cc * 128:(cc + 1) * 128],
                                     xln[:, cc * T + sl.start:cc * T + sl.stop],
                                     start=(cc == 0), stop=(cc == NCD - 1))
                    return r
                s_g = pb.step("tensor", mmg, deps=[ldg, rl.get(("psg", par))])
                rdr[("wbuf", par)] = s_g
                sg_buf = scr2 if par == 0 else sqln
                s_sil = pb.step("scalar", lambda e, c=c, par=par, sg_buf=sg_buf: e.activation(
                    sg_buf[:, :], ps[4 + par][:, :], AF.Silu, bias=bcols[:, 16 + c:17 + c]),
                    deps=[s_g, rl.get(("sg", par))])
                rl[("psg", par)] = s_sil
                a1 = pb.step("vector", lambda e, cs=cs, par=par: e.tensor_tensor(
                    tmpa[par][:, :], hbuf[:, cs], ps[2][:, :], ALU.mult),
                    deps=[bA, rl.get(("ta", par))])
                rl[("ps", 2)] = a1
                a2 = pb.step("vector", lambda e, c=c, par=par: e.scalar_tensor_tensor(
                    tmpa[par][:, :], tmpa[par][:, :], bcols[:, 40 + c:41 + c], ps[3][:, :], ALU.add, ALU.add),
                    deps=[bB, a1])
                rl[("ps", 3)] = a2
                s_z = pb.step("vector", lambda e, cs=cs, par=par, sg_buf=sg_buf: e.tensor_tensor(
                    hbuf[:, cs], tmpa[par][:, :], sg_buf[:, :], ALU.mult),
                    deps=[a2, s_sil])
                rl[("ta", par)] = s_z
                rl[("sg", par)] = s_z
        pb.prev = pb.step("scalar", lambda e: e.nop(), deps=[s_z])

        zdone = pb.prev
        if dbg:
            pb.dma("sync", [(dbgZ_d[c * 128:(c + 1) * 128, :], hbuf[:, c * T:(c + 1) * T])
                            for c in range(NCD)])
            pb.dma("sync", [(dbgX_d[c * 128:(c + 1) * 128, :], x[:, c * T:(c + 1) * T])
                            for c in range(NCD)])
            zdone = pb.prev
        for dc in range(NCD):
            par = dc % 2
            buf = wa if par == 0 else wb
            ldo = pb.dma("gpsimd", [(buf[:, 0:1024].rearrange("p (c o) -> p c o", o=128),
                                     colslice_ap(wout_d, dc * 128, 128, 8))],
                         deps=[rdr.get(("wbuf", par)), zdone])
            for g in range(4):
                sl = slice(g * 512, (g + 1) * 512)
                bank = 2 * par + (g % 2)

                def mmo(e, sl=sl, buf=buf, bank=bank):
                    r = None
                    for c in range(NCD):
                        r = e.matmul(ps[bank][:, :], buf[:, c * 128:(c + 1) * 128],
                                     hbuf[:, c * T + sl.start:c * T + sl.stop],
                                     start=(c == 0), stop=(c == NCD - 1))
                    return r
                s_mm = pb.step("tensor", mmo, deps=[ldo, rdr.get(("ps", bank)), zdone])
                rdr[("wbuf", par)] = s_mm
                cs = slice(dc * T + sl.start, dc * T + sl.stop)
                s_acc = pb.step("vector", lambda e, cs=cs, dc=dc, bank=bank: e.scalar_tensor_tensor(
                    x[:, cs], ps[bank][:, :], bcols[:, 24 + dc:25 + dc], x[:, cs], ALU.add, ALU.add),
                    deps=[s_mm])
                rdr[("ps", bank)] = s_acc
        pb.prev = pb.step("scalar", lambda e: e.nop(), deps=[pb.prev])

        if dbg:
            _d = pb.dma("sync", [(dbgA_d[c * 128:(c + 1) * 128, :], x[:, c * T:(c + 1) * T])
                                 for c in range(NCD)])
            pb.prev = _d
        emit_ln(x)
        ln4_done = pb.prev
        lrl = {}
        for t in range(16):
            tsl = slice(t * 128, (t + 1) * 128)
            par = t % 2
            lbank = par
            o0 = 32 * par

            def mmlg(e, tsl=tsl, lbank=lbank):
                r = None
                for c in range(NCD):
                    r = e.matmul(ps[lbank][:, :8], xln[:, c * T + tsl.start:c * T + tsl.stop],
                                 wgw[:, c * 8:(c + 1) * 8], start=(c == 0), stop=(c == NCD - 1))
                return r
            s_mm = pb.step("tensor", mmlg, deps=[ln4_done, lrl.get(("ps", lbank))])
            F = o0
            s1 = pb.step("vector", lambda e, lbank=lbank, F=F: e.tensor_copy(tmpf[:, F:F + 8], ps[lbank][:, :8]),
                         deps=[s_mm, lrl.get(("tf", par))])
            lrl[("ps", lbank)] = s1
            s2 = pb.step("vector", lambda e, F=F: e.reduce_max(tmpf[:, F + 8:F + 9], tmpf[:, F:F + 8], axis=AX.X), deps=[s1])
            s3 = pb.step("vector", lambda e, F=F: e.tensor_scalar(tmpf[:, F + 16:F + 24], tmpf[:, F:F + 8], tmpf[:, F + 8:F + 9], None, ALU.is_ge), deps=[s2])
            s4 = pb.step("vector", lambda e, F=F: e.scalar_tensor_tensor(tmpf[:, F + 24:F + 32], tmpf[:, F + 16:F + 24], -1e30, tmpf[:, F:F + 8], ALU.mult, ALU.add), deps=[s3])
            s5 = pb.step("vector", lambda e, F=F: e.reduce_max(tmpf[:, F + 10:F + 11], tmpf[:, F + 24:F + 32], axis=AX.X), deps=[s4])
            s6 = pb.step("vector", lambda e, F=F: e.tensor_scalar(tmpf[:, F + 9:F + 10], tmpf[:, F + 8:F + 9], -1.0, None, ALU.mult), deps=[s5])
            s7 = pb.step("scalar", lambda e, F=F: e.activation(tmpf[:, F + 11:F + 12], tmpf[:, F + 10:F + 11], AF.Exp, bias=tmpf[:, F + 9:F + 10]), deps=[s6])
            s8 = pb.step("vector", lambda e, F=F: e.tensor_scalar(tmpf[:, F + 11:F + 12], tmpf[:, F + 11:F + 12], 1.0, None, ALU.add), deps=[s7])
            s9 = pb.step("vector", lambda e, F=F: e.reciprocal(tmpf[:, F + 12:F + 13], tmpf[:, F + 11:F + 12]), deps=[s8])
            s10 = pb.step("scalar", lambda e, F=F: e.activation(tmpf[:, F + 24:F + 32], tmpf[:, F:F + 8], AF.Exp, bias=tmpf[:, F + 9:F + 10]), deps=[s9])
            s11 = pb.step("vector", lambda e, F=F: e.tensor_scalar(tmpf[:, F + 16:F + 24], tmpf[:, F:F + 8], tmpf[:, F + 10:F + 11], None, ALU.is_ge), deps=[s10])
            s12 = pb.step("vector", lambda e, F=F: e.tensor_tensor(tmpf[:, F + 24:F + 32], tmpf[:, F + 16:F + 24], tmpf[:, F + 24:F + 32], ALU.mult), deps=[s11])
            s13 = pb.step("vector", lambda e, t=t, F=F: e.tensor_scalar(cmbt[:, t * 8:(t + 1) * 8], tmpf[:, F + 24:F + 32], tmpf[:, F + 12:F + 13], None, ALU.mult), deps=[s12])
            lrl[("tf", par)] = s13
        pb.prev = pb.step("scalar", lambda e: e.nop(), deps=[s13])
        pb.step("tensor", lambda e: e.transpose(psT[:, :128], cmbt[:, :], idf[:, :]))
        pb.step("vector", lambda e: e.tensor_copy(cmbu[:, :], psT[:, :128]))
        pb.dma("sync", [(cmb_dram.ap(), cmbu[:, :])])
        pb.dma("gpsimd", [(cfm8[:, :].rearrange("e (t j) -> e t j", j=128),
                          cmb_dram.ap().rearrange("(t e) j -> e t j", e=8))])
        if dbg:
            pb.dma("sync", [(cmb_d[:, :], cfm8[:, :])])
            pb.dma("sync", [(cmbt_d[:, :], cmbt[:, :])])
        cmb_ready = pb.prev
        sAm = [scr2, scr3]
        sBm = [sqln, scr4]
        prev_dn = cmb_ready
        prev_ue_last = None
        for ex in range(NE):
            cb_evs = []
            for g in range(4):
                sl = slice(g * 512, (g + 1) * 512)
                s_cb = pb.step("tensor", lambda e, ex=ex, sl=sl: e.matmul(
                    ps[4][:, :], sel[:, ex * 128:(ex + 1) * 128], cfm8[:, sl], start=True, stop=True),
                    deps=[cmb_ready, rdr.get(("ps", 4)), prev_ue_last])
                s_ev = pb.step("vector", lambda e, sl=sl: e.tensor_copy(cb[:, sl], ps[4][:, :]),
                               deps=[s_cb, prev_ue_last])
                rdr[("ps", 4)] = s_ev
                cb_evs.append(s_ev)
            cb_done = cb_evs[-1]
            for t2 in range(NT2):
                base = t2 * TT2
                hms = []
                for oc in range(NH):
                    par = oc % 2
                    buf = wa if par == 0 else wb
                    ldgu = pb.dma("gpsimd", [
                        (buf[:, 0:1024].rearrange("p (c o) -> p c o", o=128),
                         colslice_ap(wgT_d[ex], oc * 128, 128, 8)),
                        (buf[:, 1024:2048].rearrange("p (c o) -> p c o", o=128),
                         colslice_ap(wuT_d[ex], oc * 128, 128, 8))],
                        deps=[rdr.get(("wbuf", par))])
                    for hf in range(2):
                        sl = slice(base + hf * 512, base + hf * 512 + 512)
                        b1, b2 = 2 * par, 2 * par + 1

                        def mmg2(e, sl=sl, buf=buf, b1=b1):
                            r = None
                            for c in range(NCD):
                                r = e.matmul(ps[b1][:, :], buf[:, c * 128:(c + 1) * 128],
                                             xln[:, c * T + sl.start:c * T + sl.stop],
                                             start=(c == 0), stop=(c == NCD - 1))
                            return r
                        s_g = pb.step("tensor", mmg2, deps=[ldgu, rdr.get(("ps", b1)), cmb_ready])
                        rdr[("wbuf", par)] = s_g

                        def mmu2(e, sl=sl, buf=buf, b2=b2):
                            r = None
                            for c in range(NCD):
                                r = e.matmul(ps[b2][:, :], buf[:, 1024 + c * 128:1024 + (c + 1) * 128],
                                             xln[:, c * T + sl.start:c * T + sl.stop],
                                             start=(c == 0), stop=(c == NCD - 1))
                            return r
                        s_u = pb.step("tensor", mmu2, deps=[ldgu, rdr.get(("ps", b2)), cmb_ready])
                        rdr[("wbuf", par)] = s_u
                        s_si = pb.step("scalar", lambda e, par=par, b1=b1: e.activation(
                            sAm[par][:, :], ps[b1][:, :], AF.Silu),
                            deps=[s_g, rdr.get(("sA", par))])
                        rdr[("ps", b1)] = s_si
                        s_ue = pb.step("vector", lambda e, sl=sl, par=par, b2=b2: e.tensor_tensor(
                            sBm[par][:, :], ps[b2][:, :], cb[:, sl], ALU.mult),
                            deps=[s_u, cb_done, rdr.get(("sB", par))])
                        rdr[("ps", b2)] = s_ue
                        prev_ue_last = s_ue
                        s_hm = pb.step("vector", lambda e, oc=oc, hf=hf, par=par: e.tensor_tensor(
                            hmoe[:, oc * TT2 + hf * 512:oc * TT2 + hf * 512 + 512],
                            sAm[par][:, :], sBm[par][:, :], ALU.mult),
                            deps=[s_si, s_ue, prev_dn])
                        rdr[("sA", par)] = s_hm
                        rdr[("sB", par)] = s_hm
                        hms.append(s_hm)
                hm_all = hms[-1]
                for dc in range(NCD):
                    par = dc % 2
                    buf = wa if par == 0 else wb
                    ldd = pb.dma("gpsimd", [(buf[:, :].rearrange("p (c o) -> p c o", o=128),
                                             colslice_ap(wdT_d[ex], dc * 128, 128, 32))],
                                 deps=[rdr.get(("wbuf", par))])
                    for hf in range(2):
                        pdn = ps[5] if hf == 0 else psX
                        bkey = ("dn", hf)

                        def mmd(e, hf=hf, buf=buf, pdn=pdn):
                            r = None
                            for ic in range(NH):
                                r = e.matmul(pdn[:, :], buf[:, ic * 128:(ic + 1) * 128],
                                             hmoe[:, ic * TT2 + hf * 512:ic * TT2 + hf * 512 + 512],
                                             start=(ic == 0), stop=(ic == NH - 1))
                            return r
                        s_d = pb.step("tensor", mmd, deps=[ldd, rdr.get(bkey), hm_all])
                        rdr[("wbuf", par)] = s_d
                        cs = slice(dc * T + base + hf * 512, dc * T + base + hf * 512 + 512)
                        s_a = pb.step("vector", lambda e, cs=cs, pdn=pdn: e.tensor_tensor(
                            x[:, cs], x[:, cs], pdn[:, :], ALU.add), deps=[s_d])
                        rdr[bkey] = s_a
                        prev_dn = s_a
        pb.prev = pb.step("scalar", lambda e: e.nop(), deps=[pb.prev, prev_dn])

        stv = pb.counts["sync"] + 16 * NCD
        pb.dma("sync", [(outT_d[c * 128:(c + 1) * 128, :], x[:, c * T:(c + 1) * T]) for c in range(NCD)])

        def fin(e):
            e.wait_ge(sems["sync"], stv)
            return e.nop()
        pb.step("sync", fin)

        pb.emit(block, sems)
    return nc


def prep_inputs_B(inputs, yT_list, core):
    asc = np.ascontiguousarray

    def colpack(vecs):
        out = np.zeros((128, 48), np.float32)
        for j, v in enumerate(vecs):
            out[:, j * 8:(j + 1) * 8] = v.reshape(8, 128).T
        return out
    xT = np.concatenate([yT_list[b][:, core * TB:(core + 1) * TB] for b in range(NB)], axis=1)
    gwc = np.zeros((128, 64), np.float32)
    gw = inputs["L1_ffn_gate_w"]
    for c in range(8):
        gwc[:, c * 8:(c + 1) * 8] = gw[:, c * 128:(c + 1) * 128].T
    return {
        "xT": asc(xT),
        "WinT": asc(inputs["L1_attn_W_in"].T), "WlamT": asc(inputs["L1_attn_W_lam"].T),
        "WgateT": asc(inputs["L1_attn_W_gate"].T), "WoutT": asc(inputs["L1_attn_W_out"].T),
        "bcols": colpack([inputs["L1_attn_b_in"], inputs["L1_attn_b_lam"],
                          inputs["L1_attn_b_gate"], inputs["L1_attn_b_out"],
                          inputs["L1_attn_gamma"], inputs["L1_attn_beta"]]),
        "gwc": asc(gwc),
        "grows": inputs["L1_attn_gamma"].reshape(1, D).astype(np.float32),
        "wgT": asc(np.transpose(inputs["L1_ffn_wg"], (0, 2, 1))),
        "wuT": asc(np.transpose(inputs["L1_ffn_wu"], (0, 2, 1))),
        "wdT": asc(np.transpose(inputs["L1_ffn_wd"], (0, 2, 1))),
    }


from concourse.bass_utils import run_bass_kernel_spmd

_CACHE = {}


def kernel(**inputs):
    if "A" not in _CACHE:
        _CACHE["A"] = build_launchA()
        _CACHE["B"] = build_launchB()
    ncA, ncB = _CACHE["A"], _CACHE["B"]

    in_maps_A = [prep_inputs_A(inputs, b) for b in range(8)]
    resA = run_bass_kernel_spmd(ncA, in_maps_A, core_ids=list(range(8)))
    yT_list = [resA.results[b]["yT"] for b in range(8)]

    in_maps_B = [prep_inputs_B(inputs, yT_list, c) for c in range(8)]
    resB = run_bass_kernel_spmd(ncB, in_maps_B, core_ids=list(range(8)))

    out = np.empty((8, 2048, 1024), np.float32)
    for c in range(8):
        out[:, c * TB:(c + 1) * TB, :] = resB.results[c]["outT"].T.reshape(NB, TB, D)
    return out


# version 09 (final)
# speedup vs baseline: 1.2935x; 1.0480x over previous
import sys
sys.path.insert(0, "/opt/trn_rl_repo")
import numpy as np
import ml_dtypes
import concourse.bass as bass
import concourse.mybir as mybir

F32 = mybir.dt.float32
BF16 = mybir.dt.bfloat16
AF = mybir.ActivationFunctionType
ALU = mybir.AluOpType
AX = mybir.AxisListType

ENGS = ("tensor", "vector", "scalar", "gpsimd", "sync")


DMA_CHANS = ("gA", "gB", "gC", "syA", "syB", "syC")


class Step:
    __slots__ = ("eng", "fn", "deps", "idx", "val", "ndma", "sem", "_sem_holder")

    def __init__(self, eng, fn, deps, idx, ndma, sem):
        self.eng, self.fn, self.deps, self.idx, self.ndma = eng, fn, deps, idx, ndma
        self.sem = sem
        self.val = None


class PB:
    def __init__(self, nc):
        self.nc = nc
        self.steps = []
        self.counts = {e: 0 for e in ENGS + DMA_CHANS}
        self.prev = None

    def step(self, eng, fn, deps=None, ndma=0, chan=None):
        if deps is None:
            deps = [self.prev] if self.prev is not None else []
        deps = [d for d in deps if d is not None]
        sem = chan if ndma else eng
        assert sem in self.counts, sem
        s = Step(eng, fn, deps, len(self.steps), ndma, sem)
        self.counts[sem] += (16 * ndma) if ndma else 1
        s.val = self.counts[sem]
        self.steps.append(s)
        self.prev = s
        return s

    def dma(self, queue, pairs, deps=None, chan="gC"):
        holder = {}

        def fn(e):
            sem = holder["sem"]
            for o, i in pairs:
                e.dma_start(out=o, in_=i).then_inc(sem, 16)
            return None
        s = self.step(queue, fn, deps=deps, ndma=len(pairs), chan=chan)
        s._sem_holder = holder
        return s

    def emit(self, block, sems):
        per_eng = {e: [s for s in self.steps if s.eng == e] for e in ENGS}

        def make_body(eng):
            def body(e):
                watermark = {}
                for s in per_eng[eng]:
                    best = {}
                    for d in s.deps:
                        if d.sem not in best or d.val > best[d.sem].val:
                            best[d.sem] = d
                    for d in best.values():
                        if d.sem == "tensor" and eng == "tensor":
                            continue
                        if d.val <= watermark.get(d.sem, -1):
                            continue
                        e.wait_ge(sems[d.sem], d.val)
                        watermark[d.sem] = d.val
                    if s.ndma:
                        s._sem_holder["sem"] = sems[s.sem]
                        s.fn(e)
                    else:
                        last = s.fn(e)
                        assert last is not None, f"step {s.idx} on {eng} returned None"
                        last.then_inc(sems[eng], 1)
            return body

        for eng in ENGS:
            if per_eng[eng]:
                getattr(block, eng)(make_body(eng))


D, T = 1024, 2048
NCD = 8
H, DK = 8, 128
CH = 64
NCHUNK = T // CH
TT = 512
NTT = T // TT
DH = 4096
NH = DH // 128
LN_EPS = 1e-5


def colslice_ap(w_d, o0, width, nchunks):
    return w_d.rearrange("(c p) o -> p c o", p=128)[:, :nchunks, o0:o0 + width]


class SMap:

    def __init__(self, nc):
        self.nc = nc
        self.off = (nc.sbuf_base + 31) & ~31
        self.top = nc.sbuf_top
        self.offsets = {}

    def at(self, name, shape, dtype, offset):
        return self.nc.alloc_sbuf_tensor_at(name, shape, dtype, offset=offset)

    def seq(self, name, shape, dtype):
        per = int(np.prod(shape[1:])) * mybir.dt.size(dtype)
        h = self.at(name, shape, dtype, self.off)
        self.offsets[name] = self.off
        self.off = (self.off + per + 31) & ~31
        assert self.off <= self.top, f"SBUF overflow at {name}: {self.off} > {self.top}"
        return h


def build_launchA():
    nc = bass.Bass("TRN2", target_bir_lowering=False)

    xT_d = nc.dram_tensor("xT", [D, T], F32, kind="ExternalInput").ap()
    wq_d = nc.dram_tensor("Wq", [D, D], F32, kind="ExternalInput").ap()
    wk_d = nc.dram_tensor("Wk", [D, D], F32, kind="ExternalInput").ap()
    wv_d = nc.dram_tensor("Wv", [D, D], F32, kind="ExternalInput").ap()
    wg_d = nc.dram_tensor("Wg", [D, D], F32, kind="ExternalInput").ap()
    wo_d = nc.dram_tensor("Wo", [D, D], F32, kind="ExternalInput").ap()
    w1_d = nc.dram_tensor("w1T", [D, DH], F32, kind="ExternalInput").ap()
    w2_d = nc.dram_tensor("w2T", [D, DH], F32, kind="ExternalInput").ap()
    h1_d = nc.dram_tensor("h1T", [DH, DH], F32, kind="ExternalInput").ap()
    h2_d = nc.dram_tensor("h2T", [DH, DH], F32, kind="ExternalInput").ap()
    w3_d = nc.dram_tensor("w3T", [DH, D], F32, kind="ExternalInput").ap()
    s1_d = nc.dram_tensor("s1", [128, NH], F32, kind="ExternalInput").ap()
    s2_d = nc.dram_tensor("s2", [128, NH], F32, kind="ExternalInput").ap()
    sh1_d = nc.dram_tensor("sh1", [128, NH], F32, kind="ExternalInput").ap()
    sh2_d = nc.dram_tensor("sh2", [128, NH], F32, kind="ExternalInput").ap()
    s3_d = nc.dram_tensor("s3", [128, NCD], F32, kind="ExternalInput").ap()
    yT_d = nc.dram_tensor("yT", [D, T], F32, kind="ExternalOutput").ap()

    w1b_d = nc.dram_tensor("w1b", [NH, 128, 1024], BF16)
    w2b_d = nc.dram_tensor("w2b", [NH, 128, 1024], BF16)
    h1b_d = nc.dram_tensor("h1b", [NH, 128, 4096], BF16)
    h2b_d = nc.dram_tensor("h2b", [NH, 128, 4096], BF16)
    w3b_d = nc.dram_tensor("w3b", [NCD, 128, 4096], BF16)
    tri_d = nc.inline_tensor(np.triu(np.ones((CH, CH), dtype=np.float32)), name="tric")
    id_d = nc.inline_tensor(np.eye(128, dtype=ml_dtypes.bfloat16), name="idc")

    sm = SMap(nc)
    x = sm.seq("x", [128, NCD * T], F32)
    xln = sm.seq("xln", [128, NCD * T], BF16)
    ofm = sm.seq("ofm", [128, H * T], BF16)
    wa = sm.seq("wa", [128, 4096], BF16)
    rowa = sm.seq("rowa", [1, 512], F32)
    rowb = sm.seq("rowb", [1, 512], F32)
    rowc = sm.seq("rowc", [1, 512], F32)
    tmpf = sm.seq("tmpf", [128, 512], F32)
    tmpf2 = sm.seq("tmpf2", [128, 512], F32)
    scr2 = sm.seq("scr2", [128, 512], BF16)
    sqln = sm.seq("sqln", [128, 512], BF16)
    sq2 = sm.seq("sq2", [128, 512], BF16)
    sq3 = sm.seq("sq3", [128, 512], BF16)
    ones1 = sm.seq("ones1", [1, 128], F32)
    onesc = sm.seq("onesc", [128, 1], BF16)
    tri = sm.seq("tri", [CH, CH], F32)
    idm = sm.seq("idm", [128, 128], BF16)
    scol = sm.seq("scol", [128, 2 * NH + NCD], F32)
    scol2 = sm.seq("scol2", [128, 2 * NH], F32)
    cns = sm.seq("cns", [1, 8], F32)
    U = sm.off
    vtm = sm.at("vtm", [64, NCHUNK * 128], BF16, U + 0)
    qh = sm.at("qh", [128, T], BF16, U + 8192)
    kh = sm.at("kh", [128, T], BF16, U + 12288)
    gh = sm.at("gh", [128, T], F32, U + 16384)
    ke = sm.at("ke", [128, T], BF16, U + 16384)
    ah = sm.at("ah", [128, T], F32, U + 24576)
    qs = sm.at("qs", [128, T], BF16, U + 32768)
    ks = sm.at("ks", [128, T], BF16, U + 36864)
    sst = sm.at("sst", [128, 128], BF16, U + 40960)
    pm = sm.at("pm", [64, 2 * CH], BF16, U + 41472)
    rahg = sm.at("rahg", [128, 512], F32, U + 41984)
    whead = sm.at("whead", [128, 4096], BF16, U + 48032)
    ketm = sm.at("ketm", [64, NCHUNK * 128], BF16, U + 48032)
    wb = sm.at("wb", [128, 4096], BF16, U + 48032)
    GLA_END = U + 48032 + 8192
    hb = sm.at("hb", [128, NH * TT], BF16, U)
    g1 = sm.at("g1", [128, NH * TT], BF16, sm.offsets["ofm"])
    assert GLA_END <= sm.top and U + NH * TT * 2 <= sm.top, (U, GLA_END, sm.top)

    from contextlib import ExitStack
    with ExitStack() as es:
        E = es.enter_context
        ps = [E(nc.psum_tensor(f"ps{i}", [128, 512], F32)) for i in range(6)]
        psT = E(nc.psum_tensor("psT", [128, 1024], BF16))
        psP = E(nc.psum_tensor("psP", [64, 512], F32))
        sems = {e: E(nc.semaphore("sem_" + e)) for e in ENGS + DMA_CHANS}
        block = E(nc.Block())

        pb = PB(nc)

        ld0 = pb.dma("gpsimd", [(x[:, c * T:(c + 1) * T], xT_d[c * 128:(c + 1) * 128, :])
                                for c in range(NCD)], deps=[])
        ldc = pb.dma("sync", [(tri[:, :], tri_d.ap()), (idm[:, :], id_d.ap()),
                              (scol[:, 0:NH], s1_d[:, :]), (scol[:, NH:2 * NH], s2_d[:, :]),
                              (scol[:, 2 * NH:2 * NH + NCD], s3_d[:, :]),
                              (scol2[:, 0:NH], sh1_d[:, :]), (scol2[:, NH:2 * NH], sh2_d[:, :])],
                     deps=[], chan="syC")

        def ones_fn(e):
            e.memset(ones1[:, :], 1.0)
            e.memset(cns[:1, :], LN_EPS)
            return e.memset(onesc[:, :], 1.0)
        pb.step("vector", ones_fn, deps=[])
        pb.step("scalar", lambda e: e.nop(), deps=[ld0, ldc, pb.prev])

        tmpq = [sq2, sq3]
        tmpa = [tmpf, tmpf2]
        def emit_ln(src_t):
            entry = pb.prev
            rl = {}
            bcast = {}
            for g in range(4):
                sl = slice(g * 512, (g + 1) * 512)
                for c in range(NCD):
                    cs = slice(c * T + sl.start, c * T + sl.stop)
                    par = c % 2
                    cbuf = scr2 if par == 0 else sqln
                    s_cast = pb.step("vector", lambda e, cs=cs, cbuf=cbuf: e.tensor_copy(cbuf[:, :512], src_t[:, cs]),
                                     deps=[rl.get(("cb", par)), entry])
                    s_mm1 = pb.step("tensor", lambda e, c=c, cbuf=cbuf: e.matmul(
                        ps[0][:1, :], onesc[:, :], cbuf[:, :512], start=(c == 0), stop=(c == NCD - 1)),
                        deps=[s_cast, rl.get(("ps", 0))])
                    s_sq = pb.step("vector", lambda e, cbuf=cbuf, par=par: e.tensor_tensor(
                        tmpq[par][:, :], cbuf[:, :512], cbuf[:, :512], ALU.mult),
                        deps=[s_cast, rl.get(("sq", par))])
                    s_mm2 = pb.step("tensor", lambda e, c=c, par=par: e.matmul(
                        ps[1][:1, :], onesc[:, :], tmpq[par][:, :], start=(c == 0), stop=(c == NCD - 1)),
                        deps=[s_sq, rl.get(("ps", 1))])
                    rl[("cb", par)] = s_mm1
                    rl[("sq", par)] = s_mm2
                r1 = pb.step("vector", lambda e: e.tensor_scalar(rowa[:1, :], ps[0][:1, :], 1.0 / D, None, ALU.mult),
                             deps=[s_mm1, rl.get("rows")])
                rl[("ps", 0)] = r1
                r2 = pb.step("vector", lambda e: e.tensor_scalar(rowb[:1, :], ps[1][:1, :], 1.0 / D, None, ALU.mult),
                             deps=[s_mm2, r1, rl.get("bA")])
                rl[("ps", 1)] = r2
                r3 = pb.step("vector", lambda e: e.tensor_tensor(rowc[:1, :], rowa[:1, :], rowa[:1, :], ALU.mult),
                             deps=[r2, rl.get("bB")])
                r4 = pb.step("vector", lambda e: e.tensor_tensor(rowb[:1, :], rowb[:1, :], rowc[:1, :], ALU.subtract), deps=[r3])
                r5 = pb.step("scalar", lambda e: e.activation(rowb[:1, :], rowb[:1, :], AF.Ln, bias=cns[:1, :1]), deps=[r4])
                r6 = pb.step("scalar", lambda e: e.activation(rowb[:1, :], rowb[:1, :], AF.Exp, scale=-0.5), deps=[r5])
                r7 = pb.step("vector", lambda e: e.scalar_tensor_tensor(rowc[:1, :], rowa[:1, :], -1.0, rowb[:1, :], ALU.mult, ALU.mult), deps=[r6])
                rl["rows"] = r7
                bA = pb.step("tensor", lambda e: e.matmul(ps[2][:, :], ones1[:, :], rowb[:1, :], start=True, stop=True),
                             deps=[r6, r7, rl.get(("ps", 2))])
                bB = pb.step("tensor", lambda e: e.matmul(ps[3][:, :], ones1[:, :], rowc[:1, :], start=True, stop=True),
                             deps=[r7, rl.get(("ps", 3))])
                rl["bA"] = bA
                rl["bB"] = bB
                ap_last = None
                for c in range(NCD):
                    cs = slice(c * T + sl.start, c * T + sl.stop)
                    par = c % 2
                    a1 = pb.step("vector", lambda e, cs=cs, par=par: e.tensor_tensor(
                        tmpa[par][:, :], src_t[:, cs], ps[2][:, :], ALU.mult),
                        deps=[bA, rl.get(("ta", par))])
                    a2 = pb.step("vector", lambda e, cs=cs, par=par: e.tensor_tensor(
                        xln[:, cs], tmpa[par][:, :], ps[3][:, :], ALU.add),
                        deps=[bB, a1])
                    rl[("ta", par)] = a2
                    ap_last = a2
                rl[("ps", 2)] = ap_last
                rl[("ps", 3)] = ap_last
            pb.prev = pb.step("scalar", lambda e: e.nop(), deps=[ap_last])

        emit_ln(x)

        for h in range(H):
            pb.dma("gpsimd", [
                (whead[:, 0:1024].rearrange("p (c o) -> p c o", o=128), colslice_ap(wq_d, h * 128, 128, 8)),
                (whead[:, 1024:2048].rearrange("p (c o) -> p c o", o=128), colslice_ap(wk_d, h * 128, 128, 8)),
                (whead[:, 2048:3072].rearrange("p (c o) -> p c o", o=128), colslice_ap(wv_d, h * 128, 128, 8)),
                (whead[:, 3072:4096].rearrange("p (c o) -> p c o", o=128), colslice_ap(wg_d, h * 128, 128, 8))])
            lw = pb.prev
            grdr = {}
            for off, dst, mode in ((0, qh, "c"), (1024, kh, "c"), (3072, gh, "s")):
                for g in range(4):
                    sl = slice(g * 512, (g + 1) * 512)
                    bank = g % 2

                    def mmfn(e, off=off, sl=sl, bank=bank):
                        r = None
                        for c in range(NCD):
                            r = e.matmul(ps[bank][:, :], whead[:, off + c * 128:off + (c + 1) * 128],
                                         xln[:, c * T + sl.start:c * T + sl.stop],
                                         start=(c == 0), stop=(c == NCD - 1))
                        return r
                    s_mm = pb.step("tensor", mmfn, deps=[lw, grdr.get(("ps", bank))])
                    if mode == "c":
                        s_ev = pb.step("vector", lambda e, dst=dst, sl=sl, bank=bank: e.tensor_copy(dst[:, sl], ps[bank][:, :]),
                                       deps=[s_mm])
                    else:
                        s_ev = pb.step("scalar", lambda e, dst=dst, sl=sl, bank=bank: e.activation(dst[:, sl], ps[bank][:, :], AF.Sigmoid),
                                       deps=[s_mm])
                    grdr[("ps", bank)] = s_ev
            for cc in range(NCHUNK):
                tsl = slice(cc * CH, (cc + 1) * CH)
                bank = 2 + cc % 2

                def vfn(e, tsl=tsl, bank=bank):
                    r = None
                    for c in range(NCD):
                        r = e.matmul(ps[bank][:64, :128], xln[:, c * T + tsl.start:c * T + tsl.stop],
                                     whead[:, 2048 + c * 128:2048 + (c + 1) * 128],
                                     start=(c == 0), stop=(c == NCD - 1))
                    return r
                s_mm = pb.step("tensor", vfn, deps=[lw, grdr.get(("ps", bank))])
                s_ev = pb.step("vector", lambda e, cc=cc, bank=bank: e.tensor_copy(vtm[:, cc * 128:(cc + 1) * 128], ps[bank][:64, :128]),
                               deps=[s_mm])
                grdr[("ps", bank)] = s_ev
            pb.prev = pb.step("scalar", lambda e: e.nop(), deps=[pb.prev, s_ev])
            pb.step("scalar", lambda e: e.activation(gh[:, :], gh[:, :], AF.Ln))
            for cc in range(NCHUNK):
                sl = slice(cc * CH, (cc + 1) * CH)
                pb.step("vector", lambda e, sl=sl: e.tensor_tensor_scan(gh[:, sl], gh[:, sl], gh[:, sl], 0.0, ALU.add, ALU.bypass))
            pb.step("scalar", lambda e: e.activation(ah[:, :], gh[:, :], AF.Exp))
            eA = pb.prev
            for g in range(4):
                sl = slice(g * 512, (g + 1) * 512)
                pb.step("vector", lambda e, sl=sl: e.tensor_tensor(sq2[:, :], qh[:, sl], qh[:, sl], ALU.mult))
                pb.step("tensor", lambda e: e.matmul(ps[0][:1, :], onesc[:, :], sq2[:, :], start=True, stop=True))
                pb.step("vector", lambda e: e.tensor_copy(rowa[:1, :], ps[0][:1, :]))
                pb.step("scalar", lambda e: e.activation(rowa[:1, :], rowa[:1, :], AF.Ln))
                pb.step("scalar", lambda e: e.activation(rowa[:1, :], rowa[:1, :], AF.Exp, scale=-0.5))
                pb.step("tensor", lambda e: e.matmul(ps[2][:, :], ones1[:, :], rowa[:1, :], start=True, stop=True))
                pb.step("vector", lambda e, sl=sl: e.tensor_tensor(tmpf[:, :], ah[:, sl], ps[2][:, :], ALU.mult),
                        deps=[eA, pb.prev])
                pb.step("vector", lambda e, sl=sl: e.tensor_tensor(qs[:, sl], tmpf[:, :], qh[:, sl], ALU.mult))
            for g in range(4):
                sl = slice(g * 512, (g + 1) * 512)
                pb.step("vector", lambda e, sl=sl: e.tensor_tensor(sq2[:, :], kh[:, sl], kh[:, sl], ALU.mult))
                pb.step("tensor", lambda e: e.matmul(ps[0][:1, :], onesc[:, :], sq2[:, :], start=True, stop=True))
                pb.step("vector", lambda e: e.tensor_copy(rowa[:1, :], ps[0][:1, :]))
                pb.step("scalar", lambda e: e.activation(rowa[:1, :], rowa[:1, :], AF.Ln))
                pb.step("scalar", lambda e: e.activation(rowa[:1, :], rowa[:1, :], AF.Exp, scale=-0.5))
                pb.step("tensor", lambda e: e.matmul(ps[2][:, :], ones1[:, :], rowa[:1, :], start=True, stop=True))
                pb.step("scalar", lambda e, sl=sl: e.activation(rahg[:, :], gh[:, sl], AF.Exp, scale=-1.0))
                pb.step("vector", lambda e: e.tensor_tensor(rahg[:, :], rahg[:, :], ps[2][:, :], ALU.mult))
                pb.step("vector", lambda e, sl=sl: e.tensor_tensor(ks[:, sl], rahg[:, :], kh[:, sl], ALU.mult))
            for cc in range(NCHUNK):
                sl = slice(cc * CH, (cc + 1) * CH)
                col = slice(cc * CH + CH - 1, cc * CH + CH)
                pb.step("vector", lambda e, sl=sl, col=col: e.tensor_scalar(ke[:, sl], ks[:, sl], ah[:, col], None, ALU.mult))
            for cc in range(NCHUNK):
                sl = slice(cc * CH, (cc + 1) * CH)
                pb.step("tensor", lambda e, sl=sl: e.transpose(psT[:64, :128], ke[:, sl], idm[:, :]))
                pb.step("vector", lambda e, cc=cc: e.tensor_copy(ketm[:, cc * 128:(cc + 1) * 128], psT[:64, :128]))
            pb.step("vector", lambda e: e.memset(sst[:, :], 0.0))
            crdr = {}
            elem_done = pb.prev
            s_supd = {}
            s_ofn = {}
            for cc in range(NCHUNK):
                sl = slice(cc * CH, (cc + 1) * CH)
                col = slice(cc * CH + CH - 1, cc * CH + CH)
                pbank = 2 + cc % 2
                tbank = 5 if cc % 2 == 0 else 0
                slot = (cc % 2) * 64
                s_pmm = pb.step("tensor", lambda e, sl=sl, pbank=pbank: e.matmul(
                    ps[pbank][:64, :64], ks[:, sl], qs[:, sl], start=True, stop=True),
                    deps=[elem_done, crdr.get(("ps", pbank))])
                s_msk = pb.step("vector", lambda e, pbank=pbank, slot=slot: e.tensor_tensor(
                    pm[:, slot:slot + 64], ps[pbank][:64, :64], tri[:, :], ALU.mult),
                    deps=[s_pmm, crdr.get(("pm", slot))])
                crdr[("ps", pbank)] = s_msk

                def ofn(e, sl=sl, cc=cc, slot=slot):
                    e.matmul(ps[4][:, :64], vtm[:, cc * 128:(cc + 1) * 128], pm[:, slot:slot + 64], start=True, stop=False)
                    return e.matmul(ps[4][:, :64], sst[:, :], qs[:, sl], start=False, stop=True)
                s_ofn[cc] = pb.step("tensor", ofn, deps=[s_msk, s_supd.get(cc - 1), crdr.get(("ps", 4))])
                crdr[("pm", slot)] = s_ofn[cc]
                s_oev = pb.step("vector", lambda e, sl=sl, h=h: e.tensor_copy(
                    ofm[:, h * T + sl.start:h * T + sl.stop], ps[4][:, :64]), deps=[s_ofn[cc]])
                crdr[("ps", 4)] = s_oev

                def tfn(e, cc=cc, tbank=tbank):
                    return e.matmul(ps[tbank][:, :128], ketm[:, cc * 128:(cc + 1) * 128],
                                    vtm[:, cc * 128:(cc + 1) * 128], start=True, stop=True)
                s_tmm = pb.step("tensor", tfn, deps=[elem_done, crdr.get(("ps", tbank))])
                s_supd[cc] = pb.step("vector", lambda e, col=col, tbank=tbank: e.scalar_tensor_tensor(
                    sst[:, :], sst[:, :], ah[:, col], ps[tbank][:, :128], ALU.mult, ALU.add),
                    deps=[s_tmm, s_ofn[cc]])
                crdr[("ps", tbank)] = s_supd[cc]
            pb.prev = pb.step("scalar", lambda e: e.nop(), deps=[s_supd[NCHUNK - 1], pb.prev])

        rdr = {}
        gla_done = pb.prev
        ldW = {}
        mmW = {}
        accW = {}
        for dc in range(NCD):
            par = dc % 2
            buf = wa if par == 0 else wb
            ldW[dc] = pb.dma("gpsimd", [(buf[:, 0:1024].rearrange("p (c o) -> p c o", o=128),
                                         colslice_ap(wo_d, dc * 128, 128, 8))],
                             deps=[rdr.get(("wbuf", par)), gla_done], chan="gA" if par == 0 else "gB")
            for g in range(4):
                sl = slice(g * 512, (g + 1) * 512)
                bank = 2 * par + (g % 2)

                def mmfn(e, sl=sl, buf=buf, bank=bank):
                    r = None
                    for c in range(NCD):
                        r = e.matmul(ps[bank][:, :], buf[:, c * 128:(c + 1) * 128],
                                     ofm[:, c * T + sl.start:c * T + sl.stop],
                                     start=(c == 0), stop=(c == NCD - 1))
                    return r
                mmW[(dc, g)] = pb.step("tensor", mmfn, deps=[ldW[dc], rdr.get(("ps", bank)), gla_done])
                rdr[("wbuf", par)] = mmW[(dc, g)]
                cs = slice(dc * T + sl.start, dc * T + sl.stop)
                accW[(dc, g)] = pb.step("vector", lambda e, cs=cs, bank=bank: e.tensor_tensor(
                    x[:, cs], x[:, cs], ps[bank][:, :], ALU.add), deps=[mmW[(dc, g)]])
                rdr[("ps", bank)] = accW[(dc, g)]
        wo_done = pb.step("scalar", lambda e: e.nop(),
                          deps=[accW[(NCD - 1, 3)], accW[(NCD - 2, 3)], accW[(NCD - 1, 2)]])

        pb.prev = wo_done
        emit_ln(x)
        ln2_done = pb.prev

        stores = {}
        last_store = {}
        ofm_free = wo_done
        hb_free = gla_done
        prev_tt_w3 = ln2_done
        prev_tt_h2 = None
        sA = [scr2, sqln]
        for tt in range(NTT):
            tsl = slice(tt * TT, (tt + 1) * TT)
            mt1, mt2, sil, hm, ld12 = {}, {}, {}, {}, {}
            for oc in range(NH):
                par = oc % 2
                buf = wa if par == 0 else wb
                if tt == 0:
                    ld12[oc] = pb.dma("gpsimd", [
                        (buf[:, 0:1024].rearrange("p (c o) -> p c o", o=128),
                         colslice_ap(w1_d, oc * 128, 128, 8)),
                        (buf[:, 1024:2048].rearrange("p (c o) -> p c o", o=128),
                         colslice_ap(w2_d, oc * 128, 128, 8))],
                        deps=[rdr.get(("wbuf", par)), last_store.get(par), ln2_done], chan="gA" if par == 0 else "gB")
                    stores[("w12", oc)] = pb.dma("sync", [
                        (w1b_d[oc], buf[:, 0:1024]), (w2b_d[oc], buf[:, 1024:2048])],
                        deps=[ld12[oc]], chan="syA" if par == 0 else "syB")
                    last_store[par] = stores[("w12", oc)]
                else:
                    ld12[oc] = pb.dma("sync", [
                        (buf[:, 0:1024], w1b_d[oc]), (buf[:, 1024:2048], w2b_d[oc])],
                        deps=[rdr.get(("wbuf", par)), last_store.get(par), stores[("w12", oc)], ln2_done], chan="syA" if par == 0 else "syB")
                b1, b2 = 2 * par, 2 * par + 1

                def m1f(e, tsl=tsl, buf=buf, b1=b1):
                    r = None
                    for c in range(NCD):
                        r = e.matmul(ps[b1][:, :], buf[:, c * 128:(c + 1) * 128],
                                     xln[:, c * T + tsl.start:c * T + tsl.stop],
                                     start=(c == 0), stop=(c == NCD - 1))
                    return r
                mt1[oc] = pb.step("tensor", m1f, deps=[ld12[oc], rdr.get(("ps", b1)), ln2_done])
                rdr[("wbuf", par)] = mt1[oc]

                def m2f(e, tsl=tsl, buf=buf, b2=b2):
                    r = None
                    for c in range(NCD):
                        r = e.matmul(ps[b2][:, :], buf[:, 1024 + c * 128:1024 + (c + 1) * 128],
                                     xln[:, c * T + tsl.start:c * T + tsl.stop],
                                     start=(c == 0), stop=(c == NCD - 1))
                    return r
                mt2[oc] = pb.step("tensor", m2f, deps=[ld12[oc], rdr.get(("ps", b2))])
                rdr[("wbuf", par)] = mt2[oc]
                sil[oc] = pb.step("scalar", lambda e, oc=oc, par=par, b1=b1: e.activation(
                    sA[par][:, :], ps[b1][:, :], AF.Silu, scale=scol[:, oc:oc + 1]),
                    deps=[mt1[oc], rdr.get(("sA", par))])
                rdr[("ps", b1)] = sil[oc]
                hm[oc] = pb.step("vector", lambda e, oc=oc, par=par, b2=b2: e.scalar_tensor_tensor(
                    hb[:, oc * TT:(oc + 1) * TT], ps[b2][:, :], scol[:, NH + oc:NH + oc + 1],
                    sA[par][:, :], ALU.mult, ALU.mult),
                    deps=[mt2[oc], sil[oc], prev_tt_w3, hb_free])
                rdr[("ps", b2)] = hm[oc]
                rdr[("sA", par)] = hm[oc]
            hm_last = hm[NH - 1]
            mmh, gel, ldh = {}, {}, {}
            for oc in range(NH):
                par = oc % 2
                buf = wa if par == 0 else wb
                if tt == 0:
                    ldh[oc] = pb.dma("gpsimd", [(buf[:, :].rearrange("p (c o) -> p c o", o=128),
                                                 colslice_ap(h1_d, oc * 128, 128, 32))],
                                     deps=[rdr.get(("wbuf", par)), last_store.get(par)], chan="gA" if par == 0 else "gB")
                    stores[("h1", oc)] = pb.dma("sync", [(h1b_d[oc], buf[:, :])], deps=[ldh[oc]], chan="syA" if par == 0 else "syB")
                    last_store[par] = stores[("h1", oc)]
                else:
                    ldh[oc] = pb.dma("sync", [(buf[:, :], h1b_d[oc])],
                                     deps=[rdr.get(("wbuf", par)), last_store.get(par), stores[("h1", oc)]], chan="syA" if par == 0 else "syB")
                bank = par

                def mhf(e, buf=buf, bank=bank):
                    r = None
                    for ic in range(NH):
                        r = e.matmul(ps[bank][:, :], buf[:, ic * 128:(ic + 1) * 128],
                                     hb[:, ic * TT:(ic + 1) * TT],
                                     start=(ic == 0), stop=(ic == NH - 1))
                    return r
                mmh[oc] = pb.step("tensor", mhf, deps=[ldh[oc], rdr.get(("ps", bank)), hm_last])
                rdr[("wbuf", par)] = mmh[oc]
                gel[oc] = pb.step("scalar", lambda e, oc=oc, bank=bank: e.activation(
                    g1[:, oc * TT:(oc + 1) * TT], ps[bank][:, :], AF.Gelu, scale=scol2[:, oc:oc + 1]),
                    deps=[mmh[oc], ofm_free, prev_tt_h2])
                rdr[("ps", bank)] = gel[oc]
            gel_last = gel[NH - 1]
            mmh2, gel2, ldh2 = {}, {}, {}
            for oc in range(NH):
                par = oc % 2
                buf = wa if par == 0 else wb
                if tt == 0:
                    ldh2[oc] = pb.dma("gpsimd", [(buf[:, :].rearrange("p (c o) -> p c o", o=128),
                                                  colslice_ap(h2_d, oc * 128, 128, 32))],
                                      deps=[rdr.get(("wbuf", par)), last_store.get(par)], chan="gA" if par == 0 else "gB")
                    stores[("h2", oc)] = pb.dma("sync", [(h2b_d[oc], buf[:, :])], deps=[ldh2[oc]], chan="syA" if par == 0 else "syB")
                    last_store[par] = stores[("h2", oc)]
                else:
                    ldh2[oc] = pb.dma("sync", [(buf[:, :], h2b_d[oc])],
                                      deps=[rdr.get(("wbuf", par)), last_store.get(par), stores[("h2", oc)]], chan="syA" if par == 0 else "syB")
                bank = 2 + par

                def mh2f(e, buf=buf, bank=bank):
                    r = None
                    for ic in range(NH):
                        r = e.matmul(ps[bank][:, :], buf[:, ic * 128:(ic + 1) * 128],
                                     g1[:, ic * TT:(ic + 1) * TT],
                                     start=(ic == 0), stop=(ic == NH - 1))
                    return r
                mmh2[oc] = pb.step("tensor", mh2f, deps=[ldh2[oc], rdr.get(("ps", bank)), gel_last])
                rdr[("wbuf", par)] = mmh2[oc]
                gel2[oc] = pb.step("scalar", lambda e, oc=oc, bank=bank: e.activation(
                    hb[:, oc * TT:(oc + 1) * TT], ps[bank][:, :], AF.Gelu, scale=scol2[:, NH + oc:NH + oc + 1]),
                    deps=[mmh2[oc]])
                rdr[("ps", bank)] = gel2[oc]
            gel2_last = gel2[NH - 1]
            prev_tt_h2 = gel2_last
            mmy, accy, ldy = {}, {}, {}
            for dc in range(NCD):
                par = dc % 2
                buf = wa if par == 0 else wb
                if tt == 0:
                    ldy[dc] = pb.dma("gpsimd", [(buf[:, :].rearrange("p (c o) -> p c o", o=128),
                                                 colslice_ap(w3_d, dc * 128, 128, 32))],
                                     deps=[rdr.get(("wbuf", par)), last_store.get(par)], chan="gA" if par == 0 else "gB")
                    stores[("w3", dc)] = pb.dma("sync", [(w3b_d[dc], buf[:, :])], deps=[ldy[dc]], chan="syA" if par == 0 else "syB")
                    last_store[par] = stores[("w3", dc)]
                else:
                    ldy[dc] = pb.dma("sync", [(buf[:, :], w3b_d[dc])],
                                     deps=[rdr.get(("wbuf", par)), last_store.get(par), stores[("w3", dc)]], chan="syA" if par == 0 else "syB")
                bank = 4 + par

                def myf(e, buf=buf, bank=bank):
                    r = None
                    for ic in range(NH):
                        r = e.matmul(ps[bank][:, :], buf[:, ic * 128:(ic + 1) * 128],
                                     hb[:, ic * TT:(ic + 1) * TT],
                                     start=(ic == 0), stop=(ic == NH - 1))
                    return r
                mmy[dc] = pb.step("tensor", myf, deps=[ldy[dc], rdr.get(("ps", bank)), gel2_last])
                rdr[("wbuf", par)] = mmy[dc]
                cs = slice(dc * T + tsl.start, dc * T + tsl.stop)
                accy[dc] = pb.step("vector", lambda e, cs=cs, dc=dc, bank=bank: e.scalar_tensor_tensor(
                    x[:, cs], ps[bank][:, :], scol[:, 2 * NH + dc:2 * NH + dc + 1], x[:, cs], ALU.mult, ALU.add),
                    deps=[mmy[dc]])
                rdr[("ps", bank)] = accy[dc]
            prev_tt_w3 = pb.step("scalar", lambda e: e.nop(),
                                 deps=[accy[NCD - 1], accy[NCD - 2]])
            pb.prev = prev_tt_w3

        stv = pb.counts["syC"] + 16 * NCD
        pb.dma("sync", [(yT_d[c * 128:(c + 1) * 128, :], x[:, c * T:(c + 1) * T]) for c in range(NCD)], chan="syC")

        def fin(e):
            e.wait_ge(sems["syC"], stv)
            return e.nop()
        pb.step("sync", fin)

        pb.emit(block, sems)
    return nc


def prep_inputs_A(inputs, b):
    asc = np.ascontiguousarray
    return {
        "xT": asc(inputs["L0_attn_x"][b].T),
        "Wq": asc(inputs["L0_attn_Wq"]), "Wk": asc(inputs["L0_attn_Wk"]),
        "Wv": asc(inputs["L0_attn_Wv"]), "Wg": asc(inputs["L0_attn_Wg"]),
        "Wo": asc(inputs["L0_attn_Wo"]),
        "w1T": asc(inputs["L0_ffn_w1_w"].T), "w2T": asc(inputs["L0_ffn_w2_w"].T),
        "h1T": asc(inputs["L0_ffn_h1_w"].T), "h2T": asc(inputs["L0_ffn_h2_w"].T),
        "w3T": asc(inputs["L0_ffn_w3_w"].T),
        "s1": asc(inputs["L0_ffn_w1_s"].reshape(NH, 128).T),
        "s2": asc(inputs["L0_ffn_w2_s"].reshape(NH, 128).T),
        "sh1": asc(inputs["L0_ffn_h1_s"].reshape(NH, 128).T),
        "sh2": asc(inputs["L0_ffn_h2_s"].reshape(NH, 128).T),
        "s3": asc(inputs["L0_ffn_w3_s"].reshape(NCD, 128).T),
    }


TT2 = 1024
NT2 = T // TT2
NE = 8
NB = 8
TB = 256


def build_launchB(dbg=False):
    nc = bass.Bass("TRN2", target_bir_lowering=False)

    xT_d = nc.dram_tensor("xT", [D, T], F32, kind="ExternalInput").ap()
    win_d = nc.dram_tensor("WinT", [D, D], F32, kind="ExternalInput").ap()
    wlam_d = nc.dram_tensor("WlamT", [D, D], F32, kind="ExternalInput").ap()
    wgate_d = nc.dram_tensor("WgateT", [D, D], F32, kind="ExternalInput").ap()
    wout_d = nc.dram_tensor("WoutT", [D, D], F32, kind="ExternalInput").ap()
    bcols_d = nc.dram_tensor("bcols", [128, 48], F32, kind="ExternalInput").ap()
    gwc_d = nc.dram_tensor("gwc", [128, 64], F32, kind="ExternalInput").ap()
    grows_d = nc.dram_tensor("grows", [1, D], F32, kind="ExternalInput").ap()
    wgT_d = nc.dram_tensor("wgT", [NE, D, DH], F32, kind="ExternalInput").ap()
    wuT_d = nc.dram_tensor("wuT", [NE, D, DH], F32, kind="ExternalInput").ap()
    wdT_d = nc.dram_tensor("wdT", [NE, DH, D], F32, kind="ExternalInput").ap()
    outT_d = nc.dram_tensor("outT", [D, T], F32, kind="ExternalOutput").ap()
    if dbg:
        dbgA_d = nc.dram_tensor("dbgA", [D, T], F32, kind="ExternalOutput").ap()
        dbgL_d = nc.dram_tensor("dbgL", [D, T], BF16, kind="ExternalOutput").ap()
        dbgH_d = nc.dram_tensor("dbgH", [D, T], BF16, kind="ExternalOutput").ap()
        cmb_d = nc.dram_tensor("cmbo", [8, T], BF16, kind="ExternalOutput").ap()
        cmbt_d = nc.dram_tensor("cmbt_o", [128, 128], F32, kind="ExternalOutput").ap()
        dbgZ_d = nc.dram_tensor("dbgZ", [D, T], BF16, kind="ExternalOutput").ap()
        dbgX_d = nc.dram_tensor("dbgX", [D, T], F32, kind="ExternalOutput").ap()

    idf_d = nc.inline_tensor(np.eye(128, dtype=np.float32), name="idf")
    sel_np = np.zeros((8, 1024), dtype=ml_dtypes.bfloat16)
    for e in range(8):
        sel_np[e, e * 128:(e + 1) * 128] = 1.0
    sel_d = nc.inline_tensor(sel_np, name="selc")
    cmb_dram = nc.dram_tensor("cmbscr", [128, 128], F32)

    sm = SMap(nc)
    x = sm.seq("x", [128, NCD * T], F32)
    xln = sm.seq("xln", [128, NCD * T], BF16)
    wa = sm.seq("wa", [128, 4096], BF16)
    wb = sm.seq("wb", [128, 4096], BF16)
    scr3 = sm.seq("scr3", [128, 512], BF16)
    scr4 = sm.seq("scr4", [128, 512], BF16)
    tmpf2 = sm.seq("tmpf2", [128, 512], F32)
    grows = sm.seq("grows", [1, D], F32)
    rowa = sm.seq("rowa", [1, 512], F32)
    rowb = sm.seq("rowb", [1, 512], F32)
    rowc = sm.seq("rowc", [1, 512], F32)
    tmpf = sm.seq("tmpf", [128, 512], F32)
    scr2 = sm.seq("scr2", [128, 512], BF16)
    sqln = sm.seq("sqln", [128, 512], BF16)
    ones1 = sm.seq("ones1", [1, 128], F32)
    onesc = sm.seq("onesc", [128, 1], BF16)
    idf = sm.seq("idf", [128, 128], F32)
    sel = sm.seq("sel", [8, 1024], BF16)
    bcols = sm.seq("bcols", [128, 48], F32)
    wgw = sm.seq("wgw", [128, 64], BF16)
    cmbt = sm.seq("cmbt", [128, 128], F32)
    cmbu = sm.seq("cmbu", [128, 128], F32)
    cfm8 = sm.seq("cfm8", [8, T], BF16)
    cns = sm.seq("cns", [1, 8], F32)
    UB = sm.off
    lam = sm.at("lam", [128, NCD * T], BF16, UB)
    hbuf = sm.at("hbuf", [128, NCD * T], BF16, UB + 32768)
    hmoe = sm.at("hmoe", [128, NH * TT2], BF16, UB)
    cb = sm.at("cb", [128, T], BF16, sm.offsets["rowa"])
    assert UB + 65536 <= sm.top, (UB, sm.top)

    from contextlib import ExitStack
    with ExitStack() as es:
        E = es.enter_context
        ps = [E(nc.psum_tensor(f"ps{i}", [128, 512], F32)) for i in range(6)]
        psT = E(nc.psum_tensor("psT", [128, 512], F32))
        psX = E(nc.psum_tensor("psX", [128, 512], F32))
        sems = {e: E(nc.semaphore("sem_" + e)) for e in ENGS + DMA_CHANS}
        block = E(nc.Block())

        pb = PB(nc)

        pb.dma("gpsimd", [(x[:, c * T:(c + 1) * T], xT_d[c * 128:(c + 1) * 128, :])
                          for c in range(NCD)], deps=[])
        ldc = pb.dma("sync", [(idf[:, :], idf_d.ap()), (sel[:, :], sel_d.ap()),
                              (bcols[:, :], bcols_d[:, :]), (grows[:1, :], grows_d[:, :])], deps=[], chan="syC")
        ldw = pb.dma("gpsimd", [(wgw[:, :], gwc_d[:, :])], deps=[])

        def ones_fn(e):
            e.memset(ones1[:, :], 1.0)
            e.memset(cns[:1, :], LN_EPS)
            return e.memset(onesc[:, :], 1.0)
        pb.step("vector", ones_fn, deps=[])
        pb.step("scalar", lambda e: e.nop(), deps=[pb.steps[0], ldc, ldw, pb.prev])

        tmpq = [scr3, scr4]
        tmpa = [tmpf, tmpf2]

        def emit_ln(src_t):
            entry = pb.prev
            rl = {}
            ap_last = None
            for g in range(4):
                sl = slice(g * 512, (g + 1) * 512)
                for c in range(NCD):
                    cs = slice(c * T + sl.start, c * T + sl.stop)
                    par = c % 2
                    cbuf = scr2 if par == 0 else sqln
                    s_cast = pb.step("vector", lambda e, cs=cs, cbuf=cbuf: e.tensor_copy(cbuf[:, :512], src_t[:, cs]),
                                     deps=[rl.get(("cb", par)), entry], chan="gA" if par == 0 else "gB")
                    s_mm1 = pb.step("tensor", lambda e, c=c, cbuf=cbuf: e.matmul(
                        ps[0][:1, :], onesc[:, :], cbuf[:, :512], start=(c == 0), stop=(c == NCD - 1)),
                        deps=[s_cast, rl.get(("ps", 0))])
                    s_sq = pb.step("vector", lambda e, cbuf=cbuf, par=par: e.tensor_tensor(
                        tmpq[par][:, :], cbuf[:, :512], cbuf[:, :512], ALU.mult),
                        deps=[s_cast, rl.get(("sq", par))])
                    s_mm2 = pb.step("tensor", lambda e, c=c, par=par: e.matmul(
                        ps[1][:1, :], onesc[:, :], tmpq[par][:, :], start=(c == 0), stop=(c == NCD - 1)),
                        deps=[s_sq, rl.get(("ps", 1))])
                    rl[("cb", par)] = s_mm1
                    rl[("sq", par)] = s_mm2
                r1 = pb.step("vector", lambda e: e.tensor_scalar(rowa[:1, :], ps[0][:1, :], 1.0 / D, None, ALU.mult),
                             deps=[s_mm1, rl.get("rows")])
                rl[("ps", 0)] = r1
                r2 = pb.step("vector", lambda e: e.tensor_scalar(rowb[:1, :], ps[1][:1, :], 1.0 / D, None, ALU.mult),
                             deps=[s_mm2, r1, rl.get("bA")])
                rl[("ps", 1)] = r2
                r3 = pb.step("vector", lambda e: e.tensor_tensor(rowc[:1, :], rowa[:1, :], rowa[:1, :], ALU.mult),
                             deps=[r2, rl.get("bB")])
                r4 = pb.step("vector", lambda e: e.tensor_tensor(rowb[:1, :], rowb[:1, :], rowc[:1, :], ALU.subtract), deps=[r3])
                r5 = pb.step("scalar", lambda e: e.activation(rowb[:1, :], rowb[:1, :], AF.Ln, bias=cns[:1, :1]), deps=[r4])
                r6 = pb.step("scalar", lambda e: e.activation(rowb[:1, :], rowb[:1, :], AF.Exp, scale=-0.5), deps=[r5])
                r7 = pb.step("vector", lambda e: e.scalar_tensor_tensor(rowc[:1, :], rowa[:1, :], -1.0, rowb[:1, :], ALU.mult, ALU.mult), deps=[r6])
                rl["rows"] = r7
                bA = pb.step("tensor", lambda e: e.matmul(ps[2][:, :], ones1[:, :], rowb[:1, :], start=True, stop=True),
                             deps=[r6, r7, rl.get(("ps", 2))])
                bB = pb.step("tensor", lambda e: e.matmul(ps[3][:, :], ones1[:, :], rowc[:1, :], start=True, stop=True),
                             deps=[r7, rl.get(("ps", 3))])
                rl["bA"] = bA
                rl["bB"] = bB
                for c in range(NCD):
                    cs = slice(c * T + sl.start, c * T + sl.stop)
                    par = c % 2
                    a1 = pb.step("vector", lambda e, cs=cs, par=par: e.tensor_tensor(
                        tmpa[par][:, :], src_t[:, cs], ps[2][:, :], ALU.mult),
                        deps=[bA, rl.get(("ta", par))])
                    a2 = pb.step("vector", lambda e, cs=cs, par=par: e.tensor_tensor(
                        xln[:, cs], tmpa[par][:, :], ps[3][:, :], ALU.add),
                        deps=[bB, a1])
                    rl[("ta", par)] = a2
                    ap_last = a2
                rl[("ps", 2)] = ap_last
                rl[("ps", 3)] = ap_last
            pb.prev = pb.step("scalar", lambda e: e.nop(), deps=[ap_last])

        emit_ln(x)
        rdr = {}
        ln3_done = pb.prev
        sA = [scr2, scr3]
        sB = [tmpf]
        for oc in range(NCD):
            par = oc % 2
            buf = wa if par == 0 else wb
            ldL = pb.dma("gpsimd", [(buf[:, 0:1024].rearrange("p (c o) -> p c o", o=128),
                                     colslice_ap(wlam_d, oc * 128, 128, 8)),
                                    (buf[:, 1024:2048].rearrange("p (c o) -> p c o", o=128),
                                     colslice_ap(win_d, oc * 128, 128, 8))],
                         deps=[rdr.get(("wbuf", par)), ln3_done], chan="gA" if par == 0 else "gB")
            for g in range(4):
                sl = slice(g * 512, (g + 1) * 512)
                b1, b2 = 2 * par, 2 * par + 1

                def mml(e, sl=sl, buf=buf, b1=b1):
                    r = None
                    for c in range(NCD):
                        r = e.matmul(ps[b1][:, :], buf[:, c * 128:(c + 1) * 128],
                                     xln[:, c * T + sl.start:c * T + sl.stop],
                                     start=(c == 0), stop=(c == NCD - 1))
                    return r
                s_mml = pb.step("tensor", mml, deps=[ldL, rdr.get(("ps", b1)), ln3_done])
                rdr[("wbuf", par)] = s_mml
                s_sig = pb.step("scalar", lambda e, oc=oc, sl=sl, b1=b1: e.activation(
                    lam[:, oc * T + sl.start:oc * T + sl.stop], ps[b1][:, :], AF.Sigmoid,
                    bias=bcols[:, 8 + oc:9 + oc]), deps=[s_mml])
                rdr[("ps", b1)] = s_sig

                def mmi(e, sl=sl, buf=buf, b2=b2):
                    r = None
                    for c in range(NCD):
                        r = e.matmul(ps[b2][:, :], buf[:, 1024 + c * 128:1024 + (c + 1) * 128],
                                     xln[:, c * T + sl.start:c * T + sl.stop],
                                     start=(c == 0), stop=(c == NCD - 1))
                    return r
                s_mmi = pb.step("tensor", mmi, deps=[ldL, rdr.get(("ps", b2))])
                rdr[("wbuf", par)] = s_mmi
                s_sil = pb.step("scalar", lambda e, oc=oc, par=par, b2=b2: e.activation(
                    sA[par][:, :], ps[b2][:, :], AF.Silu, bias=bcols[:, oc:1 + oc]),
                    deps=[s_mmi, rdr.get(("sA", par))])
                rdr[("ps", b2)] = s_sil
                s_om = pb.step("vector", lambda e, oc=oc, sl=sl: e.tensor_scalar(
                    tmpf[:, :], lam[:, oc * T + sl.start:oc * T + sl.stop], -1.0, 1.0, ALU.mult, ALU.add),
                    deps=[s_sig, rdr.get("tmpf")])
                s_hm = pb.step("vector", lambda e, oc=oc, sl=sl, par=par: e.tensor_tensor(
                    hbuf[:, oc * T + sl.start:oc * T + sl.stop], tmpf[:, :], sA[par][:, :], ALU.mult),
                    deps=[s_om, s_sil])
                rdr[("sA", par)] = s_hm
                rdr["tmpf"] = s_hm
        pb.prev = pb.step("scalar", lambda e: e.nop(), deps=[pb.prev])
        for c in range(NCD):
            for b in range(1, NB):
                s_prev = slice(c * T + (b - 1) * TB, c * T + b * TB)
                s_cur = slice(c * T + b * TB, c * T + (b + 1) * TB)
                pb.step("vector", lambda e, s_prev=s_prev, s_cur=s_cur: e.tensor_tensor(
                    tmpf[:, :TB], lam[:, s_cur], hbuf[:, s_prev], ALU.mult))
                pb.step("vector", lambda e, s_cur=s_cur: e.tensor_tensor(
                    hbuf[:, s_cur], hbuf[:, s_cur], tmpf[:, :TB], ALU.add))
        if dbg:
            pb.dma("sync", [(dbgL_d[c * 128:(c + 1) * 128, :], lam[:, c * T:(c + 1) * T])
                            for c in range(NCD)], chan="syC")
            pb.dma("sync", [(dbgH_d[c * 128:(c + 1) * 128, :], hbuf[:, c * T:(c + 1) * T])
                            for c in range(NCD)], chan="syC")
        entry_h = pb.prev
        rl = {}
        for g in range(4):
            sl = slice(g * 512, (g + 1) * 512)
            for c in range(NCD):
                cs = slice(c * T + sl.start, c * T + sl.stop)
                par = c % 2
                s_mm1 = pb.step("tensor", lambda e, c=c, cs=cs: e.matmul(
                    ps[0][:1, :], onesc[:, :], hbuf[:, cs], start=(c == 0), stop=(c == NCD - 1)),
                    deps=[entry_h, rl.get(("ps", 0))])
                s_sq = pb.step("vector", lambda e, cs=cs, par=par: e.tensor_tensor(
                    tmpq[par][:, :], hbuf[:, cs], hbuf[:, cs], ALU.mult),
                    deps=[entry_h, rl.get(("sq", par))])
                s_mm2 = pb.step("tensor", lambda e, c=c, par=par: e.matmul(
                    ps[1][:1, :], onesc[:, :], tmpq[par][:, :], start=(c == 0), stop=(c == NCD - 1)),
                    deps=[s_sq, rl.get(("ps", 1))])
                rl[("sq", par)] = s_mm2
            r1 = pb.step("vector", lambda e: e.tensor_scalar(rowa[:1, :], ps[0][:1, :], 1.0 / D, None, ALU.mult),
                         deps=[s_mm1, rl.get("rows")])
            rl[("ps", 0)] = r1
            r2 = pb.step("vector", lambda e: e.tensor_scalar(rowb[:1, :], ps[1][:1, :], 1.0 / D, None, ALU.mult),
                         deps=[s_mm2, r1, rl.get("bA")])
            rl[("ps", 1)] = r2
            r3 = pb.step("vector", lambda e: e.tensor_tensor(rowc[:1, :], rowa[:1, :], rowa[:1, :], ALU.mult),
                         deps=[r2, rl.get("bB")])
            r4 = pb.step("vector", lambda e: e.tensor_tensor(rowb[:1, :], rowb[:1, :], rowc[:1, :], ALU.subtract), deps=[r3])
            r5 = pb.step("scalar", lambda e: e.activation(rowb[:1, :], rowb[:1, :], AF.Ln, bias=cns[:1, :1]), deps=[r4])
            r6 = pb.step("scalar", lambda e: e.activation(rowb[:1, :], rowb[:1, :], AF.Exp, scale=-0.5), deps=[r5])
            r7 = pb.step("vector", lambda e: e.scalar_tensor_tensor(rowc[:1, :], rowa[:1, :], -1.0, rowb[:1, :], ALU.mult, ALU.mult), deps=[r6])
            rl["rows"] = r7
            last_bA = None
            for c in range(NCD):
                cs = slice(c * T + sl.start, c * T + sl.stop)
                par = c % 2
                buf = wa if par == 0 else wb
                ldg = pb.dma("gpsimd", [(buf[:, 0:1024].rearrange("p (cc o) -> p cc o", o=128),
                                         colslice_ap(wgate_d, c * 128, 128, 8))],
                             deps=[rdr.get(("wbuf", par))], chan="gA" if par == 0 else "gB")
                bA = pb.step("tensor", lambda e, c=c: e.matmul(
                    ps[2][:, :], grows[:1, c * 128:(c + 1) * 128], rowb[:1, :], start=True, stop=True),
                    deps=[r7, rl.get(("ps", 2))])
                bB = pb.step("tensor", lambda e, c=c: e.matmul(
                    ps[3][:, :], grows[:1, c * 128:(c + 1) * 128], rowc[:1, :], start=True, stop=True),
                    deps=[r7, rl.get(("ps", 3))])
                rl["bA"] = bA
                rl["bB"] = bB
                last_bA = bA

                def mmg(e, sl=sl, buf=buf, par=par):
                    r = None
                    for cc in range(NCD):
                        r = e.matmul(ps[4 + par][:, :], buf[:, cc * 128:(cc + 1) * 128],
                                     xln[:, cc * T + sl.start:cc * T + sl.stop],
                                     start=(cc == 0), stop=(cc == NCD - 1))
                    return r
                s_g = pb.step("tensor", mmg, deps=[ldg, rl.get(("psg", par))])
                rdr[("wbuf", par)] = s_g
                sg_buf = scr2 if par == 0 else sqln
                s_sil = pb.step("scalar", lambda e, c=c, par=par, sg_buf=sg_buf: e.activation(
                    sg_buf[:, :], ps[4 + par][:, :], AF.Silu, bias=bcols[:, 16 + c:17 + c]),
                    deps=[s_g, rl.get(("sg", par))])
                rl[("psg", par)] = s_sil
                a1 = pb.step("vector", lambda e, cs=cs, par=par: e.tensor_tensor(
                    tmpa[par][:, :], hbuf[:, cs], ps[2][:, :], ALU.mult),
                    deps=[bA, rl.get(("ta", par))])
                rl[("ps", 2)] = a1
                a2 = pb.step("vector", lambda e, c=c, par=par: e.scalar_tensor_tensor(
                    tmpa[par][:, :], tmpa[par][:, :], bcols[:, 40 + c:41 + c], ps[3][:, :], ALU.add, ALU.add),
                    deps=[bB, a1])
                rl[("ps", 3)] = a2
                s_z = pb.step("vector", lambda e, cs=cs, par=par, sg_buf=sg_buf: e.tensor_tensor(
                    hbuf[:, cs], tmpa[par][:, :], sg_buf[:, :], ALU.mult),
                    deps=[a2, s_sil])
                rl[("ta", par)] = s_z
                rl[("sg", par)] = s_z
        pb.prev = pb.step("scalar", lambda e: e.nop(), deps=[s_z])

        zdone = pb.prev
        if dbg:
            pb.dma("sync", [(dbgZ_d[c * 128:(c + 1) * 128, :], hbuf[:, c * T:(c + 1) * T])
                            for c in range(NCD)], chan="syC")
            pb.dma("sync", [(dbgX_d[c * 128:(c + 1) * 128, :], x[:, c * T:(c + 1) * T])
                            for c in range(NCD)], chan="syC")
            zdone = pb.prev
        for dc in range(NCD):
            par = dc % 2
            buf = wa if par == 0 else wb
            ldo = pb.dma("gpsimd", [(buf[:, 0:1024].rearrange("p (c o) -> p c o", o=128),
                                     colslice_ap(wout_d, dc * 128, 128, 8))],
                         deps=[rdr.get(("wbuf", par)), zdone], chan="gA" if par == 0 else "gB")
            for g in range(4):
                sl = slice(g * 512, (g + 1) * 512)
                bank = 2 * par + (g % 2)

                def mmo(e, sl=sl, buf=buf, bank=bank):
                    r = None
                    for c in range(NCD):
                        r = e.matmul(ps[bank][:, :], buf[:, c * 128:(c + 1) * 128],
                                     hbuf[:, c * T + sl.start:c * T + sl.stop],
                                     start=(c == 0), stop=(c == NCD - 1))
                    return r
                s_mm = pb.step("tensor", mmo, deps=[ldo, rdr.get(("ps", bank)), zdone])
                rdr[("wbuf", par)] = s_mm
                cs = slice(dc * T + sl.start, dc * T + sl.stop)
                s_acc = pb.step("vector", lambda e, cs=cs, dc=dc, bank=bank: e.scalar_tensor_tensor(
                    x[:, cs], ps[bank][:, :], bcols[:, 24 + dc:25 + dc], x[:, cs], ALU.add, ALU.add),
                    deps=[s_mm])
                rdr[("ps", bank)] = s_acc
        pb.prev = pb.step("scalar", lambda e: e.nop(), deps=[pb.prev])

        if dbg:
            _d = pb.dma("sync", [(dbgA_d[c * 128:(c + 1) * 128, :], x[:, c * T:(c + 1) * T])
                                 for c in range(NCD)], chan="syC")
            pb.prev = _d
        emit_ln(x)
        ln4_done = pb.prev
        lrl = {}
        for t in range(16):
            tsl = slice(t * 128, (t + 1) * 128)
            par = t % 2
            lbank = par
            o0 = 32 * par

            def mmlg(e, tsl=tsl, lbank=lbank):
                r = None
                for c in range(NCD):
                    r = e.matmul(ps[lbank][:, :8], xln[:, c * T + tsl.start:c * T + tsl.stop],
                                 wgw[:, c * 8:(c + 1) * 8], start=(c == 0), stop=(c == NCD - 1))
                return r
            s_mm = pb.step("tensor", mmlg, deps=[ln4_done, lrl.get(("ps", lbank))])
            F = o0
            s1 = pb.step("vector", lambda e, lbank=lbank, F=F: e.tensor_copy(tmpf[:, F:F + 8], ps[lbank][:, :8]),
                         deps=[s_mm, lrl.get(("tf", par))], chan="syA" if par == 0 else "syB")
            lrl[("ps", lbank)] = s1
            s2 = pb.step("vector", lambda e, F=F: e.reduce_max(tmpf[:, F + 8:F + 9], tmpf[:, F:F + 8], axis=AX.X), deps=[s1])
            s3 = pb.step("vector", lambda e, F=F: e.tensor_scalar(tmpf[:, F + 16:F + 24], tmpf[:, F:F + 8], tmpf[:, F + 8:F + 9], None, ALU.is_ge), deps=[s2])
            s4 = pb.step("vector", lambda e, F=F: e.scalar_tensor_tensor(tmpf[:, F + 24:F + 32], tmpf[:, F + 16:F + 24], -1e30, tmpf[:, F:F + 8], ALU.mult, ALU.add), deps=[s3])
            s5 = pb.step("vector", lambda e, F=F: e.reduce_max(tmpf[:, F + 10:F + 11], tmpf[:, F + 24:F + 32], axis=AX.X), deps=[s4])
            s6 = pb.step("vector", lambda e, F=F: e.tensor_scalar(tmpf[:, F + 9:F + 10], tmpf[:, F + 8:F + 9], -1.0, None, ALU.mult), deps=[s5])
            s7 = pb.step("scalar", lambda e, F=F: e.activation(tmpf[:, F + 11:F + 12], tmpf[:, F + 10:F + 11], AF.Exp, bias=tmpf[:, F + 9:F + 10]), deps=[s6])
            s8 = pb.step("vector", lambda e, F=F: e.tensor_scalar(tmpf[:, F + 11:F + 12], tmpf[:, F + 11:F + 12], 1.0, None, ALU.add), deps=[s7])
            s9 = pb.step("vector", lambda e, F=F: e.reciprocal(tmpf[:, F + 12:F + 13], tmpf[:, F + 11:F + 12]), deps=[s8])
            s10 = pb.step("scalar", lambda e, F=F: e.activation(tmpf[:, F + 24:F + 32], tmpf[:, F:F + 8], AF.Exp, bias=tmpf[:, F + 9:F + 10]), deps=[s9])
            s11 = pb.step("vector", lambda e, F=F: e.tensor_scalar(tmpf[:, F + 16:F + 24], tmpf[:, F:F + 8], tmpf[:, F + 10:F + 11], None, ALU.is_ge), deps=[s10])
            s12 = pb.step("vector", lambda e, F=F: e.tensor_tensor(tmpf[:, F + 24:F + 32], tmpf[:, F + 16:F + 24], tmpf[:, F + 24:F + 32], ALU.mult), deps=[s11])
            s13 = pb.step("vector", lambda e, t=t, F=F: e.tensor_scalar(cmbt[:, t * 8:(t + 1) * 8], tmpf[:, F + 24:F + 32], tmpf[:, F + 12:F + 13], None, ALU.mult), deps=[s12])
            lrl[("tf", par)] = s13
        pb.prev = pb.step("scalar", lambda e: e.nop(), deps=[s13])
        pb.step("tensor", lambda e: e.transpose(psT[:, :128], cmbt[:, :], idf[:, :]))
        pb.step("vector", lambda e: e.tensor_copy(cmbu[:, :], psT[:, :128]))
        pb.dma("sync", [(cmb_dram.ap(), cmbu[:, :])], chan="syC")
        pb.dma("gpsimd", [(cfm8[:, :].rearrange("e (t j) -> e t j", j=128),
                          cmb_dram.ap().rearrange("(t e) j -> e t j", e=8))], chan="gC")
        if dbg:
            pb.dma("sync", [(cmb_d[:, :], cfm8[:, :])], chan="syC")
            pb.dma("sync", [(cmbt_d[:, :], cmbt[:, :])], chan="syC")
        cmb_ready = pb.prev
        sAm = [scr2, scr3]
        sBm = [sqln, scr4]
        prev_dn = cmb_ready
        prev_ue_last = None
        for ex in range(NE):
            cb_evs = []
            for g in range(4):
                sl = slice(g * 512, (g + 1) * 512)
                s_cb = pb.step("tensor", lambda e, ex=ex, sl=sl: e.matmul(
                    ps[4][:, :], sel[:, ex * 128:(ex + 1) * 128], cfm8[:, sl], start=True, stop=True),
                    deps=[cmb_ready, rdr.get(("ps", 4)), prev_ue_last])
                s_ev = pb.step("vector", lambda e, sl=sl: e.tensor_copy(cb[:, sl], ps[4][:, :]),
                               deps=[s_cb, prev_ue_last])
                rdr[("ps", 4)] = s_ev
                cb_evs.append(s_ev)
            cb_done = cb_evs[-1]
            for t2 in range(NT2):
                base = t2 * TT2
                hms = []
                for oc in range(NH):
                    par = oc % 2
                    buf = wa if par == 0 else wb
                    ldgu = pb.dma("gpsimd", [
                        (buf[:, 0:1024].rearrange("p (c o) -> p c o", o=128),
                         colslice_ap(wgT_d[ex], oc * 128, 128, 8)),
                        (buf[:, 1024:2048].rearrange("p (c o) -> p c o", o=128),
                         colslice_ap(wuT_d[ex], oc * 128, 128, 8))],
                        deps=[rdr.get(("wbuf", par))], chan="gA" if par == 0 else "gB")

                    for hf in range(2):
                        sl = slice(base + hf * 512, base + hf * 512 + 512)
                        b1, b2 = 2 * par, 2 * par + 1

                        def mmg2(e, sl=sl, buf=buf, b1=b1):
                            r = None
                            for c in range(NCD):
                                r = e.matmul(ps[b1][:, :], buf[:, c * 128:(c + 1) * 128],
                                             xln[:, c * T + sl.start:c * T + sl.stop],
                                             start=(c == 0), stop=(c == NCD - 1))
                            return r
                        s_g = pb.step("tensor", mmg2, deps=[ldgu, rdr.get(("ps", b1)), cmb_ready])
                        rdr[("wbuf", par)] = s_g

                        def mmu2(e, sl=sl, buf=buf, b2=b2):
                            r = None
                            for c in range(NCD):
                                r = e.matmul(ps[b2][:, :], buf[:, 1024 + c * 128:1024 + (c + 1) * 128],
                                             xln[:, c * T + sl.start:c * T + sl.stop],
                                             start=(c == 0), stop=(c == NCD - 1))
                            return r
                        s_u = pb.step("tensor", mmu2, deps=[ldgu, rdr.get(("ps", b2)), cmb_ready])
                        rdr[("wbuf", par)] = s_u
                        s_si = pb.step("scalar", lambda e, par=par, b1=b1: e.activation(
                            sAm[par][:, :], ps[b1][:, :], AF.Silu),
                            deps=[s_g, rdr.get(("sA", par))])
                        rdr[("ps", b1)] = s_si
                        s_ue = pb.step("vector", lambda e, sl=sl, par=par, b2=b2: e.tensor_tensor(
                            sBm[par][:, :], ps[b2][:, :], cb[:, sl], ALU.mult),
                            deps=[s_u, cb_done, rdr.get(("sB", par))])
                        rdr[("ps", b2)] = s_ue
                        prev_ue_last = s_ue
                        s_hm = pb.step("vector", lambda e, oc=oc, hf=hf, par=par: e.tensor_tensor(
                            hmoe[:, oc * TT2 + hf * 512:oc * TT2 + hf * 512 + 512],
                            sAm[par][:, :], sBm[par][:, :], ALU.mult),
                            deps=[s_si, s_ue, prev_dn])
                        rdr[("sA", par)] = s_hm
                        rdr[("sB", par)] = s_hm
                        hms.append(s_hm)
                hm_all = hms[-1]
                for dc in range(NCD):
                    par = dc % 2
                    buf = wa if par == 0 else wb
                    ldd = pb.dma("gpsimd", [(buf[:, :].rearrange("p (c o) -> p c o", o=128),
                                             colslice_ap(wdT_d[ex], dc * 128, 128, 32))],
                                 deps=[rdr.get(("wbuf", par))], chan="gA" if par == 0 else "gB")
                    for hf in range(2):
                        pdn = ps[5] if hf == 0 else psX
                        bkey = ("dn", hf)

                        def mmd(e, hf=hf, buf=buf, pdn=pdn):
                            r = None
                            for ic in range(NH):
                                r = e.matmul(pdn[:, :], buf[:, ic * 128:(ic + 1) * 128],
                                             hmoe[:, ic * TT2 + hf * 512:ic * TT2 + hf * 512 + 512],
                                             start=(ic == 0), stop=(ic == NH - 1))
                            return r
                        s_d = pb.step("tensor", mmd, deps=[ldd, rdr.get(bkey), hm_all])
                        rdr[("wbuf", par)] = s_d
                        cs = slice(dc * T + base + hf * 512, dc * T + base + hf * 512 + 512)
                        s_a = pb.step("vector", lambda e, cs=cs, pdn=pdn: e.tensor_tensor(
                            x[:, cs], x[:, cs], pdn[:, :], ALU.add), deps=[s_d])
                        rdr[bkey] = s_a
                        prev_dn = s_a
        pb.prev = pb.step("scalar", lambda e: e.nop(), deps=[pb.prev, prev_dn])

        stv = pb.counts["syC"] + 16 * NCD
        pb.dma("sync", [(outT_d[c * 128:(c + 1) * 128, :], x[:, c * T:(c + 1) * T]) for c in range(NCD)], chan="syC")

        def fin(e):
            e.wait_ge(sems["syC"], stv)
            return e.nop()
        pb.step("sync", fin)

        pb.emit(block, sems)
    return nc


def prep_inputs_B(inputs, yT_list, core):
    asc = np.ascontiguousarray

    def colpack(vecs):
        out = np.zeros((128, 48), np.float32)
        for j, v in enumerate(vecs):
            out[:, j * 8:(j + 1) * 8] = v.reshape(8, 128).T
        return out
    xT = np.concatenate([yT_list[b][:, core * TB:(core + 1) * TB] for b in range(NB)], axis=1)
    gwc = np.zeros((128, 64), np.float32)
    gw = inputs["L1_ffn_gate_w"]
    for c in range(8):
        gwc[:, c * 8:(c + 1) * 8] = gw[:, c * 128:(c + 1) * 128].T
    return {
        "xT": asc(xT),
        "WinT": asc(inputs["L1_attn_W_in"].T), "WlamT": asc(inputs["L1_attn_W_lam"].T),
        "WgateT": asc(inputs["L1_attn_W_gate"].T), "WoutT": asc(inputs["L1_attn_W_out"].T),
        "bcols": colpack([inputs["L1_attn_b_in"], inputs["L1_attn_b_lam"],
                          inputs["L1_attn_b_gate"], inputs["L1_attn_b_out"],
                          inputs["L1_attn_gamma"], inputs["L1_attn_beta"]]),
        "gwc": asc(gwc),
        "grows": inputs["L1_attn_gamma"].reshape(1, D).astype(np.float32),
        "wgT": asc(np.transpose(inputs["L1_ffn_wg"], (0, 2, 1))),
        "wuT": asc(np.transpose(inputs["L1_ffn_wu"], (0, 2, 1))),
        "wdT": asc(np.transpose(inputs["L1_ffn_wd"], (0, 2, 1))),
    }


from concourse.bass_utils import run_bass_kernel_spmd

_CACHE = {}


def kernel(**inputs):
    if "A" not in _CACHE:
        _CACHE["A"] = build_launchA()
        _CACHE["B"] = build_launchB()
    ncA, ncB = _CACHE["A"], _CACHE["B"]

    in_maps_A = [prep_inputs_A(inputs, b) for b in range(8)]
    resA = run_bass_kernel_spmd(ncA, in_maps_A, core_ids=list(range(8)))
    yT_list = [resA.results[b]["yT"] for b in range(8)]

    in_maps_B = [prep_inputs_B(inputs, yT_list, c) for c in range(8)]
    resB = run_bass_kernel_spmd(ncB, in_maps_B, core_ids=list(range(8)))

    out = np.empty((8, 2048, 1024), np.float32)
    for c in range(8):
        out[:, c * TB:(c + 1) * TB, :] = resB.results[c]["outT"].T.reshape(NB, TB, D)
    return out
